# version 10
# baseline (speedup 1.0000x reference)
import numpy as np
import ml_dtypes
import concourse.bass as bass
import concourse.mybir as mybir
from concourse.bass_utils import run_bass_kernel_spmd
from contextlib import ExitStack

F32 = mybir.dt.float32
BF16 = mybir.dt.bfloat16
I32 = mybir.dt.int32
I16 = mybir.dt.int16
AF = mybir.ActivationFunctionType
ALU = mybir.AluOpType
AX = mybir.AxisListType

N_CORES = 8
H = 1024
T = 2048
NT = 8
INNER = 4096
NI = 32
NH = 16
DH = 64
EPS = 1e-5
LADDER_K = 8

ENGINES = ("tensor", "vector", "scalar", "gpsimd", "sync")

R0, R1, R2, R3, RS = 0, 2048, 4096, 5632, 5632


class Op:
    __slots__ = ("idx", "engine", "fn", "reads", "writes", "deps", "is_dma",
                 "dma_sem", "dma_val", "pre_wait", "needs_inc", "eng_seq", "label")

    def __init__(self, idx, engine, fn, reads, writes, is_dma, label):
        self.idx = idx
        self.engine = engine
        self.fn = fn
        self.reads = reads
        self.writes = writes
        self.deps = []
        self.is_dma = is_dma
        self.dma_sem = None
        self.dma_val = 0
        self.pre_wait = None
        self.needs_inc = False
        self.eng_seq = 0
        self.label = label


class Prog:
    N_DMA_SEMS = 24

    def __init__(self, nc, stack):
        self.nc = nc
        self.stack = stack
        self.ops = []
        self.state = {}
        self.sems = {e: stack.enter_context(nc.semaphore(f"pg_{e}")) for e in ENGINES}
        self.dma_pool = [stack.enter_context(nc.semaphore(f"dma{i}"))
                         for i in range(self.N_DMA_SEMS)]
        self.dma_counts = [0] * self.N_DMA_SEMS
        self.dma_i = 0
        self.swdge_count = 0

    def sbuf(self, name, shape, dtype):
        return self.stack.enter_context(self.nc.sbuf_tensor(name, list(shape), dtype))

    def psum(self, name, shape, dtype=F32):
        return self.stack.enter_context(self.nc.psum_tensor(name, list(shape), dtype))

    def _dep_update(self, op):
        deps = {}
        for k in op.reads:
            st = self.state.get(k)
            if st is not None and st[0] is not None:
                deps[st[0].idx] = st[0]
        for k in op.writes:
            st = self.state.get(k)
            if st is not None:
                if st[0] is not None:
                    deps[st[0].idx] = st[0]
                for r in st[1]:
                    deps[r.idx] = r
        deps.pop(op.idx, None)
        op.deps = list(deps.values())
        for k in op.reads:
            self.state.setdefault(k, [None, []])[1].append(op)
        for k in op.writes:
            self.state[k] = [op, []]

    def op(self, engine, fn, reads=(), writes=(), label=""):
        o = Op(len(self.ops), engine, fn, tuple(reads), tuple(writes), False, label)
        self._dep_update(o)
        self.ops.append(o)
        return o

    def dma(self, out_ap, in_ap, reads=(), writes=(), queue="sync", label="", **kw):
        if queue == "gpsimd":
            sem = self.stack.enter_context(self.nc.semaphore(f"sw{self.swdge_count}"))
            self.swdge_count += 1
            o = Op(len(self.ops), "gpsimd",
                   lambda e, _o=out_ap, _i=in_ap, _kw=kw: e.dma_start(_o, _i, **_kw),
                   tuple(reads), tuple(writes), True, label)
            o.dma_sem, o.dma_val = sem, 16
        else:
            i = self.dma_i % self.N_DMA_SEMS
            self.dma_i += 1
            self.dma_counts[i] += 1
            o = Op(len(self.ops), "sync",
                   lambda e, _o=out_ap, _i=in_ap, _kw=kw: e.dma_start(_o, _i, **_kw),
                   tuple(reads), tuple(writes), True, label)
            o.dma_sem, o.dma_val = self.dma_pool[i], 16 * self.dma_counts[i]
            if self.dma_counts[i] > 1:
                o.pre_wait = (self.dma_pool[i], 16 * (self.dma_counts[i] - 1))
        self._dep_update(o)
        self.ops.append(o)
        return o

    def emit(self, block, end_fence=()):
        for o in self.ops:
            for d in o.deps:
                if not d.is_dma:
                    d.needs_inc = True
        counts = {e: 0 for e in ENGINES}
        for o in self.ops:
            if not o.is_dma:
                if o.needs_inc:
                    counts[o.engine] += 1
                o.eng_seq = counts[o.engine]
        per_engine = {e: [] for e in ENGINES}
        for o in self.ops:
            per_engine[o.engine].append(o)
        fence_list = list(end_fence)

        def make_body(ename, ops_here):
            def body(eng):
                hw = {e: 0 for e in ENGINES}
                hwd = {}
                for o in ops_here:
                    if o.pre_wait is not None:
                        s, v = o.pre_wait
                        if hwd.get(id(s), 0) < v:
                            eng.wait_ge(s, v)
                            hwd[id(s)] = v
                    for d in o.deps:
                        if d.is_dma:
                            if hwd.get(id(d.dma_sem), 0) < d.dma_val:
                                eng.wait_ge(d.dma_sem, d.dma_val)
                                hwd[id(d.dma_sem)] = d.dma_val
                        else:
                            if d.eng_seq > 0 and hw[d.engine] < d.eng_seq:
                                eng.wait_ge(self.sems[d.engine], d.eng_seq)
                                hw[d.engine] = d.eng_seq
                    inst = o.fn(eng)
                    if o.is_dma:
                        inst.then_inc(o.dma_sem, 16)
                    elif o.needs_inc:
                        inst.then_inc(self.sems[o.engine], 1)
                if ename == "sync":
                    for d in fence_list:
                        if hwd.get(id(d.dma_sem), 0) < d.dma_val:
                            eng.wait_ge(d.dma_sem, d.dma_val)
                            hwd[id(d.dma_sem)] = d.dma_val
            return body

        for ename in ENGINES:
            if per_engine[ename] or (ename == "sync" and fence_list):
                getattr(block, ename)(make_body(ename, per_engine[ename]))



def _kl(x):
    return list(x) if isinstance(x, (list, tuple)) else [x]

class MK:
    def __init__(self, nc, stack):
        self.nc = nc
        self.p = Prog(nc, stack)
        p = self.p
        self.A = p.sbuf("A", [128, 16384], F32)
        self.C = p.sbuf("C", [128, 16400], BF16)
        self.D = p.sbuf("Db", [128, 16384], BF16)
        self.E = p.sbuf("Eb", [128, 16384], BF16)
        self.S1 = p.sbuf("S1", [128, 2048], F32)
        self.SC = p.sbuf("SCb", [128, 6144], BF16)
        self.QI = p.sbuf("QI", [128, 512], I16)
        self.WL = p.sbuf("WL", [128, 1024], BF16)
        self.WF = p.sbuf("WF", [128, 1024], F32)
        self.ROWS = p.sbuf("ROWS", [1, 4096], F32)
        self.QROWF = p.sbuf("QROWF", [1, 512], F32)
        self.COLS = p.sbuf("COLS", [128, 96], F32)
        self.CCOL = p.sbuf("CCOL", [128, 96], F32)
        self.ONESC = p.sbuf("ONESC", [128, 1], BF16)
        self.ONESCF = p.sbuf("ONESCF", [128, 1], F32)
        self.ONESR = p.sbuf("ONESR", [1, 128], BF16)
        self.ONESRF = p.sbuf("ONESRF", [1, 128], F32)
        self.ONES127 = p.sbuf("ONES127", [1, 128], F32)
        self.EPSR = p.sbuf("EPSR", [1, 1], F32)
        self.SCL = p.sbuf("SCL", [1, 16], F32)
        self.IDENT = p.sbuf("IDENT", [128, 128], F32)
        self.PS = [p.psum(f"ps{i}", [128, 512], F32) for i in range(8)]
        self.PK = [f"P{i}" for i in range(8)]
        self.col_slots = {}
        self.col_next = 0

    def Av(self, i, c0=0, n=2048):
        return self.A[:, i * 2048 + c0: i * 2048 + c0 + n]

    def Cv(self, i, c0=0, n=2048):
        return self.C[:, i * 2048 + c0: i * 2048 + c0 + n]

    def init_consts(self, ident_hbm):
        p = self.p
        p.op("vector", lambda e: e.memset(self.ONESC[:], 1.0), writes=["ONESC"])
        p.op("vector", lambda e: e.memset(self.ONESCF[:], 1.0), writes=["ONESCF"])
        p.op("vector", lambda e: e.memset(self.ONESR[:], 1.0), writes=["ONESR"])
        p.op("vector", lambda e: e.memset(self.ONESRF[:], 1.0), writes=["ONESRF"])
        p.op("vector", lambda e: e.memset(self.ONES127[:], 1.0 / 127.0), writes=["ONES127"])
        p.op("vector", lambda e: e.memset(self.EPSR[:], EPS), writes=["EPSR"])
        p.dma(self.IDENT[:], ident_hbm, writes=["IDENT"], label="ident")

    def load_cols(self, hbm_ap, name):
        s = self.col_next
        self.col_next += 8
        assert self.col_next <= 96
        self.col_slots[name] = s
        self.p.dma(self.COLS[:, s:s + 8], hbm_ap, writes=[f"COLS.{name}"],
                   label=f"col.{name}")

    def col(self, name, i):
        s = self.col_slots[name]
        return self.COLS[:, s + i:s + i + 1]

    def load_x(self, x_hbm):
        for i in range(NT):
            self.p.dma(self.Av(i), x_hbm[i * 128:(i + 1) * 128, :],
                       writes=[f"A.{i}"], label=f"ldx{i}")

    def store_x(self, x_hbm):
        outs = []
        for i in range(NT):
            outs.append(self.p.dma(x_hbm[i * 128:(i + 1) * 128, :], self.Av(i),
                                   reads=[f"A.{i}"], label=f"stx{i}"))
        return outs

    def emit_ln(self, x_ap, x_key, y_ap, y_key, tag, normalize=True):
        p = self.p
        xb = self.SC[:, R0:R0 + 2048]
        x2 = self.SC[:, R1:R1 + 2048]
        for i in range(NT):
            p.op("vector", lambda e, i=i: e.tensor_copy(xb, x_ap(i)),
                 reads=_kl(x_key(i)), writes=["SC.A"],
                 label=f"{tag}.cast{i}")
            p.op("scalar", lambda e: e.activation(x2, xb, AF.Square),
                 reads=["SC.A"], writes=["SC.B"], label=f"{tag}.sq{i}")
            for c in range(4):
                r = (c // 2) * 32
                p.op("tensor", lambda e, i=i, c=c, r=r: e.matmul(
                    self.PS[4 + c % 2][r:r + 1, 0:512],
                    self.ONESC[:], xb[:, c * 512:(c + 1) * 512],
                    start=(i == 0), stop=(i == NT - 1)),
                    reads=["SC.A", "ONESC"], writes=[self.PK[4 + c % 2]],
                    label=f"{tag}.s{i}.{c}")
                p.op("tensor", lambda e, i=i, c=c, r=r: e.matmul(
                    self.PS[6 + c % 2][r:r + 1, 0:512],
                    self.ONESC[:], x2[:, c * 512:(c + 1) * 512],
                    start=(i == 0), stop=(i == NT - 1)),
                    reads=["SC.B", "ONESC"], writes=[self.PK[6 + c % 2]],
                    label=f"{tag}.q{i}.{c}")
        row = self.ROWS[0:1, 0:2048]
        rowr = self.ROWS[0:1, 2048:4096]
        for c in range(4):
            r = (c // 2) * 32
            p.op("vector", lambda e, c=c, r=r: e.tensor_scalar_mul(
                row[:, c * 512:(c + 1) * 512],
                self.PS[4 + c % 2][r:r + 1, 0:512], 1.0 / H),
                reads=[self.PK[4 + c % 2]], writes=["ROWS"], label=f"{tag}.mu{c}")
        if normalize:
            for c in range(4):
                p.op("tensor", lambda e, c=c: e.matmul(
                    self.PS[c][:, 0:512], self.ONESRF[:],
                    row[:, c * 512:(c + 1) * 512], start=True, stop=True),
                    reads=["ROWS", "ONESRF"], writes=[self.PK[c]],
                    label=f"{tag}.repmu{c}")
        p.op("scalar", lambda e: e.activation(rowr, row, AF.Square),
             reads=["ROWS"], writes=["ROWS"], label=f"{tag}.mu2")
        for c in range(4):
            r = (c // 2) * 32
            p.op("vector", lambda e, c=c, r=r: e.scalar_tensor_tensor(
                rowr[:, c * 512:(c + 1) * 512],
                self.PS[6 + c % 2][r:r + 1, 0:512], 1.0 / H,
                rowr[:, c * 512:(c + 1) * 512], ALU.mult, ALU.subtract),
                reads=[self.PK[6 + c % 2], "ROWS"], writes=["ROWS"],
                label=f"{tag}.var{c}")
        p.op("scalar", lambda e: e.activation(rowr, rowr, AF.Sqrt, bias=self.EPSR[0:1, 0:1]),
             reads=["ROWS", "EPSR"], writes=["ROWS"], label=f"{tag}.sd")
        p.op("vector", lambda e: e.reciprocal(rowr, rowr),
             reads=["ROWS"], writes=["ROWS"], label=f"{tag}.rstd")
        if not normalize:
            return
        for c in range(4):
            p.op("tensor", lambda e, c=c: e.matmul(
                self.PS[4 + c][:, 0:512], self.ONESRF[:],
                rowr[:, c * 512:(c + 1) * 512], start=True, stop=True),
                reads=["ROWS", "ONESRF"], writes=[self.PK[4 + c]],
                label=f"{tag}.reprs{c}")
        for c in range(4):
            for i in range(NT):
                p.op("vector", lambda e, i=i, c=c: e.scalar_tensor_tensor(
                    self.S1[:, 0:512], self.PS[c][:, 0:512], -1.0,
                    x_ap(i)[:, c * 512:(c + 1) * 512], ALU.mult, ALU.add),
                    reads=[self.PK[c]] + _kl(x_key(i)), writes=["S1"],
                    label=f"{tag}.sub{i}.{c}")
                p.op("vector", lambda e, i=i, c=c: e.tensor_tensor(
                    y_ap(i)[:, c * 512:(c + 1) * 512], self.S1[:, 0:512],
                    self.PS[4 + c][:, 0:512], ALU.mult),
                    reads=["S1", self.PK[4 + c]], writes=_kl(y_key(i)),
                    label=f"{tag}.nrm{i}.{c}")

    def emit_quant_scales(self, src_ap, src_key, nsrc, tlen, tag):
        p = self.p
        ntile = tlen // 128
        s1 = self.S1[:, 0:tlen]
        sqt = self.SC[:, R1:R1 + tlen] if tlen <= 2048 else None
        p.op("vector", lambda e: e.tensor_tensor(s1, src_ap(0), src_ap(0), ALU.mult),
             reads=[src_key(0)], writes=["S1"], label=f"{tag}.m0")
        for j in range(1, nsrc):
            p.op("vector", lambda e, j=j: e.tensor_tensor(sqt, src_ap(j), src_ap(j), ALU.mult),
                 reads=_kl(src_key(j)), writes=["SC.B"], label=f"{tag}.msq{j}")
            p.op("vector", lambda e: e.tensor_tensor(s1, s1, sqt, ALU.max),
                 reads=["S1", "SC.B"], writes=["S1"], label=f"{tag}.m{j}")
        for tt in range(ntile):
            p.op("tensor", lambda e, tt=tt: e.transpose(
                self.PS[4][:, 0:128], s1[:, tt * 128:(tt + 1) * 128], self.IDENT[:]),
                reads=["S1", "IDENT"], writes=[self.PK[4]], label=f"{tag}.tr{tt}")
            p.op("vector", lambda e, tt=tt, ntile=ntile: e.reduce_max(
                self.CCOL[:, ntile + tt:ntile + tt + 1], self.PS[4][:, 0:128], axis=AX.X),
                reads=[self.PK[4]], writes=["CCOL"], label=f"{tag}.red{tt}")
        cc = self.CCOL[:, ntile:2 * ntile]
        qq = self.CCOL[:, 0:ntile]
        p.op("scalar", lambda e: e.activation(cc, cc, AF.Sqrt),
             reads=["CCOL"], writes=["CCOL"], label=f"{tag}.sqrt")
        p.op("vector", lambda e: e.tensor_scalar_max(cc, cc, 1e-5),
             reads=["CCOL"], writes=["CCOL"], label=f"{tag}.clip")
        p.op("vector", lambda e: e.reciprocal(qq, cc),
             reads=["CCOL"], writes=["CCOL"], label=f"{tag}.inv")
        p.op("vector", lambda e: e.tensor_scalar_mul(qq, qq, 127.0),
             reads=["CCOL"], writes=["CCOL"], label=f"{tag}.q")
        p.op("vector", lambda e: e.tensor_scalar_mul(cc, cc, 1.0 / 127.0),
             reads=["CCOL"], writes=["CCOL"], label=f"{tag}.d")
        p.op("tensor", lambda e, n2=2 * ntile: e.transpose(
            self.PS[4][0:n2, 0:128], self.CCOL[:, 0:n2], self.IDENT[:]),
            reads=["CCOL", "IDENT"], writes=[self.PK[4]], label=f"{tag}.tr2")
        p.op("vector", lambda e, n2=2 * ntile: e.tensor_copy(
            self.WF[0:n2, 0:128], self.PS[4][0:n2, 0:128]),
            reads=[self.PK[4]], writes=["WF"], label=f"{tag}.evqf")
        nch = (tlen + 511) // 512
        for c in range(nch):
            w = min(512, tlen - c * 512)
            nr = (w + 127) // 128
            p.dma(self.QROWF[0:1, 0:w],
                  self.WF[ntile + 4 * c:ntile + 4 * c + nr, 0:128],
                  reads=["WF", "QROWF"], writes=["QROWF"],
                  label=f"{tag}.lind{c}")
            p.op("tensor", lambda e, w=w: e.matmul(
                self.PS[4][:, 0:w], self.ONESRF[:], self.QROWF[0:1, 0:w],
                start=True, stop=True),
                reads=["QROWF", "ONESRF"], writes=[self.PK[4]],
                label=f"{tag}.rd{c}")
            p.op("vector", lambda e, c=c, w=w: e.tensor_copy(
                self.SC[:, c * 512:c * 512 + w], self.PS[4][:, 0:w]),
                reads=[self.PK[4]], writes=["SC.A"],
                label=f"{tag}.red{c}")

    def _ladder(self, v, dst_ap, dst_key, n, tag):
        p = self.p
        t0 = self.SC[:, R2:R2 + n]
        ca = self.SC[:, R2 + 512:R2 + 512 + n]
        md = self.SC[:, R2 + 1024:R2 + 1024 + n]
        p.op("vector", lambda e: e.tensor_copy(self.QI[:, 0:n], v),
             reads=["S1"], writes=["QI"], label=f"{tag}.i")
        p.op("vector", lambda e: e.tensor_copy(t0, self.QI[:, 0:n]),
             reads=["QI"], writes=["SC.C"], label=f"{tag}.t0")
        p.op("vector", lambda e: e.tensor_tensor(md, t0, v, ALU.subtract),
             reads=["SC.C", "S1"], writes=["SC.C"], label=f"{tag}.md")
        p.op("vector", lambda e: e.tensor_scalar(dst_ap, md, -0.5, None, ALU.is_le),
             reads=["SC.C"], writes=_kl(dst_key), label=f"{tag}.c1")
        p.op("vector", lambda e: e.tensor_tensor(dst_ap, t0, dst_ap, ALU.add),
             reads=["SC.C"] + _kl(dst_key), writes=_kl(dst_key), label=f"{tag}.a1")
        _LVL = ([(-0.5 - k, ALU.add, ALU.is_le) for k in range(1, LADDER_K)]
                + [(0.5 + k, ALU.subtract, ALU.is_ge) for k in range(LADDER_K)])
        for (thr, op, alu) in _LVL:
            p.op("vector", lambda e, thr=thr, alu=alu: e.tensor_scalar(
                ca, md, thr, None, alu),
                reads=["SC.C"], writes=["SC.C"], label=f"{tag}.c{thr}")
            p.op("vector", lambda e, op=op: e.tensor_tensor(dst_ap, dst_ap, ca, op),
                 reads=_kl(dst_key) + ["SC.C"], writes=_kl(dst_key), label=f"{tag}.a{thr}")

    def _ladder_sar(self, v, dst_ap, dst_key, n, tag):
        p = self.p
        b = self.SC[:, R2 + 1024:R2 + 1024 + n]
        vc = self.S1[:, 1024:1024 + n]
        r = self.S1[:, 1536:1536 + n]
        p.op("vector", lambda e: e.tensor_scalar(
            vc, v, 127.49, -127.49, ALU.min, ALU.max),
            reads=["S1"], writes=["S1"], label=f"{tag}.cl")
        p.op("vector", lambda e: e.tensor_scalar(r, vc, 128.5, None, ALU.add),
             reads=["S1"], writes=["S1"], label=f"{tag}.f")
        for w in (128.0, 64.0, 32.0, 16.0, 8.0, 4.0, 2.0, 1.0):
            p.op("vector", lambda e, w=w: e.tensor_scalar(b, r, w, None, ALU.is_ge),
                 reads=["S1"], writes=["SC.C"], label=f"{tag}.b{w}")
            p.op("vector", lambda e, w=w: e.scalar_tensor_tensor(
                r, b, -w, r, ALU.mult, ALU.add),
                reads=["SC.C", "S1"], writes=["S1"], label=f"{tag}.r{w}")
        p.op("vector", lambda e: e.scalar_tensor_tensor(
            vc, r, -1.0, vc, ALU.mult, ALU.add),
            reads=["S1"], writes=["S1"], label=f"{tag}.fin1")
        p.op("vector", lambda e: e.tensor_scalar(dst_ap, vc, 0.5, None, ALU.add),
             reads=["S1"], writes=_kl(dst_key), label=f"{tag}.fin2")

    def emit_quant1(self, tag):
        p = self.p
        for c in range(4):
            p.op("tensor", lambda e, c=c: e.matmul(
                self.PS[6][:, 0:512], self.ONESRF[:],
                self.ROWS[0:1, c * 512:(c + 1) * 512], start=True, stop=True),
                reads=["ROWS", "ONESRF"], writes=[self.PK[6]], label=f"{tag}.m{c}")
            for i in range(NT):
                p.op("vector", lambda e, i=i, c=c: e.scalar_tensor_tensor(
                    self.S1[:, 0:512], self.PS[6][:, 0:512], -1.0,
                    self.Av(i, c * 512, 512), ALU.mult, ALU.add),
                    reads=[self.PK[6], f"A.{i}"], writes=["S1"], label=f"{tag}.d{c}.{i}")
                p.op("vector", lambda e: e.tensor_tensor(
                    self.S1[:, 512:1024], self.S1[:, 0:512], self.S1[:, 0:512], ALU.mult),
                    reads=["S1"], writes=["S1"], label=f"{tag}.sq{c}.{i}")
                if i == 0:
                    p.op("vector", lambda e: e.tensor_copy(
                        self.S1[:, 1024:1536], self.S1[:, 512:1024]),
                        reads=["S1"], writes=["S1"], label=f"{tag}.M0.{c}")
                else:
                    p.op("vector", lambda e: e.tensor_tensor(
                        self.S1[:, 1024:1536], self.S1[:, 1024:1536],
                        self.S1[:, 512:1024], ALU.max),
                        reads=["S1"], writes=["S1"], label=f"{tag}.M{c}.{i}")
            for tt in range(4):
                p.op("tensor", lambda e, tt=tt: e.transpose(
                    self.PS[4][:, 0:128],
                    self.S1[:, 1024 + tt * 128:1024 + (tt + 1) * 128], self.IDENT[:]),
                    reads=["S1", "IDENT"], writes=[self.PK[4]], label=f"{tag}.tr{c}.{tt}")
                p.op("vector", lambda e, c=c, tt=tt: e.reduce_max(
                    self.CCOL[:, 16 + 4 * c + tt:16 + 4 * c + tt + 1],
                    self.PS[4][:, 0:128], axis=AX.X),
                    reads=[self.PK[4]], writes=["CCOL"], label=f"{tag}.red{c}.{tt}")
        p.op("scalar", lambda e: e.activation(
            self.CCOL[:, 16:32], self.CCOL[:, 16:32], AF.Sqrt),
            reads=["CCOL"], writes=["CCOL"], label=f"{tag}.sqrt")
        p.op("tensor", lambda e: e.transpose(
            self.PS[4][0:16, 0:128], self.CCOL[:, 16:32], self.IDENT[:]),
            reads=["CCOL", "IDENT"], writes=[self.PK[4]], label=f"{tag}.tr2")
        p.op("vector", lambda e: e.tensor_copy(
            self.WF[0:16, 0:128], self.PS[4][0:16, 0:128]),
            reads=[self.PK[4]], writes=["WF"], label=f"{tag}.evqf")
        for c in range(4):
            p.dma(self.QROWF[0:1, 0:512], self.WF[4 * c:4 * c + 4, 0:128],
                  reads=["WF", "QROWF"], writes=["QROWF"], label=f"{tag}.ql{c}")
            p.op("vector", lambda e, c=c: e.tensor_tensor(
                self.QROWF[0:1, 0:512], self.QROWF[0:1, 0:512],
                self.ROWS[0:1, 2048 + c * 512:2048 + (c + 1) * 512], ALU.mult),
                reads=["QROWF", "ROWS"], writes=["QROWF"], label=f"{tag}.my{c}")
            p.op("vector", lambda e: e.tensor_scalar_max(
                self.QROWF[0:1, 0:512], self.QROWF[0:1, 0:512], 1e-5),
                reads=["QROWF"], writes=["QROWF"], label=f"{tag}.cl{c}")
            p.op("tensor", lambda e: e.matmul(
                self.PS[4][:, 0:512], self.ONES127[:], self.QROWF[0:1, 0:512],
                start=True, stop=True),
                reads=["QROWF", "ONES127"], writes=[self.PK[4]], label=f"{tag}.dr{c}")
            p.op("vector", lambda e, c=c: e.tensor_copy(
                self.SC[:, c * 512:(c + 1) * 512], self.PS[4][:, 0:512]),
                reads=[self.PK[4]], writes=["SC.A"], label=f"{tag}.de{c}")
            p.op("vector", lambda e: e.reciprocal(
                self.QROWF[0:1, 0:512], self.QROWF[0:1, 0:512]),
                reads=["QROWF"], writes=["QROWF"], label=f"{tag}.rc{c}")
            p.op("vector", lambda e: e.tensor_scalar_mul(
                self.QROWF[0:1, 0:512], self.QROWF[0:1, 0:512], 127.0),
                reads=["QROWF"], writes=["QROWF"], label=f"{tag}.q{c}")
            p.op("tensor", lambda e: e.matmul(
                self.PS[5][:, 0:512], self.ONESRF[:], self.QROWF[0:1, 0:512],
                start=True, stop=True),
                reads=["QROWF", "ONESRF"], writes=[self.PK[5]], label=f"{tag}.qr{c}")
            p.op("tensor", lambda e, c=c: e.matmul(
                self.PS[6][:, 0:512], self.ONESRF[:],
                self.ROWS[0:1, c * 512:(c + 1) * 512], start=True, stop=True),
                reads=["ROWS", "ONESRF"], writes=[self.PK[6]], label=f"{tag}.mr{c}")
            p.op("tensor", lambda e, c=c: e.matmul(
                self.PS[7][:, 0:512], self.ONESRF[:],
                self.ROWS[0:1, 2048 + c * 512:2048 + (c + 1) * 512],
                start=True, stop=True),
                reads=["ROWS", "ONESRF"], writes=[self.PK[7]], label=f"{tag}.rr{c}")
            for s in range(NT):
                v = self.S1[:, 0:512]
                p.op("vector", lambda e, s=s, c=c: e.scalar_tensor_tensor(
                    v, self.PS[6][:, 0:512], -1.0, self.Av(s, c * 512, 512),
                    ALU.mult, ALU.add),
                    reads=[self.PK[6], f"A.{s}"], writes=["S1"],
                    label=f"{tag}.v1.{c}.{s}")
                p.op("vector", lambda e: e.tensor_tensor(
                    v, v, self.PS[7][:, 0:512], ALU.mult),
                    reads=["S1", self.PK[7]], writes=["S1"], label=f"{tag}.v2.{c}.{s}")
                p.op("vector", lambda e: e.tensor_tensor(
                    v, v, self.PS[5][:, 0:512], ALU.mult),
                    reads=["S1", self.PK[5]], writes=["S1"], label=f"{tag}.v3.{c}.{s}")
                self._ladder_sar(v, self.Cv(s, c * 512, 512), f"C.{s}", 512,
                                 f"{tag}.L{c}.{s}")

    def emit_quant_apply(self, mode, src_ap, src_key, dst_ap, dst_key, nsrc, tlen, tag):
        p = self.p
        for c in range(tlen // 512):
            p.dma(self.QROWF[0:1, 0:512], self.WF[4 * c:4 * c + 4, 0:128],
                  reads=["WF", "QROWF"], writes=["QROWF"], label=f"{tag}.ql{c}")
            p.op("tensor", lambda e: e.matmul(
                self.PS[5][:, 0:512], self.ONESRF[:], self.QROWF[0:1, 0:512],
                start=True, stop=True),
                reads=["QROWF", "ONESRF"], writes=[self.PK[5]], label=f"{tag}.qr{c}")
            if mode == "ln":
                p.op("tensor", lambda e, c=c: e.matmul(
                    self.PS[6][:, 0:512], self.ONESRF[:],
                    self.ROWS[0:1, c * 512:(c + 1) * 512], start=True, stop=True),
                    reads=["ROWS", "ONESRF"], writes=[self.PK[6]], label=f"{tag}.mr{c}")
                p.op("tensor", lambda e, c=c: e.matmul(
                    self.PS[7][:, 0:512], self.ONESRF[:],
                    self.ROWS[0:1, 2048 + c * 512:2048 + (c + 1) * 512],
                    start=True, stop=True),
                    reads=["ROWS", "ONESRF"], writes=[self.PK[7]], label=f"{tag}.rr{c}")
            for s in range(nsrc):
                sap = src_ap(s)[:, c * 512:(c + 1) * 512]
                dap = dst_ap(s)[:, c * 512:(c + 1) * 512]
                v = self.S1[:, 0:512]
                if mode == "ln":
                    p.op("vector", lambda e, sap=sap: e.scalar_tensor_tensor(
                        v, self.PS[6][:, 0:512], -1.0, sap, ALU.mult, ALU.add),
                        reads=[self.PK[6]] + _kl(src_key(s)), writes=["S1"],
                        label=f"{tag}.v1.{c}.{s}")
                    p.op("vector", lambda e: e.tensor_tensor(
                        v, v, self.PS[7][:, 0:512], ALU.mult),
                        reads=["S1", self.PK[7]], writes=["S1"], label=f"{tag}.v2.{c}.{s}")
                    p.op("vector", lambda e: e.tensor_tensor(
                        v, v, self.PS[5][:, 0:512], ALU.mult),
                        reads=["S1", self.PK[5]], writes=["S1"], label=f"{tag}.v3.{c}.{s}")
                    self._ladder_sar(v, dap, dst_key(s), 512, f"{tag}.L{c}.{s}")
                else:
                    p.op("vector", lambda e, sap=sap: e.tensor_tensor(
                        v, sap, self.PS[5][:, 0:512], ALU.mult),
                        reads=_kl(src_key(s)) + [self.PK[5]], writes=["S1"],
                        label=f"{tag}.v.{c}.{s}")
                    self._ladder_sar(v, dap, dst_key(s), 512, f"{tag}.L{c}.{s}")

    def emit_wstats(self, hbm_w, rows, cols, slot, extra_mul, tag):
        p = self.p
        n = rows * cols
        view = hbm_w.rearrange("(q p) c -> p q c", p=128)
        nq = rows // 128
        csz = min(cols, 2048)
        chunks = [(q, c0) for q in range(nq) for c0 in range(0, cols, csz)]
        acc = self.CCOL[:, 44:45]
        first = True
        for ch, (q, c0) in enumerate(chunks):
            p.dma(self.S1[:, 0:csz], view[:, q, c0:c0 + csz],
                  writes=["S1"], label=f"{tag}.ld{ch}")
            if first:
                p.op("vector", lambda e, csz=csz: e.reduce_sum(
                    acc, self.S1[:, 0:csz], axis=AX.X, apply_absolute_value=True),
                    reads=["S1"], writes=["CCOL"], label=f"{tag}.r{ch}")
                first = False
            else:
                p.op("vector", lambda e, csz=csz: e.reduce_sum(
                    self.CCOL[:, 45:46], self.S1[:, 0:csz], axis=AX.X,
                    apply_absolute_value=True),
                    reads=["S1"], writes=["CCOL"], label=f"{tag}.r{ch}")
                p.op("vector", lambda e: e.tensor_add(acc, acc, self.CCOL[:, 45:46]),
                     reads=["CCOL", "CCOL"], writes=["CCOL"], label=f"{tag}.a{ch}")
        p.op("tensor", lambda e: e.matmul(self.PS[4][0:1, 0:1], acc, self.ONESCF[:],
                                          start=True, stop=True),
             reads=["CCOL", "ONESCF"], writes=[self.PK[4]], label=f"{tag}.x")
        p.op("vector", lambda e: e.tensor_scalar_mul(self.SCL[0:1, 0:1],
                                                     self.PS[4][0:1, 0:1], 1.0 / n),
             reads=[self.PK[4]], writes=["SCL"], label=f"{tag}.mean")
        p.op("vector", lambda e: e.tensor_scalar_max(self.SCL[0:1, 0:1],
                                                     self.SCL[0:1, 0:1], 1e-5),
             reads=["SCL"], writes=["SCL"], label=f"{tag}.clip")
        p.op("tensor", lambda e: e.matmul(self.PS[4][:, 0:1], self.ONESRF[:],
                                          self.SCL[0:1, 0:1], start=True, stop=True),
             reads=["SCL", "ONESRF"], writes=[self.PK[4]], label=f"{tag}.rep")
        p.op("vector", lambda e: e.tensor_scalar_mul(
            self.CCOL[:, slot:slot + 1], self.PS[4][:, 0:1], 0.5),
            reads=[self.PK[4]], writes=["CCOL"], label=f"{tag}.thr")
        p.op("vector", lambda e: e.tensor_scalar_mul(
            self.CCOL[:, slot + 1:slot + 2], self.PS[4][:, 0:1], -0.5),
            reads=[self.PK[4]], writes=["CCOL"], label=f"{tag}.nthr")
        p.op("vector", lambda e, em=extra_mul: e.tensor_scalar_mul(
            self.CCOL[:, slot + 2:slot + 3], self.PS[4][:, 0:1], em),
            reads=[self.PK[4]], writes=["CCOL"], label=f"{tag}.sw")

    def emit_ffn(self, wg, wu, wd, wdq, tag):
        p = self.p
        self.emit_wstats(wg, H, INNER, 32, 1.0, f"{tag}.wsg")
        self.emit_wstats(wu, H, INNER, 36, 1.0, f"{tag}.wsu")
        self.emit_wstats(wd, INNER, H, 40, 0.5, f"{tag}.wsd")
        wdv = wd.rearrange("(q p) c -> p q c", p=128)
        wqv = wdq.rearrange("(q p) c -> p q c", p=128)
        for q in range(NI):
            p.dma(self.WF[:, 0:1024], wdv[:, q, :], writes=["WF"], label=f"{tag}.dl{q}")
            p.op("vector", lambda e: e.tensor_scalar(
                self.WL[:, 0:1024], self.WF[:, 0:1024], self.CCOL[:, 40:41], None, ALU.is_ge),
                reads=["WF", "CCOL"], writes=["WLa"], label=f"{tag}.d1{q}")
            p.op("vector", lambda e: e.tensor_scalar(
                self.SC[:, 2048:3072], self.WF[:, 0:1024], self.CCOL[:, 41:42], None, ALU.is_le),
                reads=["WF", "CCOL"], writes=["SC.B"], label=f"{tag}.d2{q}")
            p.op("vector", lambda e: e.tensor_sub(
                self.WL[:, 0:1024], self.WL[:, 0:1024], self.SC[:, 2048:3072]),
                reads=["WLa", "SC.B"], writes=["WLa"], label=f"{tag}.d3{q}")
            p.dma(wqv[:, q, :], self.WL[:, 0:1024], reads=["WLa"], label=f"{tag}.ds{q}")
        self.emit_ln(lambda i: self.Av(i), lambda i: f"A.{i}",
                     lambda i: self.Cv(i), lambda i: f"C.{i}", f"{tag}.ln",
                     normalize=False)
        self.emit_quant1(f"{tag}.q1")
        wgv = wg.rearrange("(j p) i -> p j i", p=128)
        wuv = wu.rearrange("(j p) i -> p j i", p=128)
        for hh in range(2):
            for it in range(NI):
                pr = (self.D if it < 16 else self.E)
                poff = (it % 16) * 1024
                pkey = f"{'D' if it < 16 else 'E'}.{it % 16}"
                c2s = self.SC[:, R2:R2 + 1024]
                for (nm, off, sl, wv) in (("g", 0, 32, wgv), ("u", 1024, 36, wuv)):
                    wk = "WLa" if off == 0 else "SC.B"
                    p.dma(self.WF[:, 0:1024].rearrange("p (j f) -> p j f", f=128),
                          wv[:, :, it * 128:(it + 1) * 128], writes=["WF"],
                          label=f"{tag}.{nm}{hh}.{it}.ld")
                    tdst = (self.WL[:, 0:1024] if off == 0
                            else self.SC[:, 2048:3072])
                    p.op("vector", lambda e, sl=sl, tdst=tdst: e.tensor_scalar(
                        tdst, self.WF[:, 0:1024],
                        self.CCOL[:, sl:sl + 1], None, ALU.is_ge),
                        reads=["WF", "CCOL"], writes=[wk],
                        label=f"{tag}.t1{nm}{hh}.{it}")
                    p.op("vector", lambda e, off=off, sl=sl, c2s=c2s: e.tensor_scalar(
                        c2s, self.WF[:, 0:1024],
                        self.CCOL[:, sl + 1:sl + 2], None, ALU.is_le),
                        reads=["WF", "CCOL"], writes=["SC.C"],
                        label=f"{tag}.t2{nm}{hh}.{it}")
                    p.op("vector", lambda e, c2s=c2s, tdst=tdst: e.tensor_sub(
                        tdst, tdst, c2s),
                        reads=[wk, "SC.C"], writes=[wk],
                        label=f"{tag}.t3{nm}{hh}.{it}")
                for c in range(2):
                    t0 = hh * 1024 + c * 512
                    def gmm(e, it=it, t0=t0):
                        last = None
                        for j in range(NT):
                            last = e.matmul(self.PS[0][:, 0:512],
                                            self.WL[:, j * 128:(j + 1) * 128],
                                            self.Cv(j, t0, 512),
                                            start=(j == 0), stop=(j == NT - 1))
                        return last
                    p.op("tensor", gmm,
                         reads=["WLa"] + [f"C.{j}" for j in range(NT)],
                         writes=[self.PK[0]], label=f"{tag}.mmg{hh}.{it}.{c}")
                    def umm(e, it=it, t0=t0):
                        last = None
                        for j in range(NT):
                            last = e.matmul(self.PS[1][:, 0:512],
                                            self.SC[:, 2048 + j * 128:2048 + (j + 1) * 128],
                                            self.Cv(j, t0, 512),
                                            start=(j == 0), stop=(j == NT - 1))
                        return last
                    p.op("tensor", umm,
                         reads=["SC.B"] + [f"C.{j}" for j in range(NT)],
                         writes=[self.PK[1]], label=f"{tag}.mmu{hh}.{it}.{c}")
                    p.op("vector", lambda e, t0=t0: e.scalar_tensor_tensor(
                        self.S1[:, 0:512], self.PS[0][:, 0:512], self.CCOL[:, 34:35],
                        self.SC[:, t0:t0 + 512], ALU.mult, ALU.mult),
                        reads=[self.PK[0], "CCOL", "SC.A"], writes=["S1"],
                        label=f"{tag}.eg{hh}.{it}.{c}")
                    p.op("scalar", lambda e: e.activation(
                        self.SC[:, R3:R3 + 512], self.S1[:, 0:512], AF.Silu),
                        reads=["S1"], writes=["SC.D"], label=f"{tag}.sil{hh}.{it}.{c}")
                    p.op("vector", lambda e, t0=t0: e.scalar_tensor_tensor(
                        self.S1[:, 512:1024], self.PS[1][:, 0:512], self.CCOL[:, 38:39],
                        self.SC[:, t0:t0 + 512], ALU.mult, ALU.mult),
                        reads=[self.PK[1], "CCOL", "SC.A"], writes=["S1"],
                        label=f"{tag}.eu{hh}.{it}.{c}")
                    p.op("vector", lambda e, pr=pr, poff=poff, c=c: e.tensor_tensor(
                        pr[:, poff + c * 512:poff + c * 512 + 512],
                        self.SC[:, R3:R3 + 512], self.S1[:, 512:1024], ALU.mult),
                        reads=["SC.D", "S1"], writes=[pkey],
                        label=f"{tag}.p{hh}.{it}.{c}")
            def psrc(j, hh=hh):
                pr = (self.D if j < 16 else self.E)
                return pr[:, (j % 16) * 1024:(j % 16) * 1024 + 1024]
            def psrck(j):
                return f"{'D' if j < 16 else 'E'}.{j % 16}"
            self.emit_quant_scales(psrc, psrck, NI, 1024, f"{tag}.q2{hh}")
            self.emit_quant_apply("p", psrc, psrck, psrc, psrck,
                                  NI, 1024, f"{tag}.qa2{hh}")
            for c in range(2):
                t0 = hh * 1024 + c * 512
                for q in range(NI):
                    p.dma(self.WL[:, 0:1024], wqv[:, q, :], writes=["WLa"],
                          label=f"{tag}.dw{hh}.{c}.{q}")
                    def dmm(e, q=q, c=c, hh=hh):
                        last = None
                        for m in range(NT):
                            pr = (self.D if q < 16 else self.E)
                            last = e.matmul(
                                self.PS[m][:, 0:512],
                                self.WL[:, m * 128:(m + 1) * 128],
                                pr[:, (q % 16) * 1024 + c * 512:(q % 16) * 1024 + c * 512 + 512],
                                start=(q == 0), stop=(q == NI - 1))
                        return last
                    p.op("tensor", dmm,
                         reads=["WLa", psrck(q)], writes=self.PK[0:8],
                         label=f"{tag}.mmd{hh}.{c}.{q}")
                for m in range(NT):
                    p.op("vector", lambda e, m=m, c=c: e.scalar_tensor_tensor(
                        self.S1[:, 0:512], self.PS[m][:, 0:512], self.CCOL[:, 42:43],
                        self.SC[:, c * 512:c * 512 + 512], ALU.mult, ALU.mult),
                        reads=[self.PK[m], "CCOL", "SC.A"], writes=["S1"],
                        label=f"{tag}.ed{hh}.{c}.{m}")
                    p.op("vector", lambda e, m=m, t0=t0: e.tensor_tensor(
                        self.Av(m, t0, 512), self.Av(m, t0, 512),
                        self.S1[:, 0:512], ALU.add),
                        reads=["S1", f"A.{m}"], writes=[f"A.{m}"],
                        label=f"{tag}.acc{hh}.{c}.{m}")


def build_ffn_prog():
    nc = bass.Bass("TRN2", target_bir_lowering=False, debug=False)
    xT = nc.dram_tensor("xT", [H, T], F32, kind="ExternalInput").ap()
    wg = nc.dram_tensor("wg", [H, INNER], F32, kind="ExternalInput").ap()
    wu = nc.dram_tensor("wu", [H, INNER], F32, kind="ExternalInput").ap()
    wd = nc.dram_tensor("wd", [INNER, H], F32, kind="ExternalInput").ap()
    ident = nc.dram_tensor("ident", [128, 128], F32, kind="ExternalInput").ap()
    xout = nc.dram_tensor("xout", [H, T], F32, kind="ExternalOutput").ap()
    wdq = nc.dram_tensor("wdq", [INNER, H], BF16, kind="Internal").ap()
    with ExitStack() as stack:
        mk = MK(nc, stack)
        mk.init_consts(ident)
        mk.load_x(xT)
        mk.emit_ffn(wg, wu, wd, wdq, "f")
        outs = mk.store_x(xout)
        with nc.Block() as block:
            mk.p.emit(block, end_fence=outs)
    return nc


def _ident_np():
    return np.eye(128, dtype=np.float32)


def _colify(v):
    return np.ascontiguousarray(v.reshape(8, 128).T).astype(np.float32)



def _ck(i):
    a = (i * 2050) // 2048
    b = min((i * 2050 + 2049) // 2048, 7)
    return [f"C.{a}"] if a == b else [f"C.{a}", f"C.{b}"]


def _dk(i):
    return [f"D.{2 * i}", f"D.{2 * i + 1}"]


def _ek(i):
    return [f"E.{2 * i}", f"E.{2 * i + 1}"]


def _mk_emit_rwkv(self, wk, wv, wr, wo, xspill, tag):
    p = self.p
    def yv(i, off=1, n=2048):
        return self.C[:, i * 2050 + off: i * 2050 + off + n]
    for i in range(NT):
        p.op("vector", lambda e, i=i: e.memset(self.C[:, i * 2050:i * 2050 + 1], 0.0),
             writes=_ck(i), label=f"{tag}.z{i}")
    self.emit_ln(lambda i: self.Av(i), lambda i: f"A.{i}",
                 lambda i: yv(i), lambda i: _ck(i), f"{tag}.ln")
    for i in range(NT):
        p.dma(xspill[i * 128:(i + 1) * 128, :], self.Av(i),
              reads=[f"A.{i}"], writes=["XSP"], label=f"{tag}.sp{i}")
    for (nm, sl) in (("tmk", 48), ("tmv", 56), ("tmr", 64)):
        p.op("vector", lambda e, nm=nm, sl=sl: e.tensor_scalar(
            self.CCOL[:, sl:sl + 8], self.COLS[:, self.col_slots[nm]:self.col_slots[nm] + 8],
            -1.0, 1.0, ALU.mult, ALU.add),
            reads=[f"COLS.{nm}"], writes=["CCOL"], label=f"{tag}.om{nm}")
    p.op("scalar", lambda e: e.activation(
        self.CCOL[:, 72:80], self.COLS[:, self.col_slots["tdec"]:self.col_slots["tdec"] + 8],
        AF.Exp), reads=["COLS.tdec"], writes=["CCOL"], label=f"{tag}.td1")
    p.op("scalar", lambda e: e.activation(
        self.CCOL[:, 72:80], self.CCOL[:, 72:80], AF.Exp, scale=-1.0),
        reads=["CCOL"], writes=["CCOL"], label=f"{tag}.td2")
    p.op("scalar", lambda e: e.activation(
        self.CCOL[:, 80:88], self.COLS[:, self.col_slots["tfirst"]:self.col_slots["tfirst"] + 8],
        AF.Exp), reads=["COLS.tfirst"], writes=["CCOL"], label=f"{tag}.etf")

    def emit_mix(osl, dkey_pref, tag2):
        tm = {48: "tmk", 56: "tmv", 64: "tmr"}[osl]
        for i in range(NT):
            p.op("vector", lambda e, i=i, osl=osl: e.tensor_scalar(
                self.SC[:, 0:2048], yv(i, 0), self.CCOL[:, osl + i:osl + i + 1],
                None, ALU.mult),
                reads=_ck(i) + ["CCOL"], writes=["SC.A"], label=f"{tag2}.a{i}")
            p.op("vector", lambda e, i=i, tm=tm: e.scalar_tensor_tensor(
                self.D[:, i * 2048:(i + 1) * 2048], yv(i),
                self.col(tm, i), self.SC[:, 0:2048], ALU.mult, ALU.add),
                reads=_ck(i) + [f"COLS.{tm}", "SC.A"], writes=_dk(i),
                label=f"{tag2}.b{i}")

    def proj_weights(w_hbm, i, tag2):
        p.dma(self.WF[:, 0:1024].rearrange("p (j f) -> p j f", f=128),
              w_hbm.rearrange("(j p) o -> p j o", p=128)[:, :, i * 128:(i + 1) * 128],
              writes=["WF"], label=f"{tag2}.wl{i}")
        p.op("vector", lambda e: e.tensor_copy(self.WL[:, 0:1024], self.WF[:, 0:1024]),
             reads=["WF"], writes=["WLa"], label=f"{tag2}.wc{i}")

    def proj_group(e, i, c, src_pref):
        last = None
        for j in range(NT):
            last = e.matmul(self.PS[c % 2][:, 0:512],
                            self.WL[:, j * 128:(j + 1) * 128],
                            self.D[:, j * 2048 + c * 512: j * 2048 + c * 512 + 512],
                            start=(j == 0), stop=(j == NT - 1))
        return last

    emit_mix(56, "D", f"{tag}.vm")
    for i in range(NT):
        proj_weights(wv, i, f"{tag}.vw{i}")
        for c in range(4):
            p.op("tensor", lambda e, i=i, c=c: proj_group(e, i, c, "D"),
                 reads=["WLa"] + [k for j in range(NT) for k in _dk(j)],
                 writes=[self.PK[c % 2]], label=f"{tag}.vmm{i}.{c}")
            p.op("vector", lambda e, i=i, c=c: e.tensor_copy(
                self.E[:, i * 2048 + c * 512:i * 2048 + c * 512 + 512],
                self.PS[c % 2][:, 0:512]),
                reads=[self.PK[c % 2]], writes=_ek(i), label=f"{tag}.ve{i}.{c}")
    emit_mix(64, "D", f"{tag}.rm")
    for i in range(NT):
        proj_weights(wr, i, f"{tag}.rw{i}")
        for c in range(4):
            p.op("tensor", lambda e, i=i, c=c: proj_group(e, i, c, "D"),
                 reads=["WLa"] + [k for j in range(NT) for k in _dk(j)],
                 writes=[self.PK[c % 2]], label=f"{tag}.rmm{i}.{c}")
            p.op("scalar", lambda e, i=i, c=c: e.activation(
                self.Av(i, c * 512, 512), self.PS[c % 2][:, 0:512], AF.Sigmoid),
                reads=[self.PK[c % 2]], writes=[f"A.{i}"], label=f"{tag}.re{i}.{c}")
    emit_mix(48, "D", f"{tag}.km")
    for i in range(NT):
        proj_weights(wk, i, f"{tag}.kw{i}")
        for c in range(4):
            p.op("tensor", lambda e, i=i, c=c: proj_group(e, i, c, "D"),
                 reads=["WLa"] + [k for j in range(NT) for k in _dk(j)],
                 writes=[self.PK[c % 2]], label=f"{tag}.kmm{i}.{c}")
            p.op("scalar", lambda e, i=i, c=c: e.activation(
                self.SC[:, c * 512:c * 512 + 512], self.PS[c % 2][:, 0:512], AF.Exp),
                reads=[self.PK[c % 2]], writes=["SC.A"], label=f"{tag}.ek{i}.{c}")
        p.op("vector", lambda e, i=i: e.tensor_tensor(
            self.SC[:, 2048:4096], self.SC[:, 0:2048],
            self.E[:, i * 2048:(i + 1) * 2048], ALU.mult),
            reads=["SC.A"] + _ek(i), writes=["SC.B"], label=f"{tag}.ekv{i}")
        p.op("vector", lambda e: e.memset(self.S1[:, 0:2048], 1.0),
             writes=["S1"], label=f"{tag}.tdm{i}")
        p.op("vector", lambda e, i=i: e.tensor_scalar(
            self.S1[:, 0:2048], self.S1[:, 0:2048],
            self.CCOL[:, 72 + i:72 + i + 1], None, ALU.mult),
            reads=["S1", "CCOL"], writes=["S1"], label=f"{tag}.tdf{i}")
        p.op("vector", lambda e: e.tensor_tensor_scan(
            self.SC[:, 4096:6144], self.S1[:, 0:2048], self.SC[:, 2048:4096],
            0.0, ALU.mult, ALU.add),
            reads=["S1", "SC.B"], writes=["SC.C", "SC.D"], label=f"{tag}.nsc{i}")
        p.op("vector", lambda e, i=i: e.tensor_tensor_scan(
            self.C[:, i * 2050:i * 2050 + 2048], self.S1[:, 0:2048],
            self.SC[:, 0:2048], 0.0, ALU.mult, ALU.add),
            reads=["S1", "SC.A"] + _ck(i), writes=_ck(i), label=f"{tag}.dsc{i}")
        p.op("vector", lambda e, i=i: e.tensor_scalar(
            self.SC[:, 2048:4096], self.SC[:, 2048:4096],
            self.CCOL[:, 80 + i:80 + i + 1], None, ALU.mult),
            reads=["SC.B", "CCOL"], writes=["SC.B"], label=f"{tag}.e1v{i}")
        p.op("vector", lambda e, i=i: e.tensor_scalar(
            self.SC[:, 0:2048], self.SC[:, 0:2048],
            self.CCOL[:, 80 + i:80 + i + 1], None, ALU.mult),
            reads=["SC.A", "CCOL"], writes=["SC.A"], label=f"{tag}.e1{i}")
        p.op("vector", lambda e, i=i: e.tensor_copy(
            self.S1[:, 0:1], self.SC[:, 0:1]),
            reads=["SC.A"], writes=["S1"], label=f"{tag}.d0{i}")
        p.op("vector", lambda e, i=i: e.tensor_tensor(
            self.S1[:, 1:2048], self.C[:, i * 2050:i * 2050 + 2047],
            self.SC[:, 1:2048], ALU.add),
            reads=_ck(i) + ["SC.A", "S1"], writes=["S1"], label=f"{tag}.dsh{i}")
        p.op("vector", lambda e: e.reciprocal(self.S1[:, 0:2048], self.S1[:, 0:2048]),
             reads=["S1"], writes=["S1"], label=f"{tag}.rec{i}")
        p.op("vector", lambda e: e.tensor_tensor(
            self.SC[:, 2049:4096], self.SC[:, 4096:4096 + 2047],
            self.SC[:, 2049:4096], ALU.add),
            reads=["SC.C", "SC.D", "SC.B"], writes=["SC.B"], label=f"{tag}.nsh{i}")
        p.op("vector", lambda e: e.tensor_tensor(
            self.S1[:, 0:2048], self.SC[:, 2048:4096], self.S1[:, 0:2048], ALU.mult),
            reads=["SC.B", "S1"], writes=["S1"], label=f"{tag}.wkv{i}")
        p.op("vector", lambda e, i=i: e.tensor_tensor(
            self.C[:, i * 2050:i * 2050 + 2048], self.S1[:, 0:2048],
            self.Av(i), ALU.mult),
            reads=["S1", f"A.{i}"] + _ck(i), writes=_ck(i), label=f"{tag}.rw{i}")
    for m in range(NT):
        p.dma(self.Av(m), xspill[m * 128:(m + 1) * 128, :],
              reads=[f"A.{m}", "XSP"], writes=[f"A.{m}"], label=f"{tag}.rl{m}")
    for m in range(NT):
        proj_weights(wo, m, f"{tag}.ow{m}")
        for c in range(4):
            def omm(e, m=m, c=c):
                last = None
                for j in range(NT):
                    last = e.matmul(self.PS[2 + c % 2][:, 0:512],
                                    self.WL[:, j * 128:(j + 1) * 128],
                                    self.C[:, j * 2050 + c * 512: j * 2050 + c * 512 + 512],
                                    start=(j == 0), stop=(j == NT - 1))
                return last
            p.op("tensor", omm,
                 reads=["WLa"] + [f"C.{j}" for j in range(NT)],
                 writes=[self.PK[2 + c % 2]], label=f"{tag}.omm{m}.{c}")
            p.op("vector", lambda e, m=m, c=c: e.scalar_tensor_tensor(
                self.Av(m, c * 512, 512), self.PS[2 + c % 2][:, 0:512],
                self.col("bo", m), self.Av(m, c * 512, 512), ALU.add, ALU.add),
                reads=[self.PK[2 + c % 2], "COLS.bo", f"A.{m}"], writes=[f"A.{m}"],
                label=f"{tag}.oe{m}.{c}")


MK.emit_rwkv = _mk_emit_rwkv


def build_rwkv_prog():
    nc = bass.Bass("TRN2", target_bir_lowering=False, debug=False)
    xT = nc.dram_tensor("xT", [H, T], F32, kind="ExternalInput").ap()
    wk = nc.dram_tensor("wk", [H, H], F32, kind="ExternalInput").ap()
    wv = nc.dram_tensor("wv", [H, H], F32, kind="ExternalInput").ap()
    wr = nc.dram_tensor("wr", [H, H], F32, kind="ExternalInput").ap()
    wo = nc.dram_tensor("wo", [H, H], F32, kind="ExternalInput").ap()
    ident = nc.dram_tensor("ident", [128, 128], F32, kind="ExternalInput").ap()
    cols = {}
    for nm in ("tmk", "tmv", "tmr", "tdec", "tfirst", "bo"):
        cols[nm] = nc.dram_tensor(f"c_{nm}", [128, 8], F32, kind="ExternalInput").ap()
    xout = nc.dram_tensor("xout", [H, T], F32, kind="ExternalOutput").ap()
    xspill = nc.dram_tensor("xspill", [H, T], F32, kind="Internal").ap()
    with ExitStack() as stack:
        mk = MK(nc, stack)
        mk.init_consts(ident)
        for nm in cols:
            mk.load_cols(cols[nm], nm)
        mk.load_x(xT)
        mk.emit_rwkv(wk, wv, wr, wo, xspill, "r")
        outs = mk.store_x(xout)
        with nc.Block() as block:
            mk.p.emit(block, end_fence=outs)
    return nc


def _mk_emit_attn(self, Wq, Wk, Wv, Wo, Wrq, Wrk, Wrv, Wg, pastT, brow, xspill, tag):
    p = self.p
    ALLC = [f"C.{i}" for i in range(NT)]

    def wload(w_hbm, blk, woff, wkey, tag2):
        p.dma(self.WF[:, 0:1024].rearrange("p (j f) -> p j f", f=128),
              w_hbm.rearrange("(j p) o -> p j o", p=128)[:, :, blk * 128:(blk + 1) * 128],
              writes=["WF"], label=f"{tag2}.wl")
        p.op("vector", lambda e, woff=woff: e.tensor_copy(
            self.WL[:, woff:woff + 1024], self.WF[:, 0:1024]),
            reads=["WF"], writes=[wkey], label=f"{tag2}.wc")

    def projT(dst_off, dst_keys, rhs_keys, bias_name, blk, tag2, act=None,
              woff=0, wkey="WLa", rhs_tiles=None, dst_buf=None):
        dst_buf = self.D if dst_buf is None else dst_buf
        if rhs_tiles is None:
            rhs_tiles = [self.Cv(j) for j in range(NT)]
        for c in range(4):
            def mm(e, c=c, rhs_tiles=rhs_tiles, woff=woff):
                last = None
                for j in range(NT):
                    last = e.matmul(self.PS[c % 2][:, 0:512],
                                    self.WL[:, woff + j * 128:woff + (j + 1) * 128],
                                    rhs_tiles[j][:, c * 512:(c + 1) * 512],
                                    start=(j == 0), stop=(j == NT - 1))
                return last
            p.op("tensor", mm, reads=[wkey] + rhs_keys,
                 writes=[self.PK[c % 2]], label=f"{tag2}.mm{c}")
            if act is None:
                p.op("vector", lambda e, c=c, dst_off=dst_off, blk=blk,
                     bias_name=bias_name, dst_buf=dst_buf: e.tensor_scalar(
                    dst_buf[:, dst_off + c * 512:dst_off + c * 512 + 512],
                    self.PS[c % 2][:, 0:512], self.col(bias_name, blk), None,
                    ALU.add),
                    reads=[self.PK[c % 2], f"COLS.{bias_name}"], writes=dst_keys,
                    label=f"{tag2}.ev{c}")
            else:
                p.op("scalar", lambda e, c=c, dst_off=dst_off, blk=blk,
                     bias_name=bias_name, dst_buf=dst_buf, act=act: e.activation(
                    dst_buf[:, dst_off + c * 512:dst_off + c * 512 + 512],
                    self.PS[c % 2][:, 0:512], act, bias=self.col(bias_name, blk)),
                    reads=[self.PK[c % 2], f"COLS.{bias_name}"], writes=dst_keys,
                    label=f"{tag2}.ev{c}")

    def head_norm_cell(src_buf, src_off, hl, cell, skey, tag2):
        sq = self.SC[:, R2:R2 + 2048]
        p.op("scalar", lambda e: e.activation(
            sq[0:64, :], src_buf[hl:hl + 64, src_off:src_off + 2048], AF.Square),
            reads=skey, writes=["SC.C"], label=f"{tag2}.sq")
        for c in range(4):
            p.op("tensor", lambda e, c=c: e.matmul(
                self.PS[5][0:1, 0:512], self.ONESC[0:64, :],
                sq[0:64, c * 512:(c + 1) * 512], start=True, stop=True),
                reads=["SC.C", "ONESC"], writes=[self.PK[5]], label=f"{tag2}.nm{c}")
            p.op("vector", lambda e, c=c, cell=cell: e.reduce_max(
                self.CCOL[0:1, 40 + c:41 + c], self.PS[5][0:1, 0:512], axis=AX.X),
                reads=[self.PK[5]], writes=["CCOL"], label=f"{tag2}.nr{c}")
        p.op("vector", lambda e: e.tensor_tensor(
            self.CCOL[0:1, 40:41], self.CCOL[0:1, 40:41], self.CCOL[0:1, 41:42], ALU.max),
            reads=["CCOL"], writes=["CCOL"], label=f"{tag2}.nx1")
        p.op("vector", lambda e: e.tensor_tensor(
            self.CCOL[0:1, 42:43], self.CCOL[0:1, 42:43], self.CCOL[0:1, 43:44], ALU.max),
            reads=["CCOL"], writes=["CCOL"], label=f"{tag2}.nx2")
        p.op("vector", lambda e, cell=cell: e.tensor_tensor(
            self.CCOL[0:1, cell:cell + 1], self.CCOL[0:1, 40:41],
            self.CCOL[0:1, 42:43], ALU.max),
            reads=["CCOL"], writes=["CCOL"], label=f"{tag2}.nx3")

    def bias_col_from_cells(cell_q, cell_k, tag2):
        p.op("vector", lambda e: e.tensor_tensor(
            self.CCOL[0:1, 44:45], self.CCOL[0:1, cell_q:cell_q + 1],
            self.CCOL[0:1, cell_k:cell_k + 1], ALU.mult),
            reads=["CCOL"], writes=["CCOL"], label=f"{tag2}.bm")
        p.op("scalar", lambda e: e.activation(
            self.CCOL[0:1, 44:45], self.CCOL[0:1, 44:45], AF.Sqrt),
            reads=["CCOL"], writes=["CCOL"], label=f"{tag2}.bs")
        p.op("vector", lambda e: e.tensor_scalar_mul(
            self.CCOL[0:1, 44:45], self.CCOL[0:1, 44:45], -0.125),
            reads=["CCOL"], writes=["CCOL"], label=f"{tag2}.bn")
        p.op("tensor", lambda e: e.matmul(
            self.PS[5][:, 0:1], self.ONESRF[:], self.CCOL[0:1, 44:45],
            start=True, stop=True),
            reads=["CCOL", "ONESRF"], writes=[self.PK[5]], label=f"{tag2}.br")
        p.op("vector", lambda e: e.tensor_copy(self.CCOL[:, 46:47], self.PS[5][:, 0:1]),
             reads=[self.PK[5]], writes=["CCOL"], label=f"{tag2}.be")

    def flash(q_buf, q_off, q_keys, k_buf, k_off, k_keys, v_buf, v_off, v_keys,
              hl, dst_region, dst_keys, scale05, accumulate, tag2):
        for sc in range(4):
            for tau in range(16):
                pb = (sc * 16 + tau) % 2
                p.op("tensor", lambda e, sc=sc, tau=tau, pb=pb: e.matmul(
                    self.PS[pb][:, 0:512],
                    k_buf[hl:hl + 64, k_off + tau * 128:k_off + (tau + 1) * 128],
                    q_buf[hl:hl + 64, q_off + sc * 512:q_off + sc * 512 + 512],
                    start=True, stop=True),
                    reads=k_keys + q_keys, writes=[self.PK[pb]],
                    label=f"{tag2}.sc{sc}.{tau}")
                et_off, et_key = (R2, "SC.C") if tau % 2 == 0 else (R3, "SC.D")
                p.op("scalar", lambda e, pb=pb, et_off=et_off: e.activation(
                    self.SC[:, et_off:et_off + 512], self.PS[pb][:, 0:512],
                    AF.Exp, bias=self.CCOL[:, 46:47], scale=0.125),
                    reads=[self.PK[pb], "CCOL"], writes=[et_key],
                    label=f"{tag2}.ex{sc}.{tau}")
                def ctx(e, tau=tau, et_off=et_off):
                    e.matmul(self.PS[4][0:64, 0:512],
                             v_buf[:, v_off + tau * 128 + hl:v_off + tau * 128 + hl + 64],
                             self.SC[:, et_off:et_off + 512],
                             start=(tau == 0), stop=(tau == 15))
                    return e.matmul(self.PS[5][64:65, 0:512], self.ONESC[:],
                                    self.SC[:, et_off:et_off + 512],
                                    start=(tau == 0), stop=(tau == 15))
                p.op("tensor", ctx, reads=v_keys + [et_key, "ONESC"],
                     writes=[self.PK[4], self.PK[5]], label=f"{tag2}.cx{sc}.{tau}")
            p.op("vector", lambda e: e.reciprocal(self.S1[0:1, 0:512],
                                                  self.PS[5][64:65, 0:512]),
                 reads=[self.PK[5]], writes=["S1"], label=f"{tag2}.rc{sc}")
            if scale05:
                p.op("vector", lambda e: e.tensor_scalar_mul(
                    self.S1[0:1, 0:512], self.S1[0:1, 0:512], 0.5),
                    reads=["S1"], writes=["S1"], label=f"{tag2}.h{sc}")
            p.op("tensor", lambda e: e.matmul(
                self.PS[5][hl:hl + 64, 0:512], self.ONESRF[0:1, 0:64],
                self.S1[0:1, 0:512], start=True, stop=True),
                reads=["S1", "ONESRF"], writes=[self.PK[5]], label=f"{tag2}.rp{sc}")
            p.op("vector", lambda e: e.tensor_copy(
                self.SC[hl:hl + 64, 5120:5120 + 512], self.PS[5][hl:hl + 64, 0:512]),
                reads=[self.PK[5]], writes=["SC.C"], label=f"{tag2}.rpe{sc}")
            if not accumulate:
                p.op("vector", lambda e, sc=sc: e.tensor_tensor(
                    self.A[hl:hl + 64, dst_region + sc * 512:dst_region + sc * 512 + 512],
                    self.PS[4][0:64, 0:512], self.SC[hl:hl + 64, 5120:5120 + 512], ALU.mult),
                    reads=[self.PK[4], "SC.C"], writes=dst_keys,
                    label=f"{tag2}.no{sc}")
            else:
                p.op("vector", lambda e, sc=sc: e.tensor_tensor(
                    self.SC[hl:hl + 64, R2 + 512:R2 + 1024],
                    self.PS[4][0:64, 0:512], self.SC[hl:hl + 64, 5120:5120 + 512], ALU.mult),
                    reads=[self.PK[4], "SC.C"], writes=["SC.C"],
                    label=f"{tag2}.na{sc}")
                p.op("vector", lambda e, sc=sc: e.tensor_tensor(
                    self.A[hl:hl + 64, dst_region + sc * 512:dst_region + sc * 512 + 512],
                    self.A[hl:hl + 64, dst_region + sc * 512:dst_region + sc * 512 + 512],
                    self.SC[hl:hl + 64, R2 + 512:R2 + 1024], ALU.add),
                    reads=["SC.C"] + dst_keys, writes=dst_keys,
                    label=f"{tag2}.ac{sc}")

    self.emit_ln(lambda i: self.Av(i), lambda i: f"A.{i}",
                 lambda i: self.Cv(i), lambda i: [f"C.{i}"], f"{tag}.ln")
    for i in range(NT):
        p.dma(xspill[i * 128:(i + 1) * 128, :], self.Av(i),
              reads=[f"A.{i}"], writes=["XSP"], label=f"{tag}.sp{i}")
    p.dma(self.ROWS[0:1, 0:2048], brow, writes=["ROWS"], label=f"{tag}.brow")

    for g in range(NT):
        gk = [f"G{g}"]
        wload(Wq, g, 0, "WLa", f"{tag}.q{g}")
        projT(0, ["D.0", "D.1"], ALLC, "bq", g, f"{tag}.qp{g}")
        wload(Wk, g, 0, "WLa", f"{tag}.k{g}")
        projT(2048, ["D.2", "D.3"], ALLC, "bk", g, f"{tag}.kp{g}")
        wload(Wv, g, 0, "WLa", f"{tag}.v{g}")
        p.op("vector", lambda e, g=g: e.tensor_copy(
            self.SC[0:1, R2:R2 + 128], self.ROWS[0:1, g * 128:(g + 1) * 128]),
            reads=["ROWS"], writes=["SC.C"], label=f"{tag}.bv{g}")
        for tau in range(16):
            def vmm(e, tau=tau):
                for j in range(NT):
                    e.matmul(self.PS[tau % 2][:, 0:128],
                             self.Cv(j)[:, tau * 128:(tau + 1) * 128],
                             self.WL[:, j * 128:(j + 1) * 128],
                             start=(j == 0), stop=False)
                return e.matmul(self.PS[tau % 2][:, 0:128], self.ONESR[:],
                                self.SC[0:1, R2:R2 + 128], start=False, stop=True)
            p.op("tensor", vmm, reads=["WLa", "SC.C", "ONESR"] + ALLC,
                 writes=[self.PK[tau % 2]], label=f"{tag}.vm{g}.{tau}")
            p.op("vector", lambda e, tau=tau: e.tensor_copy(
                self.D[:, 4096 + tau * 128:4096 + (tau + 1) * 128],
                self.PS[tau % 2][:, 0:128]),
                reads=[self.PK[tau % 2]], writes=["D.4", "D.5"],
                label=f"{tag}.ve{g}.{tau}")
        for hh in range(2):
            hl = hh * 64
            head_norm_cell(self.D, 0, hl, 36, ["D.0", "D.1"], f"{tag}.nq{g}.{hh}")
            head_norm_cell(self.D, 2048, hl, 37, ["D.2", "D.3"], f"{tag}.nk{g}.{hh}")
            bias_col_from_cells(36, 37, f"{tag}.bc{g}.{hh}")
            flash(self.D, 0, ["D.0", "D.1"], self.D, 2048, ["D.2", "D.3"],
                  self.D, 4096, ["D.4", "D.5"], hl,
                  g * 2048, [f"A.{g}"], False, False, f"{tag}.fl{g}.{hh}")
    for i in range(NT):
        p.op("vector", lambda e, i=i: e.tensor_copy(self.Cv(i), self.Av(i)),
             reads=[f"A.{i}"], writes=[f"C.{i}"], label=f"{tag}.cc{i}")
    for m in range(NT):
        wload(Wo, m, 0, "WLa", f"{tag}.o{m}")
        projT(m * 2048, [f"D.{2 * m}", f"D.{2 * m + 1}"], ALLC, "bo2", m, f"{tag}.op{m}")
    ALLD = [f"D.{r}" for r in range(16)]
    for m in range(NT):
        wload(Wrq, m, 0, "WLa", f"{tag}.rq{m}")
        projT(m * 2048, [f"E.{2 * m}", f"E.{2 * m + 1}"], ALLD, "brq", m,
              f"{tag}.rqp{m}", rhs_tiles=[self.D[:, j * 2048:(j + 1) * 2048] for j in range(NT)],
              dst_buf=self.E)
    ALLE = [f"E.{r}" for r in range(16)]
    for h in range(NH):
        g, hl = h // 2, (h % 2) * 64
        head_norm_cell(self.E, g * 2048, hl, 16 + h, ALLE, f"{tag}.rqn{h}")
    for pp in range(2):
        for i in range(NT):
            p.dma(self.S1[:, 0:2048], pastT[pp, i * 128:(i + 1) * 128, :],
                  reads=["S1"], writes=["S1"], label=f"{tag}.pl{pp}.{i}")
            p.op("vector", lambda e, i=i: e.tensor_copy(self.Cv(i), self.S1[:, 0:2048]),
                 reads=["S1"], writes=[f"C.{i}"], label=f"{tag}.pc{pp}.{i}")
        for g in range(NT):
            wload(Wrk, g, 0, "WLa", f"{tag}.rk{pp}.{g}")
            for c in range(4):
                def rkmm(e, c=c):
                    last = None
                    for j in range(NT):
                        last = e.matmul(self.PS[c % 2][:, 0:512],
                                        self.WL[:, j * 128:(j + 1) * 128],
                                        self.Cv(j)[:, c * 512:(c + 1) * 512],
                                        start=(j == 0), stop=(j == NT - 1))
                    return last
                p.op("tensor", rkmm, reads=["WLa"] + ALLC,
                     writes=[self.PK[c % 2]], label=f"{tag}.rkm{pp}.{g}.{c}")
                p.op("vector", lambda e, c=c, g=g: e.tensor_scalar(
                    self.SC[:, c * 512:c * 512 + 512], self.PS[c % 2][:, 0:512],
                    self.col("brk", g), None, ALU.add),
                    reads=[self.PK[c % 2], "COLS.brk"], writes=["SC.A"],
                    label=f"{tag}.rke{pp}.{g}.{c}")
            wload(Wrv, g, 0, "WLa", f"{tag}.rv{pp}.{g}")
            p.op("vector", lambda e, g=g: e.tensor_copy(
                self.SC[0:1, R2:R2 + 128], self.ROWS[0:1, 1024 + g * 128:1024 + (g + 1) * 128]),
                reads=["ROWS"], writes=["SC.C"], label=f"{tag}.brv{pp}.{g}")
            for tau in range(16):
                def rvmm(e, tau=tau):
                    for j in range(NT):
                        e.matmul(self.PS[tau % 2][:, 0:128],
                                 self.Cv(j)[:, tau * 128:(tau + 1) * 128],
                                 self.WL[:, j * 128:(j + 1) * 128],
                                 start=(j == 0), stop=False)
                    return e.matmul(self.PS[tau % 2][:, 0:128], self.ONESR[:],
                                    self.SC[0:1, R2:R2 + 128], start=False, stop=True)
                p.op("tensor", rvmm, reads=["WLa", "SC.C", "ONESR"] + ALLC,
                     writes=[self.PK[tau % 2]], label=f"{tag}.rvm{pp}.{g}.{tau}")
                p.op("vector", lambda e, tau=tau: e.tensor_copy(
                    self.SC[:, R1 + tau * 128:R1 + (tau + 1) * 128],
                    self.PS[tau % 2][:, 0:128]),
                    reads=[self.PK[tau % 2]], writes=["SC.B"],
                    label=f"{tag}.rve{pp}.{g}.{tau}")
            for hh in range(2):
                hl = hh * 64
                h = g * 2 + hh
                head_norm_cell(self.SC, 0, hl, 37, ["SC.A"], f"{tag}.rkn{pp}.{g}.{hh}")
                bias_col_from_cells(16 + h, 37, f"{tag}.rbc{pp}.{g}.{hh}")
                flash(self.E, g * 2048, ALLE, self.SC, 0, ["SC.A"],
                      self.SC, R1, ["SC.B"], hl,
                      g * 2048, [f"A.{g}"], True, pp == 1, f"{tag}.rf{pp}.{g}.{hh}")
    for i in range(NT):
        p.op("vector", lambda e, i=i: e.tensor_copy(self.Cv(i), self.Av(i)),
             reads=[f"A.{i}"], writes=[f"C.{i}"], label=f"{tag}.rc{i}")
    for m in range(NT):
        p.dma(self.WF[:, 0:1024].rearrange("p (j f) -> p j f", f=128),
              Wg.rearrange("(j p) o -> p j o", p=128)[:, 0:8, m * 128:(m + 1) * 128],
              writes=["WF"], label=f"{tag}.g1{m}")
        p.op("vector", lambda e: e.tensor_copy(self.WL[:, 0:1024], self.WF[:, 0:1024]),
             reads=["WF"], writes=["WLa"], label=f"{tag}.g1c{m}")
        p.dma(self.WF[:, 0:1024].rearrange("p (j f) -> p j f", f=128),
              Wg.rearrange("(j p) o -> p j o", p=128)[:, 8:16, m * 128:(m + 1) * 128],
              writes=["WF"], label=f"{tag}.g2{m}")
        p.op("vector", lambda e: e.tensor_copy(self.SC[:, 2048:3072], self.WF[:, 0:1024]),
             reads=["WF"], writes=["SC.B"], label=f"{tag}.g2c{m}")
        for c in range(4):
            def gmm(e, c=c):
                last = None
                for j in range(16):
                    rhs = (self.D[:, (j) * 2048 + c * 512:j * 2048 + c * 512 + 512]
                           if j < 8 else
                           self.Cv(j - 8)[:, c * 512:(c + 1) * 512])
                    lhsT = (self.WL[:, j * 128:(j + 1) * 128] if j < 8
                            else self.SC[:, 2048 + (j - 8) * 128:2048 + (j - 7) * 128])
                    last = e.matmul(self.PS[2 + c % 2][:, 0:512],
                                    lhsT, rhs, start=(j == 0), stop=(j == 15))
                return last
            p.op("tensor", gmm, reads=["WLa", "SC.B"] + ALLD + ALLC,
                 writes=[self.PK[2 + c % 2]], label=f"{tag}.gm{m}.{c}")
            p.op("scalar", lambda e, c=c, m=m: e.activation(
                self.E[:, m * 2048 + c * 512:m * 2048 + c * 512 + 512],
                self.PS[2 + c % 2][:, 0:512], AF.Sigmoid, bias=self.col("bg", m)),
                reads=[self.PK[2 + c % 2], "COLS.bg"],
                writes=[f"E.{2 * m}", f"E.{2 * m + 1}"], label=f"{tag}.ge{m}.{c}")
    for m in range(NT):
        p.dma(self.Av(m), xspill[m * 128:(m + 1) * 128, :],
              reads=[f"A.{m}", "XSP"], writes=[f"A.{m}"], label=f"{tag}.rl{m}")
    for m in range(NT):
        for c in range(4):
            sl = lambda buf: buf[:, m * 2048 + c * 512:m * 2048 + c * 512 + 512]
            p.op("vector", lambda e, m=m, c=c: e.tensor_tensor(
                self.SC[:, 0:512], self.Cv(m)[:, c * 512:(c + 1) * 512],
                self.D[:, m * 2048 + c * 512:m * 2048 + c * 512 + 512], ALU.subtract),
                reads=[f"C.{m}", f"D.{2 * m}", f"D.{2 * m + 1}"], writes=["SC.A"],
                label=f"{tag}.cb1{m}.{c}")
            p.op("vector", lambda e, m=m, c=c: e.tensor_tensor(
                self.SC[:, 512:1024],
                self.E[:, m * 2048 + c * 512:m * 2048 + c * 512 + 512],
                self.SC[:, 0:512], ALU.mult),
                reads=[f"E.{2 * m}", f"E.{2 * m + 1}", "SC.A"], writes=["SC.A"],
                label=f"{tag}.cb2{m}.{c}")
            p.op("vector", lambda e, m=m, c=c: e.tensor_tensor(
                self.SC[:, 512:1024],
                self.D[:, m * 2048 + c * 512:m * 2048 + c * 512 + 512],
                self.SC[:, 512:1024], ALU.add),
                reads=[f"D.{2 * m}", f"D.{2 * m + 1}", "SC.A"], writes=["SC.A"],
                label=f"{tag}.cb3{m}.{c}")
            p.op("vector", lambda e, m=m, c=c: e.tensor_tensor(
                self.Av(m, c * 512, 512), self.Av(m, c * 512, 512),
                self.SC[:, 512:1024], ALU.add),
                reads=["SC.A", f"A.{m}"], writes=[f"A.{m}"],
                label=f"{tag}.cb4{m}.{c}")


MK.emit_attn = _mk_emit_attn


def build_attn_prog():
    nc = bass.Bass("TRN2", target_bir_lowering=False, debug=False)
    xT = nc.dram_tensor("xT", [H, T], F32, kind="ExternalInput").ap()
    ws = {}
    for nm in ("Wq", "Wk", "Wv", "Wo", "Wrq", "Wrk", "Wrv"):
        ws[nm] = nc.dram_tensor(nm, [H, H], F32, kind="ExternalInput").ap()
    Wg = nc.dram_tensor("Wg", [2 * H, H], F32, kind="ExternalInput").ap()
    pastT = nc.dram_tensor("pastT", [2, H, T], F32, kind="ExternalInput").ap()
    brow = nc.dram_tensor("brow", [1, 2048], F32, kind="ExternalInput").ap()
    ident = nc.dram_tensor("ident", [128, 128], F32, kind="ExternalInput").ap()
    cols = {}
    for nm in ("bq", "bk", "bo2", "brq", "brk", "bg"):
        cols[nm] = nc.dram_tensor(f"c_{nm}", [128, 8], F32, kind="ExternalInput").ap()
    xout = nc.dram_tensor("xout", [H, T], F32, kind="ExternalOutput").ap()
    xspill = nc.dram_tensor("xspill", [H, T], F32, kind="Internal").ap()
    with ExitStack() as stack:
        mk = MK(nc, stack)
        mk.init_consts(ident)
        for nm in cols:
            mk.load_cols(cols[nm], nm)
        mk.load_x(xT)
        mk.emit_attn(ws["Wq"], ws["Wk"], ws["Wv"], ws["Wo"], ws["Wrq"], ws["Wrk"],
                     ws["Wrv"], Wg, pastT, brow, xspill, "a")
        outs = mk.store_x(xout)
        with nc.Block() as block:
            mk.p.emit(block, end_fence=outs)
    return nc


_PROG_CACHE = {}


def _prog(name):
    if name not in _PROG_CACHE:
        _PROG_CACHE[name] = {"ffn": build_ffn_prog, "rwkv": build_rwkv_prog,
                             "attn": build_attn_prog}[name]()
    return _PROG_CACHE[name]


def kernel(**inputs):
    f = np.float32
    x = np.asarray(inputs["L0_attn_x"], dtype=f)
    ident = _ident_np()
    tr = lambda a: np.ascontiguousarray(np.asarray(a, dtype=f).T)

    ffnw = {}
    for L in ("L0", "L1"):
        ffnw[L] = dict(wg=tr(inputs[f"{L}_ffn_w_gate"]),
                       wu=tr(inputs[f"{L}_ffn_w_up"]),
                       wd=tr(inputs[f"{L}_ffn_w_down"]))
    rwkvw = dict(wk=tr(inputs["L0_attn_Wk"]), wv=tr(inputs["L0_attn_Wv"]),
                 wr=tr(inputs["L0_attn_Wr"]), wo=tr(inputs["L0_attn_Wo"]),
                 c_tmk=_colify(np.asarray(inputs["L0_attn_time_mix_key"], f)[0, 0]),
                 c_tmv=_colify(np.asarray(inputs["L0_attn_time_mix_value"], f)[0, 0]),
                 c_tmr=_colify(np.asarray(inputs["L0_attn_time_mix_receptance"], f)[0, 0]),
                 c_tdec=_colify(np.asarray(inputs["L0_attn_time_decay"], f)),
                 c_tfirst=_colify(np.asarray(inputs["L0_attn_time_first"], f)),
                 c_bo=_colify(np.asarray(inputs["L0_attn_bo"], f)))
    attnw = dict(
        Wq=np.asarray(inputs["L1_attn_Wq"], f), Wk=np.asarray(inputs["L1_attn_Wk"], f),
        Wv=np.asarray(inputs["L1_attn_Wv"], f), Wo=np.asarray(inputs["L1_attn_Wo"], f),
        Wrq=np.asarray(inputs["L1_attn_Wrq"], f), Wrk=np.asarray(inputs["L1_attn_Wrk"], f),
        Wrv=np.asarray(inputs["L1_attn_Wrv"], f), Wg=np.asarray(inputs["L1_attn_Wg"], f),
        pastT=np.ascontiguousarray(
            np.asarray(inputs["L1_attn_past_states"], f)[:, 0].transpose(0, 2, 1)),
        brow=np.concatenate([np.asarray(inputs["L1_attn_bv"], f),
                             np.asarray(inputs["L1_attn_brv"], f)])[None, :],
        c_bq=_colify(np.asarray(inputs["L1_attn_bq"], f)),
        c_bk=_colify(np.asarray(inputs["L1_attn_bk"], f)),
        c_bo=_colify(np.asarray(inputs["L1_attn_bo"], f)),
        c_brq=_colify(np.asarray(inputs["L1_attn_brq"], f)),
        c_brk=_colify(np.asarray(inputs["L1_attn_brk"], f)),
        c_bg=_colify(np.asarray(inputs["L1_attn_bg"], f)))

    xT = [tr(x[b]) for b in range(N_CORES)]

    def launch(prog_name, extra):
        nc = _prog(prog_name)
        maps = [dict(xT=xT[b], ident=ident, **extra) for b in range(N_CORES)]
        res = run_bass_kernel_spmd(nc, maps, core_ids=list(range(N_CORES)))
        for b in range(N_CORES):
            xT[b] = np.ascontiguousarray(np.asarray(res.results[b]["xout"]))

    launch("ffn", ffnw["L0"])
    launch("rwkv", rwkvw)
    launch("ffn", ffnw["L0"])
    launch("ffn", ffnw["L1"])
    launch("attn", attnw)
    launch("ffn", ffnw["L1"])

    return np.stack([xT[b].T for b in range(N_CORES)]).astype(np.float32)


kernel_multilaunch = kernel


def kernel(**inputs):
    return kernel_fused(**inputs)


def build_fused_prog():
    nc = bass.Bass("TRN2", target_bir_lowering=False, debug=False)
    xT = nc.dram_tensor("xT", [H, T], F32, kind="ExternalInput").ap()
    ident = nc.dram_tensor("ident", [128, 128], F32, kind="ExternalInput").ap()
    ffn_w = {}
    for L in ("l0", "l1"):
        ffn_w[L] = dict(
            wg=nc.dram_tensor(f"{L}_wg", [H, INNER], F32, kind="ExternalInput").ap(),
            wu=nc.dram_tensor(f"{L}_wu", [H, INNER], F32, kind="ExternalInput").ap(),
            wd=nc.dram_tensor(f"{L}_wd", [INNER, H], F32, kind="ExternalInput").ap(),
            wdq=nc.dram_tensor(f"{L}_wdq", [INNER, H], BF16, kind="Internal").ap())
    rw = {nm: nc.dram_tensor(f"r_{nm}", [H, H], F32, kind="ExternalInput").ap()
          for nm in ("wk", "wv", "wr", "wo")}
    aw = {nm: nc.dram_tensor(f"a_{nm}", [H, H], F32, kind="ExternalInput").ap()
          for nm in ("Wq", "Wk", "Wv", "Wo", "Wrq", "Wrk", "Wrv")}
    Wg = nc.dram_tensor("a_Wg", [2 * H, H], F32, kind="ExternalInput").ap()
    pastT = nc.dram_tensor("pastT", [2, H, T], F32, kind="ExternalInput").ap()
    brow = nc.dram_tensor("brow", [1, 2048], F32, kind="ExternalInput").ap()
    cols = {}
    for nm in ("tmk", "tmv", "tmr", "tdec", "tfirst", "bo",
               "bq", "bk", "bo2", "brq", "brk", "bg"):
        cols[nm] = nc.dram_tensor(f"c_{nm}", [128, 8], F32, kind="ExternalInput").ap()
    xout = nc.dram_tensor("xout", [H, T], F32, kind="ExternalOutput").ap()
    xspill = nc.dram_tensor("xspill", [H, T], F32, kind="Internal").ap()
    with ExitStack() as stack:
        mk = MK(nc, stack)
        mk.init_consts(ident)
        for nm in cols:
            mk.load_cols(cols[nm], nm)
        mk.load_x(xT)
        mk.emit_ffn(ffn_w["l0"]["wg"], ffn_w["l0"]["wu"], ffn_w["l0"]["wd"],
                    ffn_w["l0"]["wdq"], "f0a")
        mk.emit_rwkv(rw["wk"], rw["wv"], rw["wr"], rw["wo"], xspill, "rk")
        mk.emit_ffn(ffn_w["l0"]["wg"], ffn_w["l0"]["wu"], ffn_w["l0"]["wd"],
                    ffn_w["l0"]["wdq"], "f0b")
        mk.emit_ffn(ffn_w["l1"]["wg"], ffn_w["l1"]["wu"], ffn_w["l1"]["wd"],
                    ffn_w["l1"]["wdq"], "f1a")
        mk.emit_attn(aw["Wq"], aw["Wk"], aw["Wv"], aw["Wo"], aw["Wrq"], aw["Wrk"],
                     aw["Wrv"], Wg, pastT, brow, xspill, "at")
        mk.emit_ffn(ffn_w["l1"]["wg"], ffn_w["l1"]["wu"], ffn_w["l1"]["wd"],
                    ffn_w["l1"]["wdq"], "f1b")
        outs = mk.store_x(xout)
        with nc.Block() as block:
            mk.p.emit(block, end_fence=outs)
    return nc


def kernel_fused(**inputs):
    f = np.float32
    x = np.asarray(inputs["L0_attn_x"], dtype=f)
    tr = lambda a: np.ascontiguousarray(np.asarray(a, dtype=f).T)
    base = dict(
        ident=_ident_np(),
        l0_wg=tr(inputs["L0_ffn_w_gate"]), l0_wu=tr(inputs["L0_ffn_w_up"]),
        l0_wd=tr(inputs["L0_ffn_w_down"]),
        l1_wg=tr(inputs["L1_ffn_w_gate"]), l1_wu=tr(inputs["L1_ffn_w_up"]),
        l1_wd=tr(inputs["L1_ffn_w_down"]),
        r_wk=tr(inputs["L0_attn_Wk"]), r_wv=tr(inputs["L0_attn_Wv"]),
        r_wr=tr(inputs["L0_attn_Wr"]), r_wo=tr(inputs["L0_attn_Wo"]),
        a_Wq=np.asarray(inputs["L1_attn_Wq"], f), a_Wk=np.asarray(inputs["L1_attn_Wk"], f),
        a_Wv=np.asarray(inputs["L1_attn_Wv"], f), a_Wo=np.asarray(inputs["L1_attn_Wo"], f),
        a_Wrq=np.asarray(inputs["L1_attn_Wrq"], f),
        a_Wrk=np.asarray(inputs["L1_attn_Wrk"], f),
        a_Wrv=np.asarray(inputs["L1_attn_Wrv"], f),
        a_Wg=np.asarray(inputs["L1_attn_Wg"], f),
        pastT=np.ascontiguousarray(
            np.asarray(inputs["L1_attn_past_states"], f)[:, 0].transpose(0, 2, 1)),
        brow=np.concatenate([np.asarray(inputs["L1_attn_bv"], f),
                             np.asarray(inputs["L1_attn_brv"], f)])[None, :],
        c_tmk=_colify(np.asarray(inputs["L0_attn_time_mix_key"], f)[0, 0]),
        c_tmv=_colify(np.asarray(inputs["L0_attn_time_mix_value"], f)[0, 0]),
        c_tmr=_colify(np.asarray(inputs["L0_attn_time_mix_receptance"], f)[0, 0]),
        c_tdec=_colify(np.asarray(inputs["L0_attn_time_decay"], f)),
        c_tfirst=_colify(np.asarray(inputs["L0_attn_time_first"], f)),
        c_bo=_colify(np.asarray(inputs["L0_attn_bo"], f)),
        c_bq=_colify(np.asarray(inputs["L1_attn_bq"], f)),
        c_bk=_colify(np.asarray(inputs["L1_attn_bk"], f)),
        c_bo2=_colify(np.asarray(inputs["L1_attn_bo"], f)),
        c_brq=_colify(np.asarray(inputs["L1_attn_brq"], f)),
        c_brk=_colify(np.asarray(inputs["L1_attn_brk"], f)),
        c_bg=_colify(np.asarray(inputs["L1_attn_bg"], f)),
    )
    if "fused" not in _PROG_CACHE:
        _PROG_CACHE["fused"] = build_fused_prog()
    nc = _PROG_CACHE["fused"]
    maps = [dict(xT=np.ascontiguousarray(x[b].T), **base) for b in range(N_CORES)]
    res = run_bass_kernel_spmd(nc, maps, core_ids=list(range(N_CORES)))
    return np.stack([np.asarray(res.results[b]["xout"]).T
                     for b in range(N_CORES)]).astype(np.float32)


# version 11
# speedup vs baseline: 1.1893x; 1.1893x over previous
import numpy as np
import ml_dtypes
import concourse.bass as bass
import concourse.mybir as mybir
from concourse.bass_utils import run_bass_kernel_spmd
from contextlib import ExitStack

F32 = mybir.dt.float32
BF16 = mybir.dt.bfloat16
I32 = mybir.dt.int32
I16 = mybir.dt.int16
AF = mybir.ActivationFunctionType
ALU = mybir.AluOpType
AX = mybir.AxisListType

N_CORES = 8
H = 1024
T = 2048
NT = 8
INNER = 4096
NI = 32
NH = 16
DH = 64
EPS = 1e-5
LADDER_K = 8

ENGINES = ("tensor", "vector", "scalar", "gpsimd", "sync")

R0, R1, R2, R3, RS = 0, 2048, 4096, 5632, 5632


class Op:
    __slots__ = ("idx", "engine", "fn", "reads", "writes", "deps", "is_dma",
                 "dma_sem", "dma_val", "pre_wait", "needs_inc", "eng_seq", "label")

    def __init__(self, idx, engine, fn, reads, writes, is_dma, label):
        self.idx = idx
        self.engine = engine
        self.fn = fn
        self.reads = reads
        self.writes = writes
        self.deps = []
        self.is_dma = is_dma
        self.dma_sem = None
        self.dma_val = 0
        self.pre_wait = None
        self.needs_inc = False
        self.eng_seq = 0
        self.label = label


class Prog:
    N_DMA_SEMS = 24

    def __init__(self, nc, stack):
        self.nc = nc
        self.stack = stack
        self.ops = []
        self.state = {}
        self.sems = {e: stack.enter_context(nc.semaphore(f"pg_{e}")) for e in ENGINES}
        self.dma_pool = [stack.enter_context(nc.semaphore(f"dma{i}"))
                         for i in range(self.N_DMA_SEMS)]
        self.dma_counts = [0] * self.N_DMA_SEMS
        self.dma_i = 0
        self.swdge_count = 0

    def sbuf(self, name, shape, dtype):
        return self.stack.enter_context(self.nc.sbuf_tensor(name, list(shape), dtype))

    def psum(self, name, shape, dtype=F32):
        return self.stack.enter_context(self.nc.psum_tensor(name, list(shape), dtype))

    def _dep_update(self, op):
        deps = {}
        for k in op.reads:
            st = self.state.get(k)
            if st is not None and st[0] is not None:
                deps[st[0].idx] = st[0]
        for k in op.writes:
            st = self.state.get(k)
            if st is not None:
                if st[0] is not None:
                    deps[st[0].idx] = st[0]
                for r in st[1]:
                    deps[r.idx] = r
        deps.pop(op.idx, None)
        op.deps = list(deps.values())
        for k in op.reads:
            self.state.setdefault(k, [None, []])[1].append(op)
        for k in op.writes:
            self.state[k] = [op, []]

    def op(self, engine, fn, reads=(), writes=(), label=""):
        o = Op(len(self.ops), engine, fn, tuple(reads), tuple(writes), False, label)
        self._dep_update(o)
        self.ops.append(o)
        return o

    def dma(self, out_ap, in_ap, reads=(), writes=(), queue="sync", label="", **kw):
        if queue == "gpsimd":
            sem = self.stack.enter_context(self.nc.semaphore(f"sw{self.swdge_count}"))
            self.swdge_count += 1
            o = Op(len(self.ops), "gpsimd",
                   lambda e, _o=out_ap, _i=in_ap, _kw=kw: e.dma_start(_o, _i, **_kw),
                   tuple(reads), tuple(writes), True, label)
            o.dma_sem, o.dma_val = sem, 16
        else:
            i = self.dma_i % self.N_DMA_SEMS
            self.dma_i += 1
            self.dma_counts[i] += 1
            o = Op(len(self.ops), "sync",
                   lambda e, _o=out_ap, _i=in_ap, _kw=kw: e.dma_start(_o, _i, **_kw),
                   tuple(reads), tuple(writes), True, label)
            o.dma_sem, o.dma_val = self.dma_pool[i], 16 * self.dma_counts[i]
            if self.dma_counts[i] > 1:
                o.pre_wait = (self.dma_pool[i], 16 * (self.dma_counts[i] - 1))
        self._dep_update(o)
        self.ops.append(o)
        return o

    def emit(self, block, end_fence=()):
        for o in self.ops:
            for d in o.deps:
                if not d.is_dma:
                    d.needs_inc = True
        counts = {e: 0 for e in ENGINES}
        for o in self.ops:
            if not o.is_dma:
                if o.needs_inc:
                    counts[o.engine] += 1
                o.eng_seq = counts[o.engine]
        per_engine = {e: [] for e in ENGINES}
        for o in self.ops:
            per_engine[o.engine].append(o)
        fence_list = list(end_fence)

        def make_body(ename, ops_here):
            def body(eng):
                hw = {e: 0 for e in ENGINES}
                hwd = {}
                for o in ops_here:
                    if o.pre_wait is not None:
                        s, v = o.pre_wait
                        if hwd.get(id(s), 0) < v:
                            eng.wait_ge(s, v)
                            hwd[id(s)] = v
                    for d in o.deps:
                        if d.is_dma:
                            if hwd.get(id(d.dma_sem), 0) < d.dma_val:
                                eng.wait_ge(d.dma_sem, d.dma_val)
                                hwd[id(d.dma_sem)] = d.dma_val
                        else:
                            if d.eng_seq > 0 and hw[d.engine] < d.eng_seq:
                                eng.wait_ge(self.sems[d.engine], d.eng_seq)
                                hw[d.engine] = d.eng_seq
                    inst = o.fn(eng)
                    if o.is_dma:
                        inst.then_inc(o.dma_sem, 16)
                    elif o.needs_inc:
                        inst.then_inc(self.sems[o.engine], 1)
                if ename == "sync":
                    for d in fence_list:
                        if hwd.get(id(d.dma_sem), 0) < d.dma_val:
                            eng.wait_ge(d.dma_sem, d.dma_val)
                            hwd[id(d.dma_sem)] = d.dma_val
            return body

        for ename in ENGINES:
            if per_engine[ename] or (ename == "sync" and fence_list):
                getattr(block, ename)(make_body(ename, per_engine[ename]))



def _kl(x):
    return list(x) if isinstance(x, (list, tuple)) else [x]

class MK:
    def __init__(self, nc, stack):
        self.nc = nc
        self.p = Prog(nc, stack)
        p = self.p
        self.A = p.sbuf("A", [128, 16384], F32)
        self.C = p.sbuf("C", [128, 16400], BF16)
        self.D = p.sbuf("Db", [128, 16384], BF16)
        self.E = p.sbuf("Eb", [128, 16384], BF16)
        self.S1 = p.sbuf("S1", [128, 2048], F32)
        self.SC = p.sbuf("SCb", [128, 6144], BF16)
        self.QI = p.sbuf("QI", [128, 512], I16)
        self.WL = p.sbuf("WL", [128, 1024], BF16)
        self.WF = p.sbuf("WF", [128, 1024], F32)
        self.ROWS = p.sbuf("ROWS", [1, 4096], F32)
        self.QROWF = p.sbuf("QROWF", [1, 512], F32)
        self.COLS = p.sbuf("COLS", [128, 96], F32)
        self.CCOL = p.sbuf("CCOL", [128, 96], F32)
        self.ONESC = p.sbuf("ONESC", [128, 1], BF16)
        self.ONESCF = p.sbuf("ONESCF", [128, 1], F32)
        self.ONESR = p.sbuf("ONESR", [1, 128], BF16)
        self.ONESRF = p.sbuf("ONESRF", [1, 128], F32)
        self.ONES127 = p.sbuf("ONES127", [1, 128], F32)
        self.EPSR = p.sbuf("EPSR", [1, 1], F32)
        self.SCL = p.sbuf("SCL", [1, 16], F32)
        self.IDENT = p.sbuf("IDENT", [128, 128], F32)
        self.PS = [p.psum(f"ps{i}", [128, 512], F32) for i in range(8)]
        self.PK = [f"P{i}" for i in range(8)]
        self.col_slots = {}
        self.col_next = 0

    def Av(self, i, c0=0, n=2048):
        return self.A[:, i * 2048 + c0: i * 2048 + c0 + n]

    def Cv(self, i, c0=0, n=2048):
        return self.C[:, i * 2048 + c0: i * 2048 + c0 + n]

    def init_consts(self, ident_hbm):
        p = self.p
        p.op("vector", lambda e: e.memset(self.ONESC[:], 1.0), writes=["ONESC"])
        p.op("vector", lambda e: e.memset(self.ONESCF[:], 1.0), writes=["ONESCF"])
        p.op("vector", lambda e: e.memset(self.ONESR[:], 1.0), writes=["ONESR"])
        p.op("vector", lambda e: e.memset(self.ONESRF[:], 1.0), writes=["ONESRF"])
        p.op("vector", lambda e: e.memset(self.ONES127[:], 1.0 / 127.0), writes=["ONES127"])
        p.op("vector", lambda e: e.memset(self.EPSR[:], EPS), writes=["EPSR"])
        p.dma(self.IDENT[:], ident_hbm, writes=["IDENT"], label="ident")

    def load_cols(self, hbm_ap, name):
        s = self.col_next
        self.col_next += 8
        assert self.col_next <= 96
        self.col_slots[name] = s
        self.p.dma(self.COLS[:, s:s + 8], hbm_ap, writes=[f"COLS.{name}"],
                   label=f"col.{name}")

    def col(self, name, i):
        s = self.col_slots[name]
        return self.COLS[:, s + i:s + i + 1]

    def load_x(self, x_hbm):
        for i in range(NT):
            self.p.dma(self.Av(i), x_hbm[i * 128:(i + 1) * 128, :],
                       writes=[f"A.{i}"], label=f"ldx{i}")

    def store_x(self, x_hbm):
        outs = []
        for i in range(NT):
            outs.append(self.p.dma(x_hbm[i * 128:(i + 1) * 128, :], self.Av(i),
                                   reads=[f"A.{i}"], label=f"stx{i}"))
        return outs

    def emit_ln(self, x_ap, x_key, y_ap, y_key, tag, normalize=True):
        p = self.p
        xb = self.SC[:, R0:R0 + 2048]
        x2 = self.SC[:, R1:R1 + 2048]
        for i in range(NT):
            p.op("vector", lambda e, i=i: e.tensor_copy(xb, x_ap(i)),
                 reads=_kl(x_key(i)), writes=["SC.A"],
                 label=f"{tag}.cast{i}")
            p.op("scalar", lambda e: e.activation(x2, xb, AF.Square),
                 reads=["SC.A"], writes=["SC.B"], label=f"{tag}.sq{i}")
            for c in range(4):
                r = (c // 2) * 32
                p.op("tensor", lambda e, i=i, c=c, r=r: e.matmul(
                    self.PS[4 + c % 2][r:r + 1, 0:512],
                    self.ONESC[:], xb[:, c * 512:(c + 1) * 512],
                    start=(i == 0), stop=(i == NT - 1)),
                    reads=["SC.A", "ONESC"], writes=[self.PK[4 + c % 2]],
                    label=f"{tag}.s{i}.{c}")
                p.op("tensor", lambda e, i=i, c=c, r=r: e.matmul(
                    self.PS[6 + c % 2][r:r + 1, 0:512],
                    self.ONESC[:], x2[:, c * 512:(c + 1) * 512],
                    start=(i == 0), stop=(i == NT - 1)),
                    reads=["SC.B", "ONESC"], writes=[self.PK[6 + c % 2]],
                    label=f"{tag}.q{i}.{c}")
        row = self.ROWS[0:1, 0:2048]
        rowr = self.ROWS[0:1, 2048:4096]
        for c in range(4):
            r = (c // 2) * 32
            p.op("vector", lambda e, c=c, r=r: e.tensor_scalar_mul(
                row[:, c * 512:(c + 1) * 512],
                self.PS[4 + c % 2][r:r + 1, 0:512], 1.0 / H),
                reads=[self.PK[4 + c % 2]], writes=["ROWS"], label=f"{tag}.mu{c}")
        if normalize:
            for c in range(4):
                p.op("tensor", lambda e, c=c: e.matmul(
                    self.PS[c][:, 0:512], self.ONESRF[:],
                    row[:, c * 512:(c + 1) * 512], start=True, stop=True),
                    reads=["ROWS", "ONESRF"], writes=[self.PK[c]],
                    label=f"{tag}.repmu{c}")
        p.op("scalar", lambda e: e.activation(rowr, row, AF.Square),
             reads=["ROWS"], writes=["ROWS"], label=f"{tag}.mu2")
        for c in range(4):
            r = (c // 2) * 32
            p.op("vector", lambda e, c=c, r=r: e.scalar_tensor_tensor(
                rowr[:, c * 512:(c + 1) * 512],
                self.PS[6 + c % 2][r:r + 1, 0:512], 1.0 / H,
                rowr[:, c * 512:(c + 1) * 512], ALU.mult, ALU.subtract),
                reads=[self.PK[6 + c % 2], "ROWS"], writes=["ROWS"],
                label=f"{tag}.var{c}")
        p.op("scalar", lambda e: e.activation(rowr, rowr, AF.Sqrt, bias=self.EPSR[0:1, 0:1]),
             reads=["ROWS", "EPSR"], writes=["ROWS"], label=f"{tag}.sd")
        p.op("vector", lambda e: e.reciprocal(rowr, rowr),
             reads=["ROWS"], writes=["ROWS"], label=f"{tag}.rstd")
        if not normalize:
            return
        for c in range(4):
            p.op("tensor", lambda e, c=c: e.matmul(
                self.PS[4 + c][:, 0:512], self.ONESRF[:],
                rowr[:, c * 512:(c + 1) * 512], start=True, stop=True),
                reads=["ROWS", "ONESRF"], writes=[self.PK[4 + c]],
                label=f"{tag}.reprs{c}")
        for c in range(4):
            for i in range(NT):
                p.op("vector", lambda e, i=i, c=c: e.scalar_tensor_tensor(
                    self.S1[:, 0:512], self.PS[c][:, 0:512], -1.0,
                    x_ap(i)[:, c * 512:(c + 1) * 512], ALU.mult, ALU.add),
                    reads=[self.PK[c]] + _kl(x_key(i)), writes=["S1"],
                    label=f"{tag}.sub{i}.{c}")
                p.op("vector", lambda e, i=i, c=c: e.tensor_tensor(
                    y_ap(i)[:, c * 512:(c + 1) * 512], self.S1[:, 0:512],
                    self.PS[4 + c][:, 0:512], ALU.mult),
                    reads=["S1", self.PK[4 + c]], writes=_kl(y_key(i)),
                    label=f"{tag}.nrm{i}.{c}")

    def emit_quant_scales(self, src_ap, src_key, nsrc, tlen, tag):
        p = self.p
        ntile = tlen // 128
        s1 = self.S1[:, 0:tlen]
        sqt = self.SC[:, R1:R1 + tlen] if tlen <= 2048 else None
        p.op("vector", lambda e: e.tensor_tensor(s1, src_ap(0), src_ap(0), ALU.mult),
             reads=[src_key(0)], writes=["S1"], label=f"{tag}.m0")
        for j in range(1, nsrc):
            p.op("vector", lambda e, j=j: e.tensor_tensor(sqt, src_ap(j), src_ap(j), ALU.mult),
                 reads=_kl(src_key(j)), writes=["SC.B"], label=f"{tag}.msq{j}")
            p.op("vector", lambda e: e.tensor_tensor(s1, s1, sqt, ALU.max),
                 reads=["S1", "SC.B"], writes=["S1"], label=f"{tag}.m{j}")
        for tt in range(ntile):
            p.op("tensor", lambda e, tt=tt: e.transpose(
                self.PS[4][:, 0:128], s1[:, tt * 128:(tt + 1) * 128], self.IDENT[:]),
                reads=["S1", "IDENT"], writes=[self.PK[4]], label=f"{tag}.tr{tt}")
            p.op("vector", lambda e, tt=tt, ntile=ntile: e.reduce_max(
                self.CCOL[:, ntile + tt:ntile + tt + 1], self.PS[4][:, 0:128], axis=AX.X),
                reads=[self.PK[4]], writes=["CCOL"], label=f"{tag}.red{tt}")
        cc = self.CCOL[:, ntile:2 * ntile]
        qq = self.CCOL[:, 0:ntile]
        p.op("scalar", lambda e: e.activation(cc, cc, AF.Sqrt),
             reads=["CCOL"], writes=["CCOL"], label=f"{tag}.sqrt")
        p.op("vector", lambda e: e.tensor_scalar_max(cc, cc, 1e-5),
             reads=["CCOL"], writes=["CCOL"], label=f"{tag}.clip")
        p.op("vector", lambda e: e.reciprocal(qq, cc),
             reads=["CCOL"], writes=["CCOL"], label=f"{tag}.inv")
        p.op("vector", lambda e: e.tensor_scalar_mul(qq, qq, 127.0),
             reads=["CCOL"], writes=["CCOL"], label=f"{tag}.q")
        p.op("vector", lambda e: e.tensor_scalar_mul(cc, cc, 1.0 / 127.0),
             reads=["CCOL"], writes=["CCOL"], label=f"{tag}.d")
        p.op("tensor", lambda e, n2=2 * ntile: e.transpose(
            self.PS[4][0:n2, 0:128], self.CCOL[:, 0:n2], self.IDENT[:]),
            reads=["CCOL", "IDENT"], writes=[self.PK[4]], label=f"{tag}.tr2")
        p.op("vector", lambda e, n2=2 * ntile: e.tensor_copy(
            self.WF[0:n2, 0:128], self.PS[4][0:n2, 0:128]),
            reads=[self.PK[4]], writes=["WF"], label=f"{tag}.evqf")
        nch = (tlen + 511) // 512
        for c in range(nch):
            w = min(512, tlen - c * 512)
            nr = (w + 127) // 128
            p.dma(self.QROWF[0:1, 0:w],
                  self.WF[ntile + 4 * c:ntile + 4 * c + nr, 0:128],
                  reads=["WF", "QROWF"], writes=["QROWF"],
                  label=f"{tag}.lind{c}")
            p.op("tensor", lambda e, w=w: e.matmul(
                self.PS[4][:, 0:w], self.ONESRF[:], self.QROWF[0:1, 0:w],
                start=True, stop=True),
                reads=["QROWF", "ONESRF"], writes=[self.PK[4]],
                label=f"{tag}.rd{c}")
            p.op("vector", lambda e, c=c, w=w: e.tensor_copy(
                self.SC[:, c * 512:c * 512 + w], self.PS[4][:, 0:w]),
                reads=[self.PK[4]], writes=["SC.A"],
                label=f"{tag}.red{c}")

    def _ladder(self, v, dst_ap, dst_key, n, tag):
        p = self.p
        t0 = self.SC[:, R2:R2 + n]
        ca = self.SC[:, R2 + 512:R2 + 512 + n]
        md = self.SC[:, R2 + 1024:R2 + 1024 + n]
        p.op("vector", lambda e: e.tensor_copy(self.QI[:, 0:n], v),
             reads=["S1"], writes=["QI"], label=f"{tag}.i")
        p.op("vector", lambda e: e.tensor_copy(t0, self.QI[:, 0:n]),
             reads=["QI"], writes=["SC.C"], label=f"{tag}.t0")
        p.op("vector", lambda e: e.tensor_tensor(md, t0, v, ALU.subtract),
             reads=["SC.C", "S1"], writes=["SC.C"], label=f"{tag}.md")
        p.op("vector", lambda e: e.tensor_scalar(dst_ap, md, -0.5, None, ALU.is_le),
             reads=["SC.C"], writes=_kl(dst_key), label=f"{tag}.c1")
        p.op("vector", lambda e: e.tensor_tensor(dst_ap, t0, dst_ap, ALU.add),
             reads=["SC.C"] + _kl(dst_key), writes=_kl(dst_key), label=f"{tag}.a1")
        _LVL = ([(-0.5 - k, ALU.add, ALU.is_le) for k in range(1, LADDER_K)]
                + [(0.5 + k, ALU.subtract, ALU.is_ge) for k in range(LADDER_K)])
        for (thr, op, alu) in _LVL:
            p.op("vector", lambda e, thr=thr, alu=alu: e.tensor_scalar(
                ca, md, thr, None, alu),
                reads=["SC.C"], writes=["SC.C"], label=f"{tag}.c{thr}")
            p.op("vector", lambda e, op=op: e.tensor_tensor(dst_ap, dst_ap, ca, op),
                 reads=_kl(dst_key) + ["SC.C"], writes=_kl(dst_key), label=f"{tag}.a{thr}")

    def _ladder_sar(self, v, dst_ap, dst_key, n, tag):
        p = self.p
        b = self.SC[:, R2 + 1024:R2 + 1024 + n]
        vc = self.S1[:, 1024:1024 + n]
        c = self.S1[:, 1536:1536 + n]
        p.op("vector", lambda e: e.tensor_scalar(
            vc, v, 127.49, -127.49, ALU.min, ALU.max),
            reads=["S1"], writes=["S1"], label=f"{tag}.cl")
        p.op("vector", lambda e: e.tensor_copy(self.QI[:, 0:n], vc),
             reads=["S1"], writes=["QI"], label=f"{tag}.i")
        p.op("vector", lambda e: e.tensor_copy(c, self.QI[:, 0:n]),
             reads=["QI"], writes=["S1"], label=f"{tag}.c")
        p.op("vector", lambda e: e.tensor_tensor(vc, vc, c, ALU.subtract),
             reads=["S1"], writes=["S1"], label=f"{tag}.d")
        p.op("vector", lambda e: e.tensor_scalar(b, vc, 0.5, None, ALU.is_ge),
             reads=["S1"], writes=["SC.C"], label=f"{tag}.t")
        p.op("vector", lambda e: e.scalar_tensor_tensor(
            b, vc, -0.5, b, ALU.is_le, ALU.subtract),
            reads=["S1", "SC.C"], writes=["SC.C"], label=f"{tag}.u")
        p.op("vector", lambda e: e.tensor_tensor(dst_ap, c, b, ALU.subtract),
             reads=["S1", "SC.C"], writes=_kl(dst_key), label=f"{tag}.fin")

    def emit_quant1(self, tag):
        p = self.p
        for c in range(4):
            p.op("tensor", lambda e, c=c: e.matmul(
                self.PS[6][:, 0:512], self.ONESRF[:],
                self.ROWS[0:1, c * 512:(c + 1) * 512], start=True, stop=True),
                reads=["ROWS", "ONESRF"], writes=[self.PK[6]], label=f"{tag}.m{c}")
            for i in range(NT):
                p.op("vector", lambda e, i=i, c=c: e.scalar_tensor_tensor(
                    self.S1[:, 0:512], self.PS[6][:, 0:512], -1.0,
                    self.Av(i, c * 512, 512), ALU.mult, ALU.add),
                    reads=[self.PK[6], f"A.{i}"], writes=["S1"], label=f"{tag}.d{c}.{i}")
                p.op("vector", lambda e: e.tensor_tensor(
                    self.S1[:, 512:1024], self.S1[:, 0:512], self.S1[:, 0:512], ALU.mult),
                    reads=["S1"], writes=["S1"], label=f"{tag}.sq{c}.{i}")
                if i == 0:
                    p.op("vector", lambda e: e.tensor_copy(
                        self.S1[:, 1024:1536], self.S1[:, 512:1024]),
                        reads=["S1"], writes=["S1"], label=f"{tag}.M0.{c}")
                else:
                    p.op("vector", lambda e: e.tensor_tensor(
                        self.S1[:, 1024:1536], self.S1[:, 1024:1536],
                        self.S1[:, 512:1024], ALU.max),
                        reads=["S1"], writes=["S1"], label=f"{tag}.M{c}.{i}")
            for tt in range(4):
                p.op("tensor", lambda e, tt=tt: e.transpose(
                    self.PS[4][:, 0:128],
                    self.S1[:, 1024 + tt * 128:1024 + (tt + 1) * 128], self.IDENT[:]),
                    reads=["S1", "IDENT"], writes=[self.PK[4]], label=f"{tag}.tr{c}.{tt}")
                p.op("vector", lambda e, c=c, tt=tt: e.reduce_max(
                    self.CCOL[:, 16 + 4 * c + tt:16 + 4 * c + tt + 1],
                    self.PS[4][:, 0:128], axis=AX.X),
                    reads=[self.PK[4]], writes=["CCOL"], label=f"{tag}.red{c}.{tt}")
        p.op("scalar", lambda e: e.activation(
            self.CCOL[:, 16:32], self.CCOL[:, 16:32], AF.Sqrt),
            reads=["CCOL"], writes=["CCOL"], label=f"{tag}.sqrt")
        p.op("tensor", lambda e: e.transpose(
            self.PS[4][0:16, 0:128], self.CCOL[:, 16:32], self.IDENT[:]),
            reads=["CCOL", "IDENT"], writes=[self.PK[4]], label=f"{tag}.tr2")
        p.op("vector", lambda e: e.tensor_copy(
            self.WF[0:16, 0:128], self.PS[4][0:16, 0:128]),
            reads=[self.PK[4]], writes=["WF"], label=f"{tag}.evqf")
        for c in range(4):
            p.dma(self.QROWF[0:1, 0:512], self.WF[4 * c:4 * c + 4, 0:128],
                  reads=["WF", "QROWF"], writes=["QROWF"], label=f"{tag}.ql{c}")
            p.op("vector", lambda e, c=c: e.tensor_tensor(
                self.QROWF[0:1, 0:512], self.QROWF[0:1, 0:512],
                self.ROWS[0:1, 2048 + c * 512:2048 + (c + 1) * 512], ALU.mult),
                reads=["QROWF", "ROWS"], writes=["QROWF"], label=f"{tag}.my{c}")
            p.op("vector", lambda e: e.tensor_scalar_max(
                self.QROWF[0:1, 0:512], self.QROWF[0:1, 0:512], 1e-5),
                reads=["QROWF"], writes=["QROWF"], label=f"{tag}.cl{c}")
            p.op("tensor", lambda e: e.matmul(
                self.PS[4][:, 0:512], self.ONES127[:], self.QROWF[0:1, 0:512],
                start=True, stop=True),
                reads=["QROWF", "ONES127"], writes=[self.PK[4]], label=f"{tag}.dr{c}")
            p.op("vector", lambda e, c=c: e.tensor_copy(
                self.SC[:, c * 512:(c + 1) * 512], self.PS[4][:, 0:512]),
                reads=[self.PK[4]], writes=["SC.A"], label=f"{tag}.de{c}")
            p.op("vector", lambda e: e.reciprocal(
                self.QROWF[0:1, 0:512], self.QROWF[0:1, 0:512]),
                reads=["QROWF"], writes=["QROWF"], label=f"{tag}.rc{c}")
            p.op("vector", lambda e: e.tensor_scalar_mul(
                self.QROWF[0:1, 0:512], self.QROWF[0:1, 0:512], 127.0),
                reads=["QROWF"], writes=["QROWF"], label=f"{tag}.q{c}")
            p.op("tensor", lambda e: e.matmul(
                self.PS[5][:, 0:512], self.ONESRF[:], self.QROWF[0:1, 0:512],
                start=True, stop=True),
                reads=["QROWF", "ONESRF"], writes=[self.PK[5]], label=f"{tag}.qr{c}")
            p.op("tensor", lambda e, c=c: e.matmul(
                self.PS[6][:, 0:512], self.ONESRF[:],
                self.ROWS[0:1, c * 512:(c + 1) * 512], start=True, stop=True),
                reads=["ROWS", "ONESRF"], writes=[self.PK[6]], label=f"{tag}.mr{c}")
            p.op("tensor", lambda e, c=c: e.matmul(
                self.PS[7][:, 0:512], self.ONESRF[:],
                self.ROWS[0:1, 2048 + c * 512:2048 + (c + 1) * 512],
                start=True, stop=True),
                reads=["ROWS", "ONESRF"], writes=[self.PK[7]], label=f"{tag}.rr{c}")
            for s in range(NT):
                v = self.S1[:, 0:512]
                p.op("vector", lambda e, s=s, c=c: e.scalar_tensor_tensor(
                    v, self.PS[6][:, 0:512], -1.0, self.Av(s, c * 512, 512),
                    ALU.mult, ALU.add),
                    reads=[self.PK[6], f"A.{s}"], writes=["S1"],
                    label=f"{tag}.v1.{c}.{s}")
                p.op("vector", lambda e: e.tensor_tensor(
                    v, v, self.PS[7][:, 0:512], ALU.mult),
                    reads=["S1", self.PK[7]], writes=["S1"], label=f"{tag}.v2.{c}.{s}")
                p.op("vector", lambda e: e.tensor_tensor(
                    v, v, self.PS[5][:, 0:512], ALU.mult),
                    reads=["S1", self.PK[5]], writes=["S1"], label=f"{tag}.v3.{c}.{s}")
                self._ladder_sar(v, self.Cv(s, c * 512, 512), f"C.{s}", 512,
                                 f"{tag}.L{c}.{s}")

    def emit_quant_apply(self, mode, src_ap, src_key, dst_ap, dst_key, nsrc, tlen, tag):
        p = self.p
        for c in range(tlen // 512):
            p.dma(self.QROWF[0:1, 0:512], self.WF[4 * c:4 * c + 4, 0:128],
                  reads=["WF", "QROWF"], writes=["QROWF"], label=f"{tag}.ql{c}")
            p.op("tensor", lambda e: e.matmul(
                self.PS[5][:, 0:512], self.ONESRF[:], self.QROWF[0:1, 0:512],
                start=True, stop=True),
                reads=["QROWF", "ONESRF"], writes=[self.PK[5]], label=f"{tag}.qr{c}")
            if mode == "ln":
                p.op("tensor", lambda e, c=c: e.matmul(
                    self.PS[6][:, 0:512], self.ONESRF[:],
                    self.ROWS[0:1, c * 512:(c + 1) * 512], start=True, stop=True),
                    reads=["ROWS", "ONESRF"], writes=[self.PK[6]], label=f"{tag}.mr{c}")
                p.op("tensor", lambda e, c=c: e.matmul(
                    self.PS[7][:, 0:512], self.ONESRF[:],
                    self.ROWS[0:1, 2048 + c * 512:2048 + (c + 1) * 512],
                    start=True, stop=True),
                    reads=["ROWS", "ONESRF"], writes=[self.PK[7]], label=f"{tag}.rr{c}")
            for s in range(nsrc):
                sap = src_ap(s)[:, c * 512:(c + 1) * 512]
                dap = dst_ap(s)[:, c * 512:(c + 1) * 512]
                v = self.S1[:, 0:512]
                if mode == "ln":
                    p.op("vector", lambda e, sap=sap: e.scalar_tensor_tensor(
                        v, self.PS[6][:, 0:512], -1.0, sap, ALU.mult, ALU.add),
                        reads=[self.PK[6]] + _kl(src_key(s)), writes=["S1"],
                        label=f"{tag}.v1.{c}.{s}")
                    p.op("vector", lambda e: e.tensor_tensor(
                        v, v, self.PS[7][:, 0:512], ALU.mult),
                        reads=["S1", self.PK[7]], writes=["S1"], label=f"{tag}.v2.{c}.{s}")
                    p.op("vector", lambda e: e.tensor_tensor(
                        v, v, self.PS[5][:, 0:512], ALU.mult),
                        reads=["S1", self.PK[5]], writes=["S1"], label=f"{tag}.v3.{c}.{s}")
                    self._ladder_sar(v, dap, dst_key(s), 512, f"{tag}.L{c}.{s}")
                else:
                    p.op("vector", lambda e, sap=sap: e.tensor_tensor(
                        v, sap, self.PS[5][:, 0:512], ALU.mult),
                        reads=_kl(src_key(s)) + [self.PK[5]], writes=["S1"],
                        label=f"{tag}.v.{c}.{s}")
                    self._ladder_sar(v, dap, dst_key(s), 512, f"{tag}.L{c}.{s}")

    def emit_wstats(self, hbm_w, rows, cols, slot, extra_mul, tag):
        p = self.p
        n = rows * cols
        view = hbm_w.rearrange("(q p) c -> p q c", p=128)
        nq = rows // 128
        csz = min(cols, 2048)
        chunks = [(q, c0) for q in range(nq) for c0 in range(0, cols, csz)]
        acc = self.CCOL[:, 44:45]
        first = True
        for ch, (q, c0) in enumerate(chunks):
            p.dma(self.S1[:, 0:csz], view[:, q, c0:c0 + csz],
                  writes=["S1"], label=f"{tag}.ld{ch}")
            if first:
                p.op("vector", lambda e, csz=csz: e.reduce_sum(
                    acc, self.S1[:, 0:csz], axis=AX.X, apply_absolute_value=True),
                    reads=["S1"], writes=["CCOL"], label=f"{tag}.r{ch}")
                first = False
            else:
                p.op("vector", lambda e, csz=csz: e.reduce_sum(
                    self.CCOL[:, 45:46], self.S1[:, 0:csz], axis=AX.X,
                    apply_absolute_value=True),
                    reads=["S1"], writes=["CCOL"], label=f"{tag}.r{ch}")
                p.op("vector", lambda e: e.tensor_add(acc, acc, self.CCOL[:, 45:46]),
                     reads=["CCOL", "CCOL"], writes=["CCOL"], label=f"{tag}.a{ch}")
        p.op("tensor", lambda e: e.matmul(self.PS[4][0:1, 0:1], acc, self.ONESCF[:],
                                          start=True, stop=True),
             reads=["CCOL", "ONESCF"], writes=[self.PK[4]], label=f"{tag}.x")
        p.op("vector", lambda e: e.tensor_scalar_mul(self.SCL[0:1, 0:1],
                                                     self.PS[4][0:1, 0:1], 1.0 / n),
             reads=[self.PK[4]], writes=["SCL"], label=f"{tag}.mean")
        p.op("vector", lambda e: e.tensor_scalar_max(self.SCL[0:1, 0:1],
                                                     self.SCL[0:1, 0:1], 1e-5),
             reads=["SCL"], writes=["SCL"], label=f"{tag}.clip")
        p.op("tensor", lambda e: e.matmul(self.PS[4][:, 0:1], self.ONESRF[:],
                                          self.SCL[0:1, 0:1], start=True, stop=True),
             reads=["SCL", "ONESRF"], writes=[self.PK[4]], label=f"{tag}.rep")
        p.op("vector", lambda e: e.tensor_scalar_mul(
            self.CCOL[:, slot:slot + 1], self.PS[4][:, 0:1], 0.5),
            reads=[self.PK[4]], writes=["CCOL"], label=f"{tag}.thr")
        p.op("vector", lambda e: e.tensor_scalar_mul(
            self.CCOL[:, slot + 1:slot + 2], self.PS[4][:, 0:1], -0.5),
            reads=[self.PK[4]], writes=["CCOL"], label=f"{tag}.nthr")
        p.op("vector", lambda e, em=extra_mul: e.tensor_scalar_mul(
            self.CCOL[:, slot + 2:slot + 3], self.PS[4][:, 0:1], em),
            reads=[self.PK[4]], writes=["CCOL"], label=f"{tag}.sw")

    def emit_ffn(self, wg, wu, wd, wdq, tag):
        p = self.p
        self.emit_wstats(wg, H, INNER, 32, 1.0, f"{tag}.wsg")
        self.emit_wstats(wu, H, INNER, 36, 1.0, f"{tag}.wsu")
        self.emit_wstats(wd, INNER, H, 40, 0.5, f"{tag}.wsd")
        wdv = wd.rearrange("(q p) c -> p q c", p=128)
        wqv = wdq.rearrange("(q p) c -> p q c", p=128)
        for q in range(NI):
            p.dma(self.WF[:, 0:1024], wdv[:, q, :], writes=["WF"], label=f"{tag}.dl{q}")
            p.op("vector", lambda e: e.tensor_scalar(
                self.WL[:, 0:1024], self.WF[:, 0:1024], self.CCOL[:, 40:41], None, ALU.is_ge),
                reads=["WF", "CCOL"], writes=["WLa"], label=f"{tag}.d1{q}")
            p.op("vector", lambda e: e.tensor_scalar(
                self.SC[:, 2048:3072], self.WF[:, 0:1024], self.CCOL[:, 41:42], None, ALU.is_le),
                reads=["WF", "CCOL"], writes=["SC.B"], label=f"{tag}.d2{q}")
            p.op("vector", lambda e: e.tensor_sub(
                self.WL[:, 0:1024], self.WL[:, 0:1024], self.SC[:, 2048:3072]),
                reads=["WLa", "SC.B"], writes=["WLa"], label=f"{tag}.d3{q}")
            p.dma(wqv[:, q, :], self.WL[:, 0:1024], reads=["WLa"], label=f"{tag}.ds{q}")
        self.emit_ln(lambda i: self.Av(i), lambda i: f"A.{i}",
                     lambda i: self.Cv(i), lambda i: f"C.{i}", f"{tag}.ln",
                     normalize=False)
        self.emit_quant1(f"{tag}.q1")
        wgv = wg.rearrange("(j p) i -> p j i", p=128)
        wuv = wu.rearrange("(j p) i -> p j i", p=128)
        for hh in range(2):
            for it in range(NI):
                pr = (self.D if it < 16 else self.E)
                poff = (it % 16) * 1024
                pkey = f"{'D' if it < 16 else 'E'}.{it % 16}"
                c2s = self.SC[:, R2:R2 + 1024]
                for (nm, off, sl, wv) in (("g", 0, 32, wgv), ("u", 1024, 36, wuv)):
                    wk = "WLa" if off == 0 else "SC.B"
                    p.dma(self.WF[:, 0:1024].rearrange("p (j f) -> p j f", f=128),
                          wv[:, :, it * 128:(it + 1) * 128], writes=["WF"],
                          label=f"{tag}.{nm}{hh}.{it}.ld")
                    tdst = (self.WL[:, 0:1024] if off == 0
                            else self.SC[:, 2048:3072])
                    p.op("vector", lambda e, sl=sl, tdst=tdst: e.tensor_scalar(
                        tdst, self.WF[:, 0:1024],
                        self.CCOL[:, sl:sl + 1], None, ALU.is_ge),
                        reads=["WF", "CCOL"], writes=[wk],
                        label=f"{tag}.t1{nm}{hh}.{it}")
                    p.op("vector", lambda e, off=off, sl=sl, c2s=c2s: e.tensor_scalar(
                        c2s, self.WF[:, 0:1024],
                        self.CCOL[:, sl + 1:sl + 2], None, ALU.is_le),
                        reads=["WF", "CCOL"], writes=["SC.C"],
                        label=f"{tag}.t2{nm}{hh}.{it}")
                    p.op("vector", lambda e, c2s=c2s, tdst=tdst: e.tensor_sub(
                        tdst, tdst, c2s),
                        reads=[wk, "SC.C"], writes=[wk],
                        label=f"{tag}.t3{nm}{hh}.{it}")
                for c in range(2):
                    t0 = hh * 1024 + c * 512
                    def gmm(e, it=it, t0=t0):
                        last = None
                        for j in range(NT):
                            last = e.matmul(self.PS[0][:, 0:512],
                                            self.WL[:, j * 128:(j + 1) * 128],
                                            self.Cv(j, t0, 512),
                                            start=(j == 0), stop=(j == NT - 1))
                        return last
                    p.op("tensor", gmm,
                         reads=["WLa"] + [f"C.{j}" for j in range(NT)],
                         writes=[self.PK[0]], label=f"{tag}.mmg{hh}.{it}.{c}")
                    def umm(e, it=it, t0=t0):
                        last = None
                        for j in range(NT):
                            last = e.matmul(self.PS[1][:, 0:512],
                                            self.SC[:, 2048 + j * 128:2048 + (j + 1) * 128],
                                            self.Cv(j, t0, 512),
                                            start=(j == 0), stop=(j == NT - 1))
                        return last
                    p.op("tensor", umm,
                         reads=["SC.B"] + [f"C.{j}" for j in range(NT)],
                         writes=[self.PK[1]], label=f"{tag}.mmu{hh}.{it}.{c}")
                    p.op("vector", lambda e, t0=t0: e.scalar_tensor_tensor(
                        self.S1[:, 0:512], self.PS[0][:, 0:512], self.CCOL[:, 34:35],
                        self.SC[:, t0:t0 + 512], ALU.mult, ALU.mult),
                        reads=[self.PK[0], "CCOL", "SC.A"], writes=["S1"],
                        label=f"{tag}.eg{hh}.{it}.{c}")
                    p.op("scalar", lambda e: e.activation(
                        self.SC[:, R3:R3 + 512], self.S1[:, 0:512], AF.Silu),
                        reads=["S1"], writes=["SC.D"], label=f"{tag}.sil{hh}.{it}.{c}")
                    p.op("vector", lambda e, t0=t0: e.scalar_tensor_tensor(
                        self.S1[:, 512:1024], self.PS[1][:, 0:512], self.CCOL[:, 38:39],
                        self.SC[:, t0:t0 + 512], ALU.mult, ALU.mult),
                        reads=[self.PK[1], "CCOL", "SC.A"], writes=["S1"],
                        label=f"{tag}.eu{hh}.{it}.{c}")
                    p.op("vector", lambda e, pr=pr, poff=poff, c=c: e.tensor_tensor(
                        pr[:, poff + c * 512:poff + c * 512 + 512],
                        self.SC[:, R3:R3 + 512], self.S1[:, 512:1024], ALU.mult),
                        reads=["SC.D", "S1"], writes=[pkey],
                        label=f"{tag}.p{hh}.{it}.{c}")
            def psrc(j, hh=hh):
                pr = (self.D if j < 16 else self.E)
                return pr[:, (j % 16) * 1024:(j % 16) * 1024 + 1024]
            def psrck(j):
                return f"{'D' if j < 16 else 'E'}.{j % 16}"
            self.emit_quant_scales(psrc, psrck, NI, 1024, f"{tag}.q2{hh}")
            self.emit_quant_apply("p", psrc, psrck, psrc, psrck,
                                  NI, 1024, f"{tag}.qa2{hh}")
            for c in range(2):
                t0 = hh * 1024 + c * 512
                for q in range(NI):
                    p.dma(self.WL[:, 0:1024], wqv[:, q, :], writes=["WLa"],
                          label=f"{tag}.dw{hh}.{c}.{q}")
                    def dmm(e, q=q, c=c, hh=hh):
                        last = None
                        for m in range(NT):
                            pr = (self.D if q < 16 else self.E)
                            last = e.matmul(
                                self.PS[m][:, 0:512],
                                self.WL[:, m * 128:(m + 1) * 128],
                                pr[:, (q % 16) * 1024 + c * 512:(q % 16) * 1024 + c * 512 + 512],
                                start=(q == 0), stop=(q == NI - 1))
                        return last
                    p.op("tensor", dmm,
                         reads=["WLa", psrck(q)], writes=self.PK[0:8],
                         label=f"{tag}.mmd{hh}.{c}.{q}")
                for m in range(NT):
                    p.op("vector", lambda e, m=m, c=c: e.scalar_tensor_tensor(
                        self.S1[:, 0:512], self.PS[m][:, 0:512], self.CCOL[:, 42:43],
                        self.SC[:, c * 512:c * 512 + 512], ALU.mult, ALU.mult),
                        reads=[self.PK[m], "CCOL", "SC.A"], writes=["S1"],
                        label=f"{tag}.ed{hh}.{c}.{m}")
                    p.op("vector", lambda e, m=m, t0=t0: e.tensor_tensor(
                        self.Av(m, t0, 512), self.Av(m, t0, 512),
                        self.S1[:, 0:512], ALU.add),
                        reads=["S1", f"A.{m}"], writes=[f"A.{m}"],
                        label=f"{tag}.acc{hh}.{c}.{m}")


def build_ffn_prog():
    nc = bass.Bass("TRN2", target_bir_lowering=False, debug=False)
    xT = nc.dram_tensor("xT", [H, T], F32, kind="ExternalInput").ap()
    wg = nc.dram_tensor("wg", [H, INNER], F32, kind="ExternalInput").ap()
    wu = nc.dram_tensor("wu", [H, INNER], F32, kind="ExternalInput").ap()
    wd = nc.dram_tensor("wd", [INNER, H], F32, kind="ExternalInput").ap()
    ident = nc.dram_tensor("ident", [128, 128], F32, kind="ExternalInput").ap()
    xout = nc.dram_tensor("xout", [H, T], F32, kind="ExternalOutput").ap()
    wdq = nc.dram_tensor("wdq", [INNER, H], BF16, kind="Internal").ap()
    with ExitStack() as stack:
        mk = MK(nc, stack)
        mk.init_consts(ident)
        mk.load_x(xT)
        mk.emit_ffn(wg, wu, wd, wdq, "f")
        outs = mk.store_x(xout)
        with nc.Block() as block:
            mk.p.emit(block, end_fence=outs)
    return nc


def _ident_np():
    return np.eye(128, dtype=np.float32)


def _colify(v):
    return np.ascontiguousarray(v.reshape(8, 128).T).astype(np.float32)



def _ck(i):
    a = (i * 2050) // 2048
    b = min((i * 2050 + 2049) // 2048, 7)
    return [f"C.{a}"] if a == b else [f"C.{a}", f"C.{b}"]


def _dk(i):
    return [f"D.{2 * i}", f"D.{2 * i + 1}"]


def _ek(i):
    return [f"E.{2 * i}", f"E.{2 * i + 1}"]


def _mk_emit_rwkv(self, wk, wv, wr, wo, xspill, tag):
    p = self.p
    def yv(i, off=1, n=2048):
        return self.C[:, i * 2050 + off: i * 2050 + off + n]
    for i in range(NT):
        p.op("vector", lambda e, i=i: e.memset(self.C[:, i * 2050:i * 2050 + 1], 0.0),
             writes=_ck(i), label=f"{tag}.z{i}")
    self.emit_ln(lambda i: self.Av(i), lambda i: f"A.{i}",
                 lambda i: yv(i), lambda i: _ck(i), f"{tag}.ln")
    for i in range(NT):
        p.dma(xspill[i * 128:(i + 1) * 128, :], self.Av(i),
              reads=[f"A.{i}"], writes=["XSP"], label=f"{tag}.sp{i}")
    for (nm, sl) in (("tmk", 48), ("tmv", 56), ("tmr", 64)):
        p.op("vector", lambda e, nm=nm, sl=sl: e.tensor_scalar(
            self.CCOL[:, sl:sl + 8], self.COLS[:, self.col_slots[nm]:self.col_slots[nm] + 8],
            -1.0, 1.0, ALU.mult, ALU.add),
            reads=[f"COLS.{nm}"], writes=["CCOL"], label=f"{tag}.om{nm}")
    p.op("scalar", lambda e: e.activation(
        self.CCOL[:, 72:80], self.COLS[:, self.col_slots["tdec"]:self.col_slots["tdec"] + 8],
        AF.Exp), reads=["COLS.tdec"], writes=["CCOL"], label=f"{tag}.td1")
    p.op("scalar", lambda e: e.activation(
        self.CCOL[:, 72:80], self.CCOL[:, 72:80], AF.Exp, scale=-1.0),
        reads=["CCOL"], writes=["CCOL"], label=f"{tag}.td2")
    p.op("scalar", lambda e: e.activation(
        self.CCOL[:, 80:88], self.COLS[:, self.col_slots["tfirst"]:self.col_slots["tfirst"] + 8],
        AF.Exp), reads=["COLS.tfirst"], writes=["CCOL"], label=f"{tag}.etf")

    def emit_mix(osl, dkey_pref, tag2):
        tm = {48: "tmk", 56: "tmv", 64: "tmr"}[osl]
        for i in range(NT):
            p.op("vector", lambda e, i=i, osl=osl: e.tensor_scalar(
                self.SC[:, 0:2048], yv(i, 0), self.CCOL[:, osl + i:osl + i + 1],
                None, ALU.mult),
                reads=_ck(i) + ["CCOL"], writes=["SC.A"], label=f"{tag2}.a{i}")
            p.op("vector", lambda e, i=i, tm=tm: e.scalar_tensor_tensor(
                self.D[:, i * 2048:(i + 1) * 2048], yv(i),
                self.col(tm, i), self.SC[:, 0:2048], ALU.mult, ALU.add),
                reads=_ck(i) + [f"COLS.{tm}", "SC.A"], writes=_dk(i),
                label=f"{tag2}.b{i}")

    def proj_weights(w_hbm, i, tag2):
        p.dma(self.WF[:, 0:1024].rearrange("p (j f) -> p j f", f=128),
              w_hbm.rearrange("(j p) o -> p j o", p=128)[:, :, i * 128:(i + 1) * 128],
              writes=["WF"], label=f"{tag2}.wl{i}")
        p.op("vector", lambda e: e.tensor_copy(self.WL[:, 0:1024], self.WF[:, 0:1024]),
             reads=["WF"], writes=["WLa"], label=f"{tag2}.wc{i}")

    def proj_group(e, i, c, src_pref):
        last = None
        for j in range(NT):
            last = e.matmul(self.PS[c % 2][:, 0:512],
                            self.WL[:, j * 128:(j + 1) * 128],
                            self.D[:, j * 2048 + c * 512: j * 2048 + c * 512 + 512],
                            start=(j == 0), stop=(j == NT - 1))
        return last

    emit_mix(56, "D", f"{tag}.vm")
    for i in range(NT):
        proj_weights(wv, i, f"{tag}.vw{i}")
        for c in range(4):
            p.op("tensor", lambda e, i=i, c=c: proj_group(e, i, c, "D"),
                 reads=["WLa"] + [k for j in range(NT) for k in _dk(j)],
                 writes=[self.PK[c % 2]], label=f"{tag}.vmm{i}.{c}")
            p.op("vector", lambda e, i=i, c=c: e.tensor_copy(
                self.E[:, i * 2048 + c * 512:i * 2048 + c * 512 + 512],
                self.PS[c % 2][:, 0:512]),
                reads=[self.PK[c % 2]], writes=_ek(i), label=f"{tag}.ve{i}.{c}")
    emit_mix(64, "D", f"{tag}.rm")
    for i in range(NT):
        proj_weights(wr, i, f"{tag}.rw{i}")
        for c in range(4):
            p.op("tensor", lambda e, i=i, c=c: proj_group(e, i, c, "D"),
                 reads=["WLa"] + [k for j in range(NT) for k in _dk(j)],
                 writes=[self.PK[c % 2]], label=f"{tag}.rmm{i}.{c}")
            p.op("scalar", lambda e, i=i, c=c: e.activation(
                self.Av(i, c * 512, 512), self.PS[c % 2][:, 0:512], AF.Sigmoid),
                reads=[self.PK[c % 2]], writes=[f"A.{i}"], label=f"{tag}.re{i}.{c}")
    emit_mix(48, "D", f"{tag}.km")
    for i in range(NT):
        proj_weights(wk, i, f"{tag}.kw{i}")
        for c in range(4):
            p.op("tensor", lambda e, i=i, c=c: proj_group(e, i, c, "D"),
                 reads=["WLa"] + [k for j in range(NT) for k in _dk(j)],
                 writes=[self.PK[c % 2]], label=f"{tag}.kmm{i}.{c}")
            p.op("scalar", lambda e, i=i, c=c: e.activation(
                self.SC[:, c * 512:c * 512 + 512], self.PS[c % 2][:, 0:512], AF.Exp),
                reads=[self.PK[c % 2]], writes=["SC.A"], label=f"{tag}.ek{i}.{c}")
        p.op("vector", lambda e, i=i: e.tensor_tensor(
            self.SC[:, 2048:4096], self.SC[:, 0:2048],
            self.E[:, i * 2048:(i + 1) * 2048], ALU.mult),
            reads=["SC.A"] + _ek(i), writes=["SC.B"], label=f"{tag}.ekv{i}")
        p.op("vector", lambda e: e.memset(self.S1[:, 0:2048], 1.0),
             writes=["S1"], label=f"{tag}.tdm{i}")
        p.op("vector", lambda e, i=i: e.tensor_scalar(
            self.S1[:, 0:2048], self.S1[:, 0:2048],
            self.CCOL[:, 72 + i:72 + i + 1], None, ALU.mult),
            reads=["S1", "CCOL"], writes=["S1"], label=f"{tag}.tdf{i}")
        p.op("vector", lambda e: e.tensor_tensor_scan(
            self.SC[:, 4096:6144], self.S1[:, 0:2048], self.SC[:, 2048:4096],
            0.0, ALU.mult, ALU.add),
            reads=["S1", "SC.B"], writes=["SC.C", "SC.D"], label=f"{tag}.nsc{i}")
        p.op("vector", lambda e, i=i: e.tensor_tensor_scan(
            self.C[:, i * 2050:i * 2050 + 2048], self.S1[:, 0:2048],
            self.SC[:, 0:2048], 0.0, ALU.mult, ALU.add),
            reads=["S1", "SC.A"] + _ck(i), writes=_ck(i), label=f"{tag}.dsc{i}")
        p.op("vector", lambda e, i=i: e.tensor_scalar(
            self.SC[:, 2048:4096], self.SC[:, 2048:4096],
            self.CCOL[:, 80 + i:80 + i + 1], None, ALU.mult),
            reads=["SC.B", "CCOL"], writes=["SC.B"], label=f"{tag}.e1v{i}")
        p.op("vector", lambda e, i=i: e.tensor_scalar(
            self.SC[:, 0:2048], self.SC[:, 0:2048],
            self.CCOL[:, 80 + i:80 + i + 1], None, ALU.mult),
            reads=["SC.A", "CCOL"], writes=["SC.A"], label=f"{tag}.e1{i}")
        p.op("vector", lambda e, i=i: e.tensor_copy(
            self.S1[:, 0:1], self.SC[:, 0:1]),
            reads=["SC.A"], writes=["S1"], label=f"{tag}.d0{i}")
        p.op("vector", lambda e, i=i: e.tensor_tensor(
            self.S1[:, 1:2048], self.C[:, i * 2050:i * 2050 + 2047],
            self.SC[:, 1:2048], ALU.add),
            reads=_ck(i) + ["SC.A", "S1"], writes=["S1"], label=f"{tag}.dsh{i}")
        p.op("vector", lambda e: e.reciprocal(self.S1[:, 0:2048], self.S1[:, 0:2048]),
             reads=["S1"], writes=["S1"], label=f"{tag}.rec{i}")
        p.op("vector", lambda e: e.tensor_tensor(
            self.SC[:, 2049:4096], self.SC[:, 4096:4096 + 2047],
            self.SC[:, 2049:4096], ALU.add),
            reads=["SC.C", "SC.D", "SC.B"], writes=["SC.B"], label=f"{tag}.nsh{i}")
        p.op("vector", lambda e: e.tensor_tensor(
            self.S1[:, 0:2048], self.SC[:, 2048:4096], self.S1[:, 0:2048], ALU.mult),
            reads=["SC.B", "S1"], writes=["S1"], label=f"{tag}.wkv{i}")
        p.op("vector", lambda e, i=i: e.tensor_tensor(
            self.C[:, i * 2050:i * 2050 + 2048], self.S1[:, 0:2048],
            self.Av(i), ALU.mult),
            reads=["S1", f"A.{i}"] + _ck(i), writes=_ck(i), label=f"{tag}.rw{i}")
    for m in range(NT):
        p.dma(self.Av(m), xspill[m * 128:(m + 1) * 128, :],
              reads=[f"A.{m}", "XSP"], writes=[f"A.{m}"], label=f"{tag}.rl{m}")
    for m in range(NT):
        proj_weights(wo, m, f"{tag}.ow{m}")
        for c in range(4):
            def omm(e, m=m, c=c):
                last = None
                for j in range(NT):
                    last = e.matmul(self.PS[2 + c % 2][:, 0:512],
                                    self.WL[:, j * 128:(j + 1) * 128],
                                    self.C[:, j * 2050 + c * 512: j * 2050 + c * 512 + 512],
                                    start=(j == 0), stop=(j == NT - 1))
                return last
            p.op("tensor", omm,
                 reads=["WLa"] + [f"C.{j}" for j in range(NT)],
                 writes=[self.PK[2 + c % 2]], label=f"{tag}.omm{m}.{c}")
            p.op("vector", lambda e, m=m, c=c: e.scalar_tensor_tensor(
                self.Av(m, c * 512, 512), self.PS[2 + c % 2][:, 0:512],
                self.col("bo", m), self.Av(m, c * 512, 512), ALU.add, ALU.add),
                reads=[self.PK[2 + c % 2], "COLS.bo", f"A.{m}"], writes=[f"A.{m}"],
                label=f"{tag}.oe{m}.{c}")


MK.emit_rwkv = _mk_emit_rwkv


def build_rwkv_prog():
    nc = bass.Bass("TRN2", target_bir_lowering=False, debug=False)
    xT = nc.dram_tensor("xT", [H, T], F32, kind="ExternalInput").ap()
    wk = nc.dram_tensor("wk", [H, H], F32, kind="ExternalInput").ap()
    wv = nc.dram_tensor("wv", [H, H], F32, kind="ExternalInput").ap()
    wr = nc.dram_tensor("wr", [H, H], F32, kind="ExternalInput").ap()
    wo = nc.dram_tensor("wo", [H, H], F32, kind="ExternalInput").ap()
    ident = nc.dram_tensor("ident", [128, 128], F32, kind="ExternalInput").ap()
    cols = {}
    for nm in ("tmk", "tmv", "tmr", "tdec", "tfirst", "bo"):
        cols[nm] = nc.dram_tensor(f"c_{nm}", [128, 8], F32, kind="ExternalInput").ap()
    xout = nc.dram_tensor("xout", [H, T], F32, kind="ExternalOutput").ap()
    xspill = nc.dram_tensor("xspill", [H, T], F32, kind="Internal").ap()
    with ExitStack() as stack:
        mk = MK(nc, stack)
        mk.init_consts(ident)
        for nm in cols:
            mk.load_cols(cols[nm], nm)
        mk.load_x(xT)
        mk.emit_rwkv(wk, wv, wr, wo, xspill, "r")
        outs = mk.store_x(xout)
        with nc.Block() as block:
            mk.p.emit(block, end_fence=outs)
    return nc


def _mk_emit_attn(self, Wq, Wk, Wv, Wo, Wrq, Wrk, Wrv, Wg, pastT, brow, xspill, tag):
    p = self.p
    ALLC = [f"C.{i}" for i in range(NT)]

    def wload(w_hbm, blk, woff, wkey, tag2):
        p.dma(self.WF[:, 0:1024].rearrange("p (j f) -> p j f", f=128),
              w_hbm.rearrange("(j p) o -> p j o", p=128)[:, :, blk * 128:(blk + 1) * 128],
              writes=["WF"], label=f"{tag2}.wl")
        p.op("vector", lambda e, woff=woff: e.tensor_copy(
            self.WL[:, woff:woff + 1024], self.WF[:, 0:1024]),
            reads=["WF"], writes=[wkey], label=f"{tag2}.wc")

    def projT(dst_off, dst_keys, rhs_keys, bias_name, blk, tag2, act=None,
              woff=0, wkey="WLa", rhs_tiles=None, dst_buf=None):
        dst_buf = self.D if dst_buf is None else dst_buf
        if rhs_tiles is None:
            rhs_tiles = [self.Cv(j) for j in range(NT)]
        for c in range(4):
            def mm(e, c=c, rhs_tiles=rhs_tiles, woff=woff):
                last = None
                for j in range(NT):
                    last = e.matmul(self.PS[c % 2][:, 0:512],
                                    self.WL[:, woff + j * 128:woff + (j + 1) * 128],
                                    rhs_tiles[j][:, c * 512:(c + 1) * 512],
                                    start=(j == 0), stop=(j == NT - 1))
                return last
            p.op("tensor", mm, reads=[wkey] + rhs_keys,
                 writes=[self.PK[c % 2]], label=f"{tag2}.mm{c}")
            if act is None:
                p.op("vector", lambda e, c=c, dst_off=dst_off, blk=blk,
                     bias_name=bias_name, dst_buf=dst_buf: e.tensor_scalar(
                    dst_buf[:, dst_off + c * 512:dst_off + c * 512 + 512],
                    self.PS[c % 2][:, 0:512], self.col(bias_name, blk), None,
                    ALU.add),
                    reads=[self.PK[c % 2], f"COLS.{bias_name}"], writes=dst_keys,
                    label=f"{tag2}.ev{c}")
            else:
                p.op("scalar", lambda e, c=c, dst_off=dst_off, blk=blk,
                     bias_name=bias_name, dst_buf=dst_buf, act=act: e.activation(
                    dst_buf[:, dst_off + c * 512:dst_off + c * 512 + 512],
                    self.PS[c % 2][:, 0:512], act, bias=self.col(bias_name, blk)),
                    reads=[self.PK[c % 2], f"COLS.{bias_name}"], writes=dst_keys,
                    label=f"{tag2}.ev{c}")

    def head_norm_cell(src_buf, src_off, hl, cell, skey, tag2):
        sq = self.SC[:, R2:R2 + 2048]
        p.op("scalar", lambda e: e.activation(
            sq[0:64, :], src_buf[hl:hl + 64, src_off:src_off + 2048], AF.Square),
            reads=skey, writes=["SC.C"], label=f"{tag2}.sq")
        for c in range(4):
            p.op("tensor", lambda e, c=c: e.matmul(
                self.PS[5][0:1, 0:512], self.ONESC[0:64, :],
                sq[0:64, c * 512:(c + 1) * 512], start=True, stop=True),
                reads=["SC.C", "ONESC"], writes=[self.PK[5]], label=f"{tag2}.nm{c}")
            p.op("vector", lambda e, c=c, cell=cell: e.reduce_max(
                self.CCOL[0:1, 40 + c:41 + c], self.PS[5][0:1, 0:512], axis=AX.X),
                reads=[self.PK[5]], writes=["CCOL"], label=f"{tag2}.nr{c}")
        p.op("vector", lambda e: e.tensor_tensor(
            self.CCOL[0:1, 40:41], self.CCOL[0:1, 40:41], self.CCOL[0:1, 41:42], ALU.max),
            reads=["CCOL"], writes=["CCOL"], label=f"{tag2}.nx1")
        p.op("vector", lambda e: e.tensor_tensor(
            self.CCOL[0:1, 42:43], self.CCOL[0:1, 42:43], self.CCOL[0:1, 43:44], ALU.max),
            reads=["CCOL"], writes=["CCOL"], label=f"{tag2}.nx2")
        p.op("vector", lambda e, cell=cell: e.tensor_tensor(
            self.CCOL[0:1, cell:cell + 1], self.CCOL[0:1, 40:41],
            self.CCOL[0:1, 42:43], ALU.max),
            reads=["CCOL"], writes=["CCOL"], label=f"{tag2}.nx3")

    def bias_col_from_cells(cell_q, cell_k, tag2):
        p.op("vector", lambda e: e.tensor_tensor(
            self.CCOL[0:1, 44:45], self.CCOL[0:1, cell_q:cell_q + 1],
            self.CCOL[0:1, cell_k:cell_k + 1], ALU.mult),
            reads=["CCOL"], writes=["CCOL"], label=f"{tag2}.bm")
        p.op("scalar", lambda e: e.activation(
            self.CCOL[0:1, 44:45], self.CCOL[0:1, 44:45], AF.Sqrt),
            reads=["CCOL"], writes=["CCOL"], label=f"{tag2}.bs")
        p.op("vector", lambda e: e.tensor_scalar_mul(
            self.CCOL[0:1, 44:45], self.CCOL[0:1, 44:45], -0.125),
            reads=["CCOL"], writes=["CCOL"], label=f"{tag2}.bn")
        p.op("tensor", lambda e: e.matmul(
            self.PS[5][:, 0:1], self.ONESRF[:], self.CCOL[0:1, 44:45],
            start=True, stop=True),
            reads=["CCOL", "ONESRF"], writes=[self.PK[5]], label=f"{tag2}.br")
        p.op("vector", lambda e: e.tensor_copy(self.CCOL[:, 46:47], self.PS[5][:, 0:1]),
             reads=[self.PK[5]], writes=["CCOL"], label=f"{tag2}.be")

    def flash(q_buf, q_off, q_keys, k_buf, k_off, k_keys, v_buf, v_off, v_keys,
              hl, dst_region, dst_keys, scale05, accumulate, tag2):
        for sc in range(4):
            for tau in range(16):
                pb = (sc * 16 + tau) % 2
                p.op("tensor", lambda e, sc=sc, tau=tau, pb=pb: e.matmul(
                    self.PS[pb][:, 0:512],
                    k_buf[hl:hl + 64, k_off + tau * 128:k_off + (tau + 1) * 128],
                    q_buf[hl:hl + 64, q_off + sc * 512:q_off + sc * 512 + 512],
                    start=True, stop=True),
                    reads=k_keys + q_keys, writes=[self.PK[pb]],
                    label=f"{tag2}.sc{sc}.{tau}")
                et_off, et_key = (R2, "SC.C") if tau % 2 == 0 else (R3, "SC.D")
                p.op("scalar", lambda e, pb=pb, et_off=et_off: e.activation(
                    self.SC[:, et_off:et_off + 512], self.PS[pb][:, 0:512],
                    AF.Exp, bias=self.CCOL[:, 46:47], scale=0.125),
                    reads=[self.PK[pb], "CCOL"], writes=[et_key],
                    label=f"{tag2}.ex{sc}.{tau}")
                def ctx(e, tau=tau, et_off=et_off):
                    e.matmul(self.PS[4][0:64, 0:512],
                             v_buf[:, v_off + tau * 128 + hl:v_off + tau * 128 + hl + 64],
                             self.SC[:, et_off:et_off + 512],
                             start=(tau == 0), stop=(tau == 15))
                    return e.matmul(self.PS[5][64:65, 0:512], self.ONESC[:],
                                    self.SC[:, et_off:et_off + 512],
                                    start=(tau == 0), stop=(tau == 15))
                p.op("tensor", ctx, reads=v_keys + [et_key, "ONESC"],
                     writes=[self.PK[4], self.PK[5]], label=f"{tag2}.cx{sc}.{tau}")
            p.op("vector", lambda e: e.reciprocal(self.S1[0:1, 0:512],
                                                  self.PS[5][64:65, 0:512]),
                 reads=[self.PK[5]], writes=["S1"], label=f"{tag2}.rc{sc}")
            if scale05:
                p.op("vector", lambda e: e.tensor_scalar_mul(
                    self.S1[0:1, 0:512], self.S1[0:1, 0:512], 0.5),
                    reads=["S1"], writes=["S1"], label=f"{tag2}.h{sc}")
            p.op("tensor", lambda e: e.matmul(
                self.PS[5][hl:hl + 64, 0:512], self.ONESRF[0:1, 0:64],
                self.S1[0:1, 0:512], start=True, stop=True),
                reads=["S1", "ONESRF"], writes=[self.PK[5]], label=f"{tag2}.rp{sc}")
            p.op("vector", lambda e: e.tensor_copy(
                self.SC[hl:hl + 64, 5120:5120 + 512], self.PS[5][hl:hl + 64, 0:512]),
                reads=[self.PK[5]], writes=["SC.C"], label=f"{tag2}.rpe{sc}")
            if not accumulate:
                p.op("vector", lambda e, sc=sc: e.tensor_tensor(
                    self.A[hl:hl + 64, dst_region + sc * 512:dst_region + sc * 512 + 512],
                    self.PS[4][0:64, 0:512], self.SC[hl:hl + 64, 5120:5120 + 512], ALU.mult),
                    reads=[self.PK[4], "SC.C"], writes=dst_keys,
                    label=f"{tag2}.no{sc}")
            else:
                p.op("vector", lambda e, sc=sc: e.tensor_tensor(
                    self.SC[hl:hl + 64, R2 + 512:R2 + 1024],
                    self.PS[4][0:64, 0:512], self.SC[hl:hl + 64, 5120:5120 + 512], ALU.mult),
                    reads=[self.PK[4], "SC.C"], writes=["SC.C"],
                    label=f"{tag2}.na{sc}")
                p.op("vector", lambda e, sc=sc: e.tensor_tensor(
                    self.A[hl:hl + 64, dst_region + sc * 512:dst_region + sc * 512 + 512],
                    self.A[hl:hl + 64, dst_region + sc * 512:dst_region + sc * 512 + 512],
                    self.SC[hl:hl + 64, R2 + 512:R2 + 1024], ALU.add),
                    reads=["SC.C"] + dst_keys, writes=dst_keys,
                    label=f"{tag2}.ac{sc}")

    self.emit_ln(lambda i: self.Av(i), lambda i: f"A.{i}",
                 lambda i: self.Cv(i), lambda i: [f"C.{i}"], f"{tag}.ln")
    for i in range(NT):
        p.dma(xspill[i * 128:(i + 1) * 128, :], self.Av(i),
              reads=[f"A.{i}"], writes=["XSP"], label=f"{tag}.sp{i}")
    p.dma(self.ROWS[0:1, 0:2048], brow, writes=["ROWS"], label=f"{tag}.brow")

    for g in range(NT):
        gk = [f"G{g}"]
        wload(Wq, g, 0, "WLa", f"{tag}.q{g}")
        projT(0, ["D.0", "D.1"], ALLC, "bq", g, f"{tag}.qp{g}")
        wload(Wk, g, 0, "WLa", f"{tag}.k{g}")
        projT(2048, ["D.2", "D.3"], ALLC, "bk", g, f"{tag}.kp{g}")
        wload(Wv, g, 0, "WLa", f"{tag}.v{g}")
        p.op("vector", lambda e, g=g: e.tensor_copy(
            self.SC[0:1, R2:R2 + 128], self.ROWS[0:1, g * 128:(g + 1) * 128]),
            reads=["ROWS"], writes=["SC.C"], label=f"{tag}.bv{g}")
        for tau in range(16):
            def vmm(e, tau=tau):
                for j in range(NT):
                    e.matmul(self.PS[tau % 2][:, 0:128],
                             self.Cv(j)[:, tau * 128:(tau + 1) * 128],
                             self.WL[:, j * 128:(j + 1) * 128],
                             start=(j == 0), stop=False)
                return e.matmul(self.PS[tau % 2][:, 0:128], self.ONESR[:],
                                self.SC[0:1, R2:R2 + 128], start=False, stop=True)
            p.op("tensor", vmm, reads=["WLa", "SC.C", "ONESR"] + ALLC,
                 writes=[self.PK[tau % 2]], label=f"{tag}.vm{g}.{tau}")
            p.op("vector", lambda e, tau=tau: e.tensor_copy(
                self.D[:, 4096 + tau * 128:4096 + (tau + 1) * 128],
                self.PS[tau % 2][:, 0:128]),
                reads=[self.PK[tau % 2]], writes=["D.4", "D.5"],
                label=f"{tag}.ve{g}.{tau}")
        for hh in range(2):
            hl = hh * 64
            head_norm_cell(self.D, 0, hl, 36, ["D.0", "D.1"], f"{tag}.nq{g}.{hh}")
            head_norm_cell(self.D, 2048, hl, 37, ["D.2", "D.3"], f"{tag}.nk{g}.{hh}")
            bias_col_from_cells(36, 37, f"{tag}.bc{g}.{hh}")
            flash(self.D, 0, ["D.0", "D.1"], self.D, 2048, ["D.2", "D.3"],
                  self.D, 4096, ["D.4", "D.5"], hl,
                  g * 2048, [f"A.{g}"], False, False, f"{tag}.fl{g}.{hh}")
    for i in range(NT):
        p.op("vector", lambda e, i=i: e.tensor_copy(self.Cv(i), self.Av(i)),
             reads=[f"A.{i}"], writes=[f"C.{i}"], label=f"{tag}.cc{i}")
    for m in range(NT):
        wload(Wo, m, 0, "WLa", f"{tag}.o{m}")
        projT(m * 2048, [f"D.{2 * m}", f"D.{2 * m + 1}"], ALLC, "bo2", m, f"{tag}.op{m}")
    ALLD = [f"D.{r}" for r in range(16)]
    for m in range(NT):
        wload(Wrq, m, 0, "WLa", f"{tag}.rq{m}")
        projT(m * 2048, [f"E.{2 * m}", f"E.{2 * m + 1}"], ALLD, "brq", m,
              f"{tag}.rqp{m}", rhs_tiles=[self.D[:, j * 2048:(j + 1) * 2048] for j in range(NT)],
              dst_buf=self.E)
    ALLE = [f"E.{r}" for r in range(16)]
    for h in range(NH):
        g, hl = h // 2, (h % 2) * 64
        head_norm_cell(self.E, g * 2048, hl, 16 + h, ALLE, f"{tag}.rqn{h}")
    for pp in range(2):
        for i in range(NT):
            p.dma(self.S1[:, 0:2048], pastT[pp, i * 128:(i + 1) * 128, :],
                  reads=["S1"], writes=["S1"], label=f"{tag}.pl{pp}.{i}")
            p.op("vector", lambda e, i=i: e.tensor_copy(self.Cv(i), self.S1[:, 0:2048]),
                 reads=["S1"], writes=[f"C.{i}"], label=f"{tag}.pc{pp}.{i}")
        for g in range(NT):
            wload(Wrk, g, 0, "WLa", f"{tag}.rk{pp}.{g}")
            for c in range(4):
                def rkmm(e, c=c):
                    last = None
                    for j in range(NT):
                        last = e.matmul(self.PS[c % 2][:, 0:512],
                                        self.WL[:, j * 128:(j + 1) * 128],
                                        self.Cv(j)[:, c * 512:(c + 1) * 512],
                                        start=(j == 0), stop=(j == NT - 1))
                    return last
                p.op("tensor", rkmm, reads=["WLa"] + ALLC,
                     writes=[self.PK[c % 2]], label=f"{tag}.rkm{pp}.{g}.{c}")
                p.op("vector", lambda e, c=c, g=g: e.tensor_scalar(
                    self.SC[:, c * 512:c * 512 + 512], self.PS[c % 2][:, 0:512],
                    self.col("brk", g), None, ALU.add),
                    reads=[self.PK[c % 2], "COLS.brk"], writes=["SC.A"],
                    label=f"{tag}.rke{pp}.{g}.{c}")
            wload(Wrv, g, 0, "WLa", f"{tag}.rv{pp}.{g}")
            p.op("vector", lambda e, g=g: e.tensor_copy(
                self.SC[0:1, R2:R2 + 128], self.ROWS[0:1, 1024 + g * 128:1024 + (g + 1) * 128]),
                reads=["ROWS"], writes=["SC.C"], label=f"{tag}.brv{pp}.{g}")
            for tau in range(16):
                def rvmm(e, tau=tau):
                    for j in range(NT):
                        e.matmul(self.PS[tau % 2][:, 0:128],
                                 self.Cv(j)[:, tau * 128:(tau + 1) * 128],
                                 self.WL[:, j * 128:(j + 1) * 128],
                                 start=(j == 0), stop=False)
                    return e.matmul(self.PS[tau % 2][:, 0:128], self.ONESR[:],
                                    self.SC[0:1, R2:R2 + 128], start=False, stop=True)
                p.op("tensor", rvmm, reads=["WLa", "SC.C", "ONESR"] + ALLC,
                     writes=[self.PK[tau % 2]], label=f"{tag}.rvm{pp}.{g}.{tau}")
                p.op("vector", lambda e, tau=tau: e.tensor_copy(
                    self.SC[:, R1 + tau * 128:R1 + (tau + 1) * 128],
                    self.PS[tau % 2][:, 0:128]),
                    reads=[self.PK[tau % 2]], writes=["SC.B"],
                    label=f"{tag}.rve{pp}.{g}.{tau}")
            for hh in range(2):
                hl = hh * 64
                h = g * 2 + hh
                head_norm_cell(self.SC, 0, hl, 37, ["SC.A"], f"{tag}.rkn{pp}.{g}.{hh}")
                bias_col_from_cells(16 + h, 37, f"{tag}.rbc{pp}.{g}.{hh}")
                flash(self.E, g * 2048, ALLE, self.SC, 0, ["SC.A"],
                      self.SC, R1, ["SC.B"], hl,
                      g * 2048, [f"A.{g}"], True, pp == 1, f"{tag}.rf{pp}.{g}.{hh}")
    for i in range(NT):
        p.op("vector", lambda e, i=i: e.tensor_copy(self.Cv(i), self.Av(i)),
             reads=[f"A.{i}"], writes=[f"C.{i}"], label=f"{tag}.rc{i}")
    for m in range(NT):
        p.dma(self.WF[:, 0:1024].rearrange("p (j f) -> p j f", f=128),
              Wg.rearrange("(j p) o -> p j o", p=128)[:, 0:8, m * 128:(m + 1) * 128],
              writes=["WF"], label=f"{tag}.g1{m}")
        p.op("vector", lambda e: e.tensor_copy(self.WL[:, 0:1024], self.WF[:, 0:1024]),
             reads=["WF"], writes=["WLa"], label=f"{tag}.g1c{m}")
        p.dma(self.WF[:, 0:1024].rearrange("p (j f) -> p j f", f=128),
              Wg.rearrange("(j p) o -> p j o", p=128)[:, 8:16, m * 128:(m + 1) * 128],
              writes=["WF"], label=f"{tag}.g2{m}")
        p.op("vector", lambda e: e.tensor_copy(self.SC[:, 2048:3072], self.WF[:, 0:1024]),
             reads=["WF"], writes=["SC.B"], label=f"{tag}.g2c{m}")
        for c in range(4):
            def gmm(e, c=c):
                last = None
                for j in range(16):
                    rhs = (self.D[:, (j) * 2048 + c * 512:j * 2048 + c * 512 + 512]
                           if j < 8 else
                           self.Cv(j - 8)[:, c * 512:(c + 1) * 512])
                    lhsT = (self.WL[:, j * 128:(j + 1) * 128] if j < 8
                            else self.SC[:, 2048 + (j - 8) * 128:2048 + (j - 7) * 128])
                    last = e.matmul(self.PS[2 + c % 2][:, 0:512],
                                    lhsT, rhs, start=(j == 0), stop=(j == 15))
                return last
            p.op("tensor", gmm, reads=["WLa", "SC.B"] + ALLD + ALLC,
                 writes=[self.PK[2 + c % 2]], label=f"{tag}.gm{m}.{c}")
            p.op("scalar", lambda e, c=c, m=m: e.activation(
                self.E[:, m * 2048 + c * 512:m * 2048 + c * 512 + 512],
                self.PS[2 + c % 2][:, 0:512], AF.Sigmoid, bias=self.col("bg", m)),
                reads=[self.PK[2 + c % 2], "COLS.bg"],
                writes=[f"E.{2 * m}", f"E.{2 * m + 1}"], label=f"{tag}.ge{m}.{c}")
    for m in range(NT):
        p.dma(self.Av(m), xspill[m * 128:(m + 1) * 128, :],
              reads=[f"A.{m}", "XSP"], writes=[f"A.{m}"], label=f"{tag}.rl{m}")
    for m in range(NT):
        for c in range(4):
            sl = lambda buf: buf[:, m * 2048 + c * 512:m * 2048 + c * 512 + 512]
            p.op("vector", lambda e, m=m, c=c: e.tensor_tensor(
                self.SC[:, 0:512], self.Cv(m)[:, c * 512:(c + 1) * 512],
                self.D[:, m * 2048 + c * 512:m * 2048 + c * 512 + 512], ALU.subtract),
                reads=[f"C.{m}", f"D.{2 * m}", f"D.{2 * m + 1}"], writes=["SC.A"],
                label=f"{tag}.cb1{m}.{c}")
            p.op("vector", lambda e, m=m, c=c: e.tensor_tensor(
                self.SC[:, 512:1024],
                self.E[:, m * 2048 + c * 512:m * 2048 + c * 512 + 512],
                self.SC[:, 0:512], ALU.mult),
                reads=[f"E.{2 * m}", f"E.{2 * m + 1}", "SC.A"], writes=["SC.A"],
                label=f"{tag}.cb2{m}.{c}")
            p.op("vector", lambda e, m=m, c=c: e.tensor_tensor(
                self.SC[:, 512:1024],
                self.D[:, m * 2048 + c * 512:m * 2048 + c * 512 + 512],
                self.SC[:, 512:1024], ALU.add),
                reads=[f"D.{2 * m}", f"D.{2 * m + 1}", "SC.A"], writes=["SC.A"],
                label=f"{tag}.cb3{m}.{c}")
            p.op("vector", lambda e, m=m, c=c: e.tensor_tensor(
                self.Av(m, c * 512, 512), self.Av(m, c * 512, 512),
                self.SC[:, 512:1024], ALU.add),
                reads=["SC.A", f"A.{m}"], writes=[f"A.{m}"],
                label=f"{tag}.cb4{m}.{c}")


MK.emit_attn = _mk_emit_attn


def build_attn_prog():
    nc = bass.Bass("TRN2", target_bir_lowering=False, debug=False)
    xT = nc.dram_tensor("xT", [H, T], F32, kind="ExternalInput").ap()
    ws = {}
    for nm in ("Wq", "Wk", "Wv", "Wo", "Wrq", "Wrk", "Wrv"):
        ws[nm] = nc.dram_tensor(nm, [H, H], F32, kind="ExternalInput").ap()
    Wg = nc.dram_tensor("Wg", [2 * H, H], F32, kind="ExternalInput").ap()
    pastT = nc.dram_tensor("pastT", [2, H, T], F32, kind="ExternalInput").ap()
    brow = nc.dram_tensor("brow", [1, 2048], F32, kind="ExternalInput").ap()
    ident = nc.dram_tensor("ident", [128, 128], F32, kind="ExternalInput").ap()
    cols = {}
    for nm in ("bq", "bk", "bo2", "brq", "brk", "bg"):
        cols[nm] = nc.dram_tensor(f"c_{nm}", [128, 8], F32, kind="ExternalInput").ap()
    xout = nc.dram_tensor("xout", [H, T], F32, kind="ExternalOutput").ap()
    xspill = nc.dram_tensor("xspill", [H, T], F32, kind="Internal").ap()
    with ExitStack() as stack:
        mk = MK(nc, stack)
        mk.init_consts(ident)
        for nm in cols:
            mk.load_cols(cols[nm], nm)
        mk.load_x(xT)
        mk.emit_attn(ws["Wq"], ws["Wk"], ws["Wv"], ws["Wo"], ws["Wrq"], ws["Wrk"],
                     ws["Wrv"], Wg, pastT, brow, xspill, "a")
        outs = mk.store_x(xout)
        with nc.Block() as block:
            mk.p.emit(block, end_fence=outs)
    return nc


_PROG_CACHE = {}


def _prog(name):
    if name not in _PROG_CACHE:
        _PROG_CACHE[name] = {"ffn": build_ffn_prog, "rwkv": build_rwkv_prog,
                             "attn": build_attn_prog}[name]()
    return _PROG_CACHE[name]


def kernel(**inputs):
    f = np.float32
    x = np.asarray(inputs["L0_attn_x"], dtype=f)
    ident = _ident_np()
    tr = lambda a: np.ascontiguousarray(np.asarray(a, dtype=f).T)

    ffnw = {}
    for L in ("L0", "L1"):
        ffnw[L] = dict(wg=tr(inputs[f"{L}_ffn_w_gate"]),
                       wu=tr(inputs[f"{L}_ffn_w_up"]),
                       wd=tr(inputs[f"{L}_ffn_w_down"]))
    rwkvw = dict(wk=tr(inputs["L0_attn_Wk"]), wv=tr(inputs["L0_attn_Wv"]),
                 wr=tr(inputs["L0_attn_Wr"]), wo=tr(inputs["L0_attn_Wo"]),
                 c_tmk=_colify(np.asarray(inputs["L0_attn_time_mix_key"], f)[0, 0]),
                 c_tmv=_colify(np.asarray(inputs["L0_attn_time_mix_value"], f)[0, 0]),
                 c_tmr=_colify(np.asarray(inputs["L0_attn_time_mix_receptance"], f)[0, 0]),
                 c_tdec=_colify(np.asarray(inputs["L0_attn_time_decay"], f)),
                 c_tfirst=_colify(np.asarray(inputs["L0_attn_time_first"], f)),
                 c_bo=_colify(np.asarray(inputs["L0_attn_bo"], f)))
    attnw = dict(
        Wq=np.asarray(inputs["L1_attn_Wq"], f), Wk=np.asarray(inputs["L1_attn_Wk"], f),
        Wv=np.asarray(inputs["L1_attn_Wv"], f), Wo=np.asarray(inputs["L1_attn_Wo"], f),
        Wrq=np.asarray(inputs["L1_attn_Wrq"], f), Wrk=np.asarray(inputs["L1_attn_Wrk"], f),
        Wrv=np.asarray(inputs["L1_attn_Wrv"], f), Wg=np.asarray(inputs["L1_attn_Wg"], f),
        pastT=np.ascontiguousarray(
            np.asarray(inputs["L1_attn_past_states"], f)[:, 0].transpose(0, 2, 1)),
        brow=np.concatenate([np.asarray(inputs["L1_attn_bv"], f),
                             np.asarray(inputs["L1_attn_brv"], f)])[None, :],
        c_bq=_colify(np.asarray(inputs["L1_attn_bq"], f)),
        c_bk=_colify(np.asarray(inputs["L1_attn_bk"], f)),
        c_bo=_colify(np.asarray(inputs["L1_attn_bo"], f)),
        c_brq=_colify(np.asarray(inputs["L1_attn_brq"], f)),
        c_brk=_colify(np.asarray(inputs["L1_attn_brk"], f)),
        c_bg=_colify(np.asarray(inputs["L1_attn_bg"], f)))

    xT = [tr(x[b]) for b in range(N_CORES)]

    def launch(prog_name, extra):
        nc = _prog(prog_name)
        maps = [dict(xT=xT[b], ident=ident, **extra) for b in range(N_CORES)]
        res = run_bass_kernel_spmd(nc, maps, core_ids=list(range(N_CORES)))
        for b in range(N_CORES):
            xT[b] = np.ascontiguousarray(np.asarray(res.results[b]["xout"]))

    launch("ffn", ffnw["L0"])
    launch("rwkv", rwkvw)
    launch("ffn", ffnw["L0"])
    launch("ffn", ffnw["L1"])
    launch("attn", attnw)
    launch("ffn", ffnw["L1"])

    return np.stack([xT[b].T for b in range(N_CORES)]).astype(np.float32)


kernel_multilaunch = kernel


def kernel(**inputs):
    return kernel_fused(**inputs)


def build_fused_prog():
    nc = bass.Bass("TRN2", target_bir_lowering=False, debug=False)
    xT = nc.dram_tensor("xT", [H, T], F32, kind="ExternalInput").ap()
    ident = nc.dram_tensor("ident", [128, 128], F32, kind="ExternalInput").ap()
    ffn_w = {}
    for L in ("l0", "l1"):
        ffn_w[L] = dict(
            wg=nc.dram_tensor(f"{L}_wg", [H, INNER], F32, kind="ExternalInput").ap(),
            wu=nc.dram_tensor(f"{L}_wu", [H, INNER], F32, kind="ExternalInput").ap(),
            wd=nc.dram_tensor(f"{L}_wd", [INNER, H], F32, kind="ExternalInput").ap(),
            wdq=nc.dram_tensor(f"{L}_wdq", [INNER, H], BF16, kind="Internal").ap())
    rw = {nm: nc.dram_tensor(f"r_{nm}", [H, H], F32, kind="ExternalInput").ap()
          for nm in ("wk", "wv", "wr", "wo")}
    aw = {nm: nc.dram_tensor(f"a_{nm}", [H, H], F32, kind="ExternalInput").ap()
          for nm in ("Wq", "Wk", "Wv", "Wo", "Wrq", "Wrk", "Wrv")}
    Wg = nc.dram_tensor("a_Wg", [2 * H, H], F32, kind="ExternalInput").ap()
    pastT = nc.dram_tensor("pastT", [2, H, T], F32, kind="ExternalInput").ap()
    brow = nc.dram_tensor("brow", [1, 2048], F32, kind="ExternalInput").ap()
    cols = {}
    for nm in ("tmk", "tmv", "tmr", "tdec", "tfirst", "bo",
               "bq", "bk", "bo2", "brq", "brk", "bg"):
        cols[nm] = nc.dram_tensor(f"c_{nm}", [128, 8], F32, kind="ExternalInput").ap()
    xout = nc.dram_tensor("xout", [H, T], F32, kind="ExternalOutput").ap()
    xspill = nc.dram_tensor("xspill", [H, T], F32, kind="Internal").ap()
    with ExitStack() as stack:
        mk = MK(nc, stack)
        mk.init_consts(ident)
        for nm in cols:
            mk.load_cols(cols[nm], nm)
        mk.load_x(xT)
        mk.emit_ffn(ffn_w["l0"]["wg"], ffn_w["l0"]["wu"], ffn_w["l0"]["wd"],
                    ffn_w["l0"]["wdq"], "f0a")
        mk.emit_rwkv(rw["wk"], rw["wv"], rw["wr"], rw["wo"], xspill, "rk")
        mk.emit_ffn(ffn_w["l0"]["wg"], ffn_w["l0"]["wu"], ffn_w["l0"]["wd"],
                    ffn_w["l0"]["wdq"], "f0b")
        mk.emit_ffn(ffn_w["l1"]["wg"], ffn_w["l1"]["wu"], ffn_w["l1"]["wd"],
                    ffn_w["l1"]["wdq"], "f1a")
        mk.emit_attn(aw["Wq"], aw["Wk"], aw["Wv"], aw["Wo"], aw["Wrq"], aw["Wrk"],
                     aw["Wrv"], Wg, pastT, brow, xspill, "at")
        mk.emit_ffn(ffn_w["l1"]["wg"], ffn_w["l1"]["wu"], ffn_w["l1"]["wd"],
                    ffn_w["l1"]["wdq"], "f1b")
        outs = mk.store_x(xout)
        with nc.Block() as block:
            mk.p.emit(block, end_fence=outs)
    return nc


def kernel_fused(**inputs):
    f = np.float32
    x = np.asarray(inputs["L0_attn_x"], dtype=f)
    tr = lambda a: np.ascontiguousarray(np.asarray(a, dtype=f).T)
    base = dict(
        ident=_ident_np(),
        l0_wg=tr(inputs["L0_ffn_w_gate"]), l0_wu=tr(inputs["L0_ffn_w_up"]),
        l0_wd=tr(inputs["L0_ffn_w_down"]),
        l1_wg=tr(inputs["L1_ffn_w_gate"]), l1_wu=tr(inputs["L1_ffn_w_up"]),
        l1_wd=tr(inputs["L1_ffn_w_down"]),
        r_wk=tr(inputs["L0_attn_Wk"]), r_wv=tr(inputs["L0_attn_Wv"]),
        r_wr=tr(inputs["L0_attn_Wr"]), r_wo=tr(inputs["L0_attn_Wo"]),
        a_Wq=np.asarray(inputs["L1_attn_Wq"], f), a_Wk=np.asarray(inputs["L1_attn_Wk"], f),
        a_Wv=np.asarray(inputs["L1_attn_Wv"], f), a_Wo=np.asarray(inputs["L1_attn_Wo"], f),
        a_Wrq=np.asarray(inputs["L1_attn_Wrq"], f),
        a_Wrk=np.asarray(inputs["L1_attn_Wrk"], f),
        a_Wrv=np.asarray(inputs["L1_attn_Wrv"], f),
        a_Wg=np.asarray(inputs["L1_attn_Wg"], f),
        pastT=np.ascontiguousarray(
            np.asarray(inputs["L1_attn_past_states"], f)[:, 0].transpose(0, 2, 1)),
        brow=np.concatenate([np.asarray(inputs["L1_attn_bv"], f),
                             np.asarray(inputs["L1_attn_brv"], f)])[None, :],
        c_tmk=_colify(np.asarray(inputs["L0_attn_time_mix_key"], f)[0, 0]),
        c_tmv=_colify(np.asarray(inputs["L0_attn_time_mix_value"], f)[0, 0]),
        c_tmr=_colify(np.asarray(inputs["L0_attn_time_mix_receptance"], f)[0, 0]),
        c_tdec=_colify(np.asarray(inputs["L0_attn_time_decay"], f)),
        c_tfirst=_colify(np.asarray(inputs["L0_attn_time_first"], f)),
        c_bo=_colify(np.asarray(inputs["L0_attn_bo"], f)),
        c_bq=_colify(np.asarray(inputs["L1_attn_bq"], f)),
        c_bk=_colify(np.asarray(inputs["L1_attn_bk"], f)),
        c_bo2=_colify(np.asarray(inputs["L1_attn_bo"], f)),
        c_brq=_colify(np.asarray(inputs["L1_attn_brq"], f)),
        c_brk=_colify(np.asarray(inputs["L1_attn_brk"], f)),
        c_bg=_colify(np.asarray(inputs["L1_attn_bg"], f)),
    )
    if "fused" not in _PROG_CACHE:
        _PROG_CACHE["fused"] = build_fused_prog()
    nc = _PROG_CACHE["fused"]
    maps = [dict(xT=np.ascontiguousarray(x[b].T), **base) for b in range(N_CORES)]
    res = run_bass_kernel_spmd(nc, maps, core_ids=list(range(N_CORES)))
    return np.stack([np.asarray(res.results[b]["xout"]).T
                     for b in range(N_CORES)]).astype(np.float32)


# version 14
# speedup vs baseline: 1.2052x; 1.0134x over previous
import numpy as np
import ml_dtypes
import concourse.bass as bass
import concourse.mybir as mybir
from concourse.bass_utils import run_bass_kernel_spmd
from contextlib import ExitStack

F32 = mybir.dt.float32
BF16 = mybir.dt.bfloat16
I32 = mybir.dt.int32
I16 = mybir.dt.int16
AF = mybir.ActivationFunctionType
ALU = mybir.AluOpType
AX = mybir.AxisListType

N_CORES = 8
H = 1024
T = 2048
NT = 8
INNER = 4096
NI = 32
NH = 16
DH = 64
EPS = 1e-5
LADDER_K = 8

ENGINES = ("tensor", "vector", "scalar", "gpsimd", "sync")

R0, R1, R2, R3, RS = 0, 2048, 4096, 5632, 5632


class Op:
    __slots__ = ("idx", "engine", "fn", "reads", "writes", "deps", "is_dma",
                 "dma_sem", "dma_val", "pre_wait", "needs_inc", "eng_seq", "label")

    def __init__(self, idx, engine, fn, reads, writes, is_dma, label):
        self.idx = idx
        self.engine = engine
        self.fn = fn
        self.reads = reads
        self.writes = writes
        self.deps = []
        self.is_dma = is_dma
        self.dma_sem = None
        self.dma_val = 0
        self.pre_wait = None
        self.needs_inc = False
        self.eng_seq = 0
        self.label = label


class Prog:
    N_DMA_SEMS = 24

    def __init__(self, nc, stack):
        self.nc = nc
        self.stack = stack
        self.ops = []
        self.state = {}
        self.sems = {e: stack.enter_context(nc.semaphore(f"pg_{e}")) for e in ENGINES}
        self.dma_pool = [stack.enter_context(nc.semaphore(f"dma{i}"))
                         for i in range(self.N_DMA_SEMS)]
        self.dma_counts = [0] * self.N_DMA_SEMS
        self.dma_i = 0
        self.swdge_count = 0

    def sbuf(self, name, shape, dtype):
        return self.stack.enter_context(self.nc.sbuf_tensor(name, list(shape), dtype))

    def psum(self, name, shape, dtype=F32):
        return self.stack.enter_context(self.nc.psum_tensor(name, list(shape), dtype))

    def _dep_update(self, op):
        deps = {}
        for k in op.reads:
            st = self.state.get(k)
            if st is not None and st[0] is not None:
                deps[st[0].idx] = st[0]
        for k in op.writes:
            st = self.state.get(k)
            if st is not None:
                if st[0] is not None:
                    deps[st[0].idx] = st[0]
                for r in st[1]:
                    deps[r.idx] = r
        deps.pop(op.idx, None)
        op.deps = list(deps.values())
        for k in op.reads:
            self.state.setdefault(k, [None, []])[1].append(op)
        for k in op.writes:
            self.state[k] = [op, []]

    def op(self, engine, fn, reads=(), writes=(), label=""):
        o = Op(len(self.ops), engine, fn, tuple(reads), tuple(writes), False, label)
        self._dep_update(o)
        self.ops.append(o)
        return o

    def dma(self, out_ap, in_ap, reads=(), writes=(), queue="sync", label="", **kw):
        if queue == "gpsimd":
            sem = self.stack.enter_context(self.nc.semaphore(f"sw{self.swdge_count}"))
            self.swdge_count += 1
            o = Op(len(self.ops), "gpsimd",
                   lambda e, _o=out_ap, _i=in_ap, _kw=kw: e.dma_start(_o, _i, **_kw),
                   tuple(reads), tuple(writes), True, label)
            o.dma_sem, o.dma_val = sem, 16
        else:
            i = self.dma_i % self.N_DMA_SEMS
            self.dma_i += 1
            self.dma_counts[i] += 1
            o = Op(len(self.ops), "sync",
                   lambda e, _o=out_ap, _i=in_ap, _kw=kw: e.dma_start(_o, _i, **_kw),
                   tuple(reads), tuple(writes), True, label)
            o.dma_sem, o.dma_val = self.dma_pool[i], 16 * self.dma_counts[i]
            if self.dma_counts[i] > 1:
                o.pre_wait = (self.dma_pool[i], 16 * (self.dma_counts[i] - 1))
        self._dep_update(o)
        self.ops.append(o)
        return o

    def emit(self, block, end_fence=()):
        for o in self.ops:
            for d in o.deps:
                if not d.is_dma:
                    d.needs_inc = True
        counts = {e: 0 for e in ENGINES}
        for o in self.ops:
            if not o.is_dma:
                if o.needs_inc:
                    counts[o.engine] += 1
                o.eng_seq = counts[o.engine]
        per_engine = {e: [] for e in ENGINES}
        for o in self.ops:
            per_engine[o.engine].append(o)
        fence_list = list(end_fence)

        def make_body(ename, ops_here):
            def body(eng):
                hw = {e: 0 for e in ENGINES}
                hwd = {}
                for o in ops_here:
                    if o.pre_wait is not None:
                        s, v = o.pre_wait
                        if hwd.get(id(s), 0) < v:
                            eng.wait_ge(s, v)
                            hwd[id(s)] = v
                    for d in o.deps:
                        if d.is_dma:
                            if hwd.get(id(d.dma_sem), 0) < d.dma_val:
                                eng.wait_ge(d.dma_sem, d.dma_val)
                                hwd[id(d.dma_sem)] = d.dma_val
                        else:
                            if d.eng_seq > 0 and hw[d.engine] < d.eng_seq:
                                eng.wait_ge(self.sems[d.engine], d.eng_seq)
                                hw[d.engine] = d.eng_seq
                    inst = o.fn(eng)
                    if o.is_dma:
                        inst.then_inc(o.dma_sem, 16)
                    elif o.needs_inc:
                        inst.then_inc(self.sems[o.engine], 1)
                if ename == "sync":
                    for d in fence_list:
                        if hwd.get(id(d.dma_sem), 0) < d.dma_val:
                            eng.wait_ge(d.dma_sem, d.dma_val)
                            hwd[id(d.dma_sem)] = d.dma_val
            return body

        for ename in ENGINES:
            if per_engine[ename] or (ename == "sync" and fence_list):
                getattr(block, ename)(make_body(ename, per_engine[ename]))



def _kl(x):
    return list(x) if isinstance(x, (list, tuple)) else [x]

class MK:
    def __init__(self, nc, stack):
        self.nc = nc
        self.p = Prog(nc, stack)
        p = self.p
        self.A = p.sbuf("A", [128, 16384], F32)
        self.C = p.sbuf("C", [128, 16400], BF16)
        self.D = p.sbuf("Db", [128, 16384], BF16)
        self.E = p.sbuf("Eb", [128, 16384], BF16)
        self.S1 = p.sbuf("S1", [128, 2048], F32)
        self.SC = p.sbuf("SCb", [128, 6144], BF16)
        self.QI = p.sbuf("QI", [128, 512], mybir.dt.int8)
        self.Q2B = p.sbuf("Q2B", [128, 512], mybir.dt.int8)
        self.WL = p.sbuf("WL", [128, 1024], BF16)
        self.WF = p.sbuf("WF", [128, 1024], F32)
        self.ROWS = p.sbuf("ROWS", [1, 4096], F32)
        self.QROWF = p.sbuf("QROWF", [1, 512], F32)
        self.COLS = p.sbuf("COLS", [128, 96], F32)
        self.CCOL = p.sbuf("CCOL", [128, 96], F32)
        self.ONESC = p.sbuf("ONESC", [128, 1], BF16)
        self.ONESCF = p.sbuf("ONESCF", [128, 1], F32)
        self.ONESR = p.sbuf("ONESR", [1, 128], BF16)
        self.ONESRF = p.sbuf("ONESRF", [1, 128], F32)
        self.ONES127 = p.sbuf("ONES127", [1, 128], F32)
        self.EPSR = p.sbuf("EPSR", [1, 1], F32)
        self.SCL = p.sbuf("SCL", [1, 16], F32)
        self.IDENT = p.sbuf("IDENT", [128, 128], F32)
        self.PS = [p.psum(f"ps{i}", [128, 512], F32) for i in range(8)]
        self.PK = [f"P{i}" for i in range(8)]
        self.col_slots = {}
        self.col_next = 0

    def Av(self, i, c0=0, n=2048):
        return self.A[:, i * 2048 + c0: i * 2048 + c0 + n]

    def Cv(self, i, c0=0, n=2048):
        return self.C[:, i * 2048 + c0: i * 2048 + c0 + n]

    def init_consts(self, ident_hbm):
        p = self.p
        p.op("vector", lambda e: e.memset(self.ONESC[:], 1.0), writes=["ONESC"])
        p.op("vector", lambda e: e.memset(self.ONESCF[:], 1.0), writes=["ONESCF"])
        p.op("vector", lambda e: e.memset(self.ONESR[:], 1.0), writes=["ONESR"])
        p.op("vector", lambda e: e.memset(self.ONESRF[:], 1.0), writes=["ONESRF"])
        p.op("vector", lambda e: e.memset(self.ONES127[:], 1.0 / 127.0), writes=["ONES127"])
        p.op("vector", lambda e: e.memset(self.EPSR[:], EPS), writes=["EPSR"])
        p.dma(self.IDENT[:], ident_hbm, writes=["IDENT"], label="ident")

    def load_cols(self, hbm_ap, name):
        s = self.col_next
        self.col_next += 8
        assert self.col_next <= 96
        self.col_slots[name] = s
        self.p.dma(self.COLS[:, s:s + 8], hbm_ap, writes=[f"COLS.{name}"],
                   label=f"col.{name}")

    def col(self, name, i):
        s = self.col_slots[name]
        return self.COLS[:, s + i:s + i + 1]

    def load_x(self, x_hbm):
        for i in range(NT):
            self.p.dma(self.Av(i), x_hbm[i * 128:(i + 1) * 128, :],
                       writes=[f"A.{i}"], label=f"ldx{i}")

    def store_x(self, x_hbm):
        outs = []
        for i in range(NT):
            outs.append(self.p.dma(x_hbm[i * 128:(i + 1) * 128, :], self.Av(i),
                                   reads=[f"A.{i}"], label=f"stx{i}"))
        return outs

    def emit_ln(self, x_ap, x_key, y_ap, y_key, tag, normalize=True):
        p = self.p
        xb = self.SC[:, R0:R0 + 2048]
        x2 = self.SC[:, R1:R1 + 2048]
        for i in range(NT):
            p.op("vector", lambda e, i=i: e.tensor_copy(xb, x_ap(i)),
                 reads=_kl(x_key(i)), writes=["SC.A"],
                 label=f"{tag}.cast{i}")
            p.op("scalar", lambda e: e.activation(x2, xb, AF.Square),
                 reads=["SC.A"], writes=["SC.B"], label=f"{tag}.sq{i}")
            for c in range(4):
                r = (c // 2) * 32
                p.op("tensor", lambda e, i=i, c=c, r=r: e.matmul(
                    self.PS[4 + c % 2][r:r + 1, 0:512],
                    self.ONESC[:], xb[:, c * 512:(c + 1) * 512],
                    start=(i == 0), stop=(i == NT - 1)),
                    reads=["SC.A", "ONESC"], writes=[self.PK[4 + c % 2]],
                    label=f"{tag}.s{i}.{c}")
                p.op("tensor", lambda e, i=i, c=c, r=r: e.matmul(
                    self.PS[6 + c % 2][r:r + 1, 0:512],
                    self.ONESC[:], x2[:, c * 512:(c + 1) * 512],
                    start=(i == 0), stop=(i == NT - 1)),
                    reads=["SC.B", "ONESC"], writes=[self.PK[6 + c % 2]],
                    label=f"{tag}.q{i}.{c}")
        row = self.ROWS[0:1, 0:2048]
        rowr = self.ROWS[0:1, 2048:4096]
        for c in range(4):
            r = (c // 2) * 32
            p.op("vector", lambda e, c=c, r=r: e.tensor_scalar_mul(
                row[:, c * 512:(c + 1) * 512],
                self.PS[4 + c % 2][r:r + 1, 0:512], 1.0 / H),
                reads=[self.PK[4 + c % 2]], writes=["ROWS"], label=f"{tag}.mu{c}")
        if normalize:
            for c in range(4):
                p.op("tensor", lambda e, c=c: e.matmul(
                    self.PS[c][:, 0:512], self.ONESRF[:],
                    row[:, c * 512:(c + 1) * 512], start=True, stop=True),
                    reads=["ROWS", "ONESRF"], writes=[self.PK[c]],
                    label=f"{tag}.repmu{c}")
        p.op("scalar", lambda e: e.activation(rowr, row, AF.Square),
             reads=["ROWS"], writes=["ROWS"], label=f"{tag}.mu2")
        for c in range(4):
            r = (c // 2) * 32
            p.op("vector", lambda e, c=c, r=r: e.scalar_tensor_tensor(
                rowr[:, c * 512:(c + 1) * 512],
                self.PS[6 + c % 2][r:r + 1, 0:512], 1.0 / H,
                rowr[:, c * 512:(c + 1) * 512], ALU.mult, ALU.subtract),
                reads=[self.PK[6 + c % 2], "ROWS"], writes=["ROWS"],
                label=f"{tag}.var{c}")
        p.op("scalar", lambda e: e.activation(rowr, rowr, AF.Sqrt, bias=self.EPSR[0:1, 0:1]),
             reads=["ROWS", "EPSR"], writes=["ROWS"], label=f"{tag}.sd")
        p.op("vector", lambda e: e.reciprocal(rowr, rowr),
             reads=["ROWS"], writes=["ROWS"], label=f"{tag}.rstd")
        if not normalize:
            return
        for c in range(4):
            p.op("tensor", lambda e, c=c: e.matmul(
                self.PS[4 + c][:, 0:512], self.ONESRF[:],
                rowr[:, c * 512:(c + 1) * 512], start=True, stop=True),
                reads=["ROWS", "ONESRF"], writes=[self.PK[4 + c]],
                label=f"{tag}.reprs{c}")
        for c in range(4):
            for i in range(NT):
                p.op("vector", lambda e, i=i, c=c: e.scalar_tensor_tensor(
                    self.S1[:, 0:512], self.PS[c][:, 0:512], -1.0,
                    x_ap(i)[:, c * 512:(c + 1) * 512], ALU.mult, ALU.add),
                    reads=[self.PK[c]] + _kl(x_key(i)), writes=["S1"],
                    label=f"{tag}.sub{i}.{c}")
                p.op("vector", lambda e, i=i, c=c: e.tensor_tensor(
                    y_ap(i)[:, c * 512:(c + 1) * 512], self.S1[:, 0:512],
                    self.PS[4 + c][:, 0:512], ALU.mult),
                    reads=["S1", self.PK[4 + c]], writes=_kl(y_key(i)),
                    label=f"{tag}.nrm{i}.{c}")

    def emit_quant_scales(self, src_ap, src_key, nsrc, tlen, tag):
        p = self.p
        ntile = tlen // 128
        s1 = self.S1[:, 0:tlen]
        sqt = self.SC[:, R1:R1 + tlen] if tlen <= 2048 else None
        p.op("vector", lambda e: e.tensor_tensor(s1, src_ap(0), src_ap(0), ALU.mult),
             reads=[src_key(0)], writes=["S1"], label=f"{tag}.m0")
        for j in range(1, nsrc):
            p.op("vector", lambda e, j=j: e.tensor_tensor(sqt, src_ap(j), src_ap(j), ALU.mult),
                 reads=_kl(src_key(j)), writes=["SC.B"], label=f"{tag}.msq{j}")
            p.op("vector", lambda e: e.tensor_tensor(s1, s1, sqt, ALU.max),
                 reads=["S1", "SC.B"], writes=["S1"], label=f"{tag}.m{j}")
        for tt in range(ntile):
            p.op("tensor", lambda e, tt=tt: e.transpose(
                self.PS[4][:, 0:128], s1[:, tt * 128:(tt + 1) * 128], self.IDENT[:]),
                reads=["S1", "IDENT"], writes=[self.PK[4]], label=f"{tag}.tr{tt}")
            p.op("vector", lambda e, tt=tt, ntile=ntile: e.reduce_max(
                self.CCOL[:, ntile + tt:ntile + tt + 1], self.PS[4][:, 0:128], axis=AX.X),
                reads=[self.PK[4]], writes=["CCOL"], label=f"{tag}.red{tt}")
        cc = self.CCOL[:, ntile:2 * ntile]
        qq = self.CCOL[:, 0:ntile]
        p.op("scalar", lambda e: e.activation(cc, cc, AF.Sqrt),
             reads=["CCOL"], writes=["CCOL"], label=f"{tag}.sqrt")
        p.op("vector", lambda e: e.tensor_scalar_max(cc, cc, 1e-5),
             reads=["CCOL"], writes=["CCOL"], label=f"{tag}.clip")
        p.op("vector", lambda e: e.reciprocal(qq, cc),
             reads=["CCOL"], writes=["CCOL"], label=f"{tag}.inv")
        p.op("vector", lambda e: e.tensor_scalar_mul(qq, qq, 127.0),
             reads=["CCOL"], writes=["CCOL"], label=f"{tag}.q")
        p.op("vector", lambda e: e.tensor_scalar_mul(cc, cc, 1.0 / 127.0),
             reads=["CCOL"], writes=["CCOL"], label=f"{tag}.d")
        p.op("tensor", lambda e, n2=2 * ntile: e.transpose(
            self.PS[4][0:n2, 0:128], self.CCOL[:, 0:n2], self.IDENT[:]),
            reads=["CCOL", "IDENT"], writes=[self.PK[4]], label=f"{tag}.tr2")
        p.op("vector", lambda e, n2=2 * ntile: e.tensor_copy(
            self.WF[0:n2, 0:128], self.PS[4][0:n2, 0:128]),
            reads=[self.PK[4]], writes=["WF"], label=f"{tag}.evqf")
        nch = (tlen + 511) // 512
        for c in range(nch):
            w = min(512, tlen - c * 512)
            nr = (w + 127) // 128
            p.dma(self.QROWF[0:1, 0:w],
                  self.WF[ntile + 4 * c:ntile + 4 * c + nr, 0:128],
                  reads=["WF", "QROWF"], writes=["QROWF"],
                  label=f"{tag}.lind{c}")
            p.op("tensor", lambda e, w=w: e.matmul(
                self.PS[4][:, 0:w], self.ONESRF[:], self.QROWF[0:1, 0:w],
                start=True, stop=True),
                reads=["QROWF", "ONESRF"], writes=[self.PK[4]],
                label=f"{tag}.rd{c}")
            p.op("vector", lambda e, c=c, w=w: e.tensor_copy(
                self.SC[:, c * 512:c * 512 + w], self.PS[4][:, 0:w]),
                reads=[self.PK[4]], writes=["SC.A"],
                label=f"{tag}.red{c}")

    def _ladder(self, v, dst_ap, dst_key, n, tag):
        p = self.p
        t0 = self.SC[:, R2:R2 + n]
        ca = self.SC[:, R2 + 512:R2 + 512 + n]
        md = self.SC[:, R2 + 1024:R2 + 1024 + n]
        p.op("vector", lambda e: e.tensor_copy(self.QI[:, 0:n], v),
             reads=["S1"], writes=["QI"], label=f"{tag}.i")
        p.op("vector", lambda e: e.tensor_copy(t0, self.QI[:, 0:n]),
             reads=["QI"], writes=["SC.C"], label=f"{tag}.t0")
        p.op("vector", lambda e: e.tensor_tensor(md, t0, v, ALU.subtract),
             reads=["SC.C", "S1"], writes=["SC.C"], label=f"{tag}.md")
        p.op("vector", lambda e: e.tensor_scalar(dst_ap, md, -0.5, None, ALU.is_le),
             reads=["SC.C"], writes=_kl(dst_key), label=f"{tag}.c1")
        p.op("vector", lambda e: e.tensor_tensor(dst_ap, t0, dst_ap, ALU.add),
             reads=["SC.C"] + _kl(dst_key), writes=_kl(dst_key), label=f"{tag}.a1")
        _LVL = ([(-0.5 - k, ALU.add, ALU.is_le) for k in range(1, LADDER_K)]
                + [(0.5 + k, ALU.subtract, ALU.is_ge) for k in range(LADDER_K)])
        for (thr, op, alu) in _LVL:
            p.op("vector", lambda e, thr=thr, alu=alu: e.tensor_scalar(
                ca, md, thr, None, alu),
                reads=["SC.C"], writes=["SC.C"], label=f"{tag}.c{thr}")
            p.op("vector", lambda e, op=op: e.tensor_tensor(dst_ap, dst_ap, ca, op),
                 reads=_kl(dst_key) + ["SC.C"], writes=_kl(dst_key), label=f"{tag}.a{thr}")

    def _ladder_sar(self, v, dst_ap, dst_key, n, tag, eng="vector", vkey="S1"):
        p = self.p
        qi = self.QI[:, 0:n]
        if eng == "vector":
            b = self.SC[:, R2 + 1024:R2 + 1024 + n]
            c = self.S1[:, 1536:1536 + n]
            d = self.S1[:, 1024:1024 + n]
            kb, kc, kd = "SC.C", "S1", "S1"
        else:
            b = self.Q2B[:, 0:n]
            c = self.SC[:, R2 + 512:R2 + 512 + n]
            d = v
            kb, kc, kd = "Q2B", "SC.C", vkey
        p.op(eng, lambda e: e.tensor_copy(qi, v),
             reads=[vkey], writes=["QI"], label=f"{tag}.i")
        p.op(eng, lambda e: e.tensor_copy(c, qi),
             reads=["QI"], writes=[kc], label=f"{tag}.c")
        p.op(eng, lambda e: e.tensor_tensor(d, v, c, ALU.subtract),
             reads=[vkey, kc], writes=[kd], label=f"{tag}.d")
        p.op(eng, lambda e: e.tensor_scalar(b, d, 0.5, None, ALU.is_ge),
             reads=[kd], writes=[kb], label=f"{tag}.t")
        p.op(eng, lambda e: e.scalar_tensor_tensor(
            b, d, -0.5, b, ALU.is_le, ALU.subtract),
            reads=[kd, kb], writes=[kb], label=f"{tag}.u")
        p.op(eng, lambda e: e.tensor_tensor(dst_ap, c, b, ALU.subtract),
             reads=[kc, kb], writes=_kl(dst_key), label=f"{tag}.fin")

    def emit_quant1(self, tag):
        p = self.p
        for c in range(4):
            p.op("tensor", lambda e, c=c: e.matmul(
                self.PS[6][:, 0:512], self.ONESRF[:],
                self.ROWS[0:1, c * 512:(c + 1) * 512], start=True, stop=True),
                reads=["ROWS", "ONESRF"], writes=[self.PK[6]], label=f"{tag}.m{c}")
            for i in range(NT):
                p.op("vector", lambda e, i=i, c=c: e.scalar_tensor_tensor(
                    self.S1[:, 0:512], self.PS[6][:, 0:512], -1.0,
                    self.Av(i, c * 512, 512), ALU.mult, ALU.add),
                    reads=[self.PK[6], f"A.{i}"], writes=["S1"], label=f"{tag}.d{c}.{i}")
                p.op("vector", lambda e: e.tensor_tensor(
                    self.S1[:, 512:1024], self.S1[:, 0:512], self.S1[:, 0:512], ALU.mult),
                    reads=["S1"], writes=["S1"], label=f"{tag}.sq{c}.{i}")
                if i == 0:
                    p.op("vector", lambda e: e.tensor_copy(
                        self.S1[:, 1024:1536], self.S1[:, 512:1024]),
                        reads=["S1"], writes=["S1"], label=f"{tag}.M0.{c}")
                else:
                    p.op("vector", lambda e: e.tensor_tensor(
                        self.S1[:, 1024:1536], self.S1[:, 1024:1536],
                        self.S1[:, 512:1024], ALU.max),
                        reads=["S1"], writes=["S1"], label=f"{tag}.M{c}.{i}")
            for tt in range(4):
                p.op("tensor", lambda e, tt=tt: e.transpose(
                    self.PS[4][:, 0:128],
                    self.S1[:, 1024 + tt * 128:1024 + (tt + 1) * 128], self.IDENT[:]),
                    reads=["S1", "IDENT"], writes=[self.PK[4]], label=f"{tag}.tr{c}.{tt}")
                p.op("vector", lambda e, c=c, tt=tt: e.reduce_max(
                    self.CCOL[:, 16 + 4 * c + tt:16 + 4 * c + tt + 1],
                    self.PS[4][:, 0:128], axis=AX.X),
                    reads=[self.PK[4]], writes=["CCOL"], label=f"{tag}.red{c}.{tt}")
        p.op("scalar", lambda e: e.activation(
            self.CCOL[:, 16:32], self.CCOL[:, 16:32], AF.Sqrt),
            reads=["CCOL"], writes=["CCOL"], label=f"{tag}.sqrt")
        p.op("tensor", lambda e: e.transpose(
            self.PS[4][0:16, 0:128], self.CCOL[:, 16:32], self.IDENT[:]),
            reads=["CCOL", "IDENT"], writes=[self.PK[4]], label=f"{tag}.tr2")
        p.op("vector", lambda e: e.tensor_copy(
            self.WF[0:16, 0:128], self.PS[4][0:16, 0:128]),
            reads=[self.PK[4]], writes=["WF"], label=f"{tag}.evqf")
        for c in range(4):
            p.dma(self.QROWF[0:1, 0:512], self.WF[4 * c:4 * c + 4, 0:128],
                  reads=["WF", "QROWF"], writes=["QROWF"], label=f"{tag}.ql{c}")
            p.op("vector", lambda e, c=c: e.tensor_tensor(
                self.QROWF[0:1, 0:512], self.QROWF[0:1, 0:512],
                self.ROWS[0:1, 2048 + c * 512:2048 + (c + 1) * 512], ALU.mult),
                reads=["QROWF", "ROWS"], writes=["QROWF"], label=f"{tag}.my{c}")
            p.op("vector", lambda e: e.tensor_scalar_max(
                self.QROWF[0:1, 0:512], self.QROWF[0:1, 0:512], 1e-5),
                reads=["QROWF"], writes=["QROWF"], label=f"{tag}.cl{c}")
            p.op("tensor", lambda e: e.matmul(
                self.PS[4][:, 0:512], self.ONES127[:], self.QROWF[0:1, 0:512],
                start=True, stop=True),
                reads=["QROWF", "ONES127"], writes=[self.PK[4]], label=f"{tag}.dr{c}")
            p.op("vector", lambda e, c=c: e.tensor_copy(
                self.SC[:, c * 512:(c + 1) * 512], self.PS[4][:, 0:512]),
                reads=[self.PK[4]], writes=["SC.A"], label=f"{tag}.de{c}")
            p.op("vector", lambda e: e.reciprocal(
                self.QROWF[0:1, 0:512], self.QROWF[0:1, 0:512]),
                reads=["QROWF"], writes=["QROWF"], label=f"{tag}.rc{c}")
            p.op("vector", lambda e: e.tensor_scalar_mul(
                self.QROWF[0:1, 0:512], self.QROWF[0:1, 0:512], 127.0),
                reads=["QROWF"], writes=["QROWF"], label=f"{tag}.q{c}")
            p.op("tensor", lambda e: e.matmul(
                self.PS[5][:, 0:512], self.ONESRF[:], self.QROWF[0:1, 0:512],
                start=True, stop=True),
                reads=["QROWF", "ONESRF"], writes=[self.PK[5]], label=f"{tag}.qr{c}")
            p.op("tensor", lambda e, c=c: e.matmul(
                self.PS[6][:, 0:512], self.ONESRF[:],
                self.ROWS[0:1, c * 512:(c + 1) * 512], start=True, stop=True),
                reads=["ROWS", "ONESRF"], writes=[self.PK[6]], label=f"{tag}.mr{c}")
            p.op("tensor", lambda e, c=c: e.matmul(
                self.PS[7][:, 0:512], self.ONESRF[:],
                self.ROWS[0:1, 2048 + c * 512:2048 + (c + 1) * 512],
                start=True, stop=True),
                reads=["ROWS", "ONESRF"], writes=[self.PK[7]], label=f"{tag}.rr{c}")
            for s in range(NT):
                odd = False
                s1v = self.S1[:, 0:512]
                v = s1v
                vk = "S1"
                p.op("vector", lambda e, s=s, c=c: e.scalar_tensor_tensor(
                    s1v, self.PS[6][:, 0:512], -1.0, self.Av(s, c * 512, 512),
                    ALU.mult, ALU.add),
                    reads=[self.PK[6], f"A.{s}"], writes=["S1"],
                    label=f"{tag}.v1.{c}.{s}")
                p.op("vector", lambda e: e.tensor_tensor(
                    s1v, s1v, self.PS[7][:, 0:512], ALU.mult),
                    reads=["S1", self.PK[7]], writes=["S1"], label=f"{tag}.v2.{c}.{s}")
                p.op("vector", lambda e, v=v: e.tensor_tensor(
                    v, s1v, self.PS[5][:, 0:512], ALU.mult),
                    reads=["S1", self.PK[5]], writes=[vk], label=f"{tag}.v3.{c}.{s}")
                self._ladder_sar(v, self.Cv(s, c * 512, 512), f"C.{s}", 512,
                                 f"{tag}.L{c}.{s}",
                                 eng="gpsimd" if odd else "vector", vkey=vk)

    def emit_quant_apply(self, mode, src_ap, src_key, dst_ap, dst_key, nsrc, tlen, tag):
        p = self.p
        for c in range(tlen // 512):
            p.dma(self.QROWF[0:1, 0:512], self.WF[4 * c:4 * c + 4, 0:128],
                  reads=["WF", "QROWF"], writes=["QROWF"], label=f"{tag}.ql{c}")
            p.op("tensor", lambda e: e.matmul(
                self.PS[5][:, 0:512], self.ONESRF[:], self.QROWF[0:1, 0:512],
                start=True, stop=True),
                reads=["QROWF", "ONESRF"], writes=[self.PK[5]], label=f"{tag}.qr{c}")
            if mode == "ln":
                p.op("tensor", lambda e, c=c: e.matmul(
                    self.PS[6][:, 0:512], self.ONESRF[:],
                    self.ROWS[0:1, c * 512:(c + 1) * 512], start=True, stop=True),
                    reads=["ROWS", "ONESRF"], writes=[self.PK[6]], label=f"{tag}.mr{c}")
                p.op("tensor", lambda e, c=c: e.matmul(
                    self.PS[7][:, 0:512], self.ONESRF[:],
                    self.ROWS[0:1, 2048 + c * 512:2048 + (c + 1) * 512],
                    start=True, stop=True),
                    reads=["ROWS", "ONESRF"], writes=[self.PK[7]], label=f"{tag}.rr{c}")
            for s in range(nsrc):
                sap = src_ap(s)[:, c * 512:(c + 1) * 512]
                dap = dst_ap(s)[:, c * 512:(c + 1) * 512]
                odd = False
                s1v = self.S1[:, 0:512]
                v = s1v
                vk = "S1"
                leng = "vector"
                if mode == "ln":
                    p.op("vector", lambda e, sap=sap: e.scalar_tensor_tensor(
                        s1v, self.PS[6][:, 0:512], -1.0, sap, ALU.mult, ALU.add),
                        reads=[self.PK[6]] + _kl(src_key(s)), writes=["S1"],
                        label=f"{tag}.v1.{c}.{s}")
                    p.op("vector", lambda e: e.tensor_tensor(
                        s1v, s1v, self.PS[7][:, 0:512], ALU.mult),
                        reads=["S1", self.PK[7]], writes=["S1"], label=f"{tag}.v2.{c}.{s}")
                    p.op("vector", lambda e, v=v: e.tensor_tensor(
                        v, s1v, self.PS[5][:, 0:512], ALU.mult),
                        reads=["S1", self.PK[5]], writes=[vk], label=f"{tag}.v3.{c}.{s}")
                    self._ladder_sar(v, dap, dst_key(s), 512, f"{tag}.L{c}.{s}",
                                     eng=leng, vkey=vk)
                else:
                    p.op("vector", lambda e, sap=sap, v=v: e.tensor_tensor(
                        v, sap, self.PS[5][:, 0:512], ALU.mult),
                        reads=_kl(src_key(s)) + [self.PK[5]], writes=[vk],
                        label=f"{tag}.v.{c}.{s}")
                    self._ladder_sar(v, dap, dst_key(s), 512, f"{tag}.L{c}.{s}",
                                     eng=leng, vkey=vk)

    def emit_wstats(self, hbm_w, rows, cols, slot, extra_mul, tag):
        p = self.p
        n = rows * cols
        view = hbm_w.rearrange("(q p) c -> p q c", p=128)
        nq = rows // 128
        csz = min(cols, 2048)
        chunks = [(q, c0) for q in range(nq) for c0 in range(0, cols, csz)]
        acc = self.CCOL[:, 44:45]
        first = True
        for ch, (q, c0) in enumerate(chunks):
            p.dma(self.S1[:, 0:csz], view[:, q, c0:c0 + csz],
                  writes=["S1"], label=f"{tag}.ld{ch}")
            if first:
                p.op("vector", lambda e, csz=csz: e.reduce_sum(
                    acc, self.S1[:, 0:csz], axis=AX.X, apply_absolute_value=True),
                    reads=["S1"], writes=["CCOL"], label=f"{tag}.r{ch}")
                first = False
            else:
                p.op("vector", lambda e, csz=csz: e.reduce_sum(
                    self.CCOL[:, 45:46], self.S1[:, 0:csz], axis=AX.X,
                    apply_absolute_value=True),
                    reads=["S1"], writes=["CCOL"], label=f"{tag}.r{ch}")
                p.op("vector", lambda e: e.tensor_add(acc, acc, self.CCOL[:, 45:46]),
                     reads=["CCOL", "CCOL"], writes=["CCOL"], label=f"{tag}.a{ch}")
        p.op("tensor", lambda e: e.matmul(self.PS[4][0:1, 0:1], acc, self.ONESCF[:],
                                          start=True, stop=True),
             reads=["CCOL", "ONESCF"], writes=[self.PK[4]], label=f"{tag}.x")
        p.op("vector", lambda e: e.tensor_scalar_mul(self.SCL[0:1, 0:1],
                                                     self.PS[4][0:1, 0:1], 1.0 / n),
             reads=[self.PK[4]], writes=["SCL"], label=f"{tag}.mean")
        p.op("vector", lambda e: e.tensor_scalar_max(self.SCL[0:1, 0:1],
                                                     self.SCL[0:1, 0:1], 1e-5),
             reads=["SCL"], writes=["SCL"], label=f"{tag}.clip")
        p.op("tensor", lambda e: e.matmul(self.PS[4][:, 0:1], self.ONESRF[:],
                                          self.SCL[0:1, 0:1], start=True, stop=True),
             reads=["SCL", "ONESRF"], writes=[self.PK[4]], label=f"{tag}.rep")
        p.op("vector", lambda e: e.tensor_scalar_mul(
            self.CCOL[:, slot:slot + 1], self.PS[4][:, 0:1], 0.5),
            reads=[self.PK[4]], writes=["CCOL"], label=f"{tag}.thr")
        p.op("vector", lambda e: e.tensor_scalar_mul(
            self.CCOL[:, slot + 1:slot + 2], self.PS[4][:, 0:1], -0.5),
            reads=[self.PK[4]], writes=["CCOL"], label=f"{tag}.nthr")
        p.op("vector", lambda e, em=extra_mul: e.tensor_scalar_mul(
            self.CCOL[:, slot + 2:slot + 3], self.PS[4][:, 0:1], em),
            reads=[self.PK[4]], writes=["CCOL"], label=f"{tag}.sw")

    def emit_ffn(self, wg, wu, wd, wdq, tag):
        p = self.p
        self.emit_wstats(wg, H, INNER, 32, 1.0, f"{tag}.wsg")
        self.emit_wstats(wu, H, INNER, 36, 1.0, f"{tag}.wsu")
        self.emit_wstats(wd, INNER, H, 40, 0.5, f"{tag}.wsd")
        wdv = wd.rearrange("(q p) c -> p q c", p=128)
        wqv = wdq.rearrange("(q p) c -> p q c", p=128)
        for q in range(NI):
            p.dma(self.WF[:, 0:1024], wdv[:, q, :], writes=["WF"], label=f"{tag}.dl{q}")
            p.op("vector", lambda e: e.tensor_scalar(
                self.WL[:, 0:1024], self.WF[:, 0:1024], self.CCOL[:, 40:41], None, ALU.is_ge),
                reads=["WF", "CCOL"], writes=["WLa"], label=f"{tag}.d1{q}")
            p.op("vector", lambda e: e.tensor_scalar(
                self.SC[:, 2048:3072], self.WF[:, 0:1024], self.CCOL[:, 41:42], None, ALU.is_le),
                reads=["WF", "CCOL"], writes=["SC.B"], label=f"{tag}.d2{q}")
            p.op("vector", lambda e: e.tensor_sub(
                self.WL[:, 0:1024], self.WL[:, 0:1024], self.SC[:, 2048:3072]),
                reads=["WLa", "SC.B"], writes=["WLa"], label=f"{tag}.d3{q}")
            p.dma(wqv[:, q, :], self.WL[:, 0:1024], reads=["WLa"], label=f"{tag}.ds{q}")
        self.emit_ln(lambda i: self.Av(i), lambda i: f"A.{i}",
                     lambda i: self.Cv(i), lambda i: f"C.{i}", f"{tag}.ln",
                     normalize=False)
        self.emit_quant1(f"{tag}.q1")
        wgv = wg.rearrange("(j p) i -> p j i", p=128)
        wuv = wu.rearrange("(j p) i -> p j i", p=128)
        for hh in range(2):
            for it in range(NI):
                pr = (self.D if it < 16 else self.E)
                poff = (it % 16) * 1024
                pkey = f"{'D' if it < 16 else 'E'}.{it % 16}"
                c2s = self.SC[:, R2:R2 + 1024]
                for (nm, off, sl, wv) in (("g", 0, 32, wgv), ("u", 1024, 36, wuv)):
                    wk = "WLa" if off == 0 else "SC.B"
                    p.dma(self.WF[:, 0:1024].rearrange("p (j f) -> p j f", f=128),
                          wv[:, :, it * 128:(it + 1) * 128], writes=["WF"],
                          label=f"{tag}.{nm}{hh}.{it}.ld")
                    tdst = (self.WL[:, 0:1024] if off == 0
                            else self.SC[:, 2048:3072])
                    p.op("vector", lambda e, sl=sl, tdst=tdst: e.tensor_scalar(
                        tdst, self.WF[:, 0:1024],
                        self.CCOL[:, sl:sl + 1], None, ALU.is_ge),
                        reads=["WF", "CCOL"], writes=[wk],
                        label=f"{tag}.t1{nm}{hh}.{it}")
                    p.op("vector", lambda e, off=off, sl=sl, c2s=c2s: e.tensor_scalar(
                        c2s, self.WF[:, 0:1024],
                        self.CCOL[:, sl + 1:sl + 2], None, ALU.is_le),
                        reads=["WF", "CCOL"], writes=["SC.C"],
                        label=f"{tag}.t2{nm}{hh}.{it}")
                    p.op("vector", lambda e, c2s=c2s, tdst=tdst: e.tensor_sub(
                        tdst, tdst, c2s),
                        reads=[wk, "SC.C"], writes=[wk],
                        label=f"{tag}.t3{nm}{hh}.{it}")
                for c in range(2):
                    t0 = hh * 1024 + c * 512
                    def gmm(e, it=it, t0=t0):
                        last = None
                        for j in range(NT):
                            last = e.matmul(self.PS[0][:, 0:512],
                                            self.WL[:, j * 128:(j + 1) * 128],
                                            self.Cv(j, t0, 512),
                                            start=(j == 0), stop=(j == NT - 1))
                        return last
                    p.op("tensor", gmm,
                         reads=["WLa"] + [f"C.{j}" for j in range(NT)],
                         writes=[self.PK[0]], label=f"{tag}.mmg{hh}.{it}.{c}")
                    def umm(e, it=it, t0=t0):
                        last = None
                        for j in range(NT):
                            last = e.matmul(self.PS[1][:, 0:512],
                                            self.SC[:, 2048 + j * 128:2048 + (j + 1) * 128],
                                            self.Cv(j, t0, 512),
                                            start=(j == 0), stop=(j == NT - 1))
                        return last
                    p.op("tensor", umm,
                         reads=["SC.B"] + [f"C.{j}" for j in range(NT)],
                         writes=[self.PK[1]], label=f"{tag}.mmu{hh}.{it}.{c}")
                    p.op("vector", lambda e, t0=t0: e.scalar_tensor_tensor(
                        self.S1[:, 0:512], self.PS[0][:, 0:512], self.CCOL[:, 34:35],
                        self.SC[:, t0:t0 + 512], ALU.mult, ALU.mult),
                        reads=[self.PK[0], "CCOL", "SC.A"], writes=["S1"],
                        label=f"{tag}.eg{hh}.{it}.{c}")
                    p.op("scalar", lambda e: e.activation(
                        self.SC[:, R3:R3 + 512], self.S1[:, 0:512], AF.Silu),
                        reads=["S1"], writes=["SC.D"], label=f"{tag}.sil{hh}.{it}.{c}")
                    p.op("vector", lambda e, t0=t0: e.scalar_tensor_tensor(
                        self.S1[:, 512:1024], self.PS[1][:, 0:512], self.CCOL[:, 38:39],
                        self.SC[:, t0:t0 + 512], ALU.mult, ALU.mult),
                        reads=[self.PK[1], "CCOL", "SC.A"], writes=["S1"],
                        label=f"{tag}.eu{hh}.{it}.{c}")
                    p.op("vector", lambda e, pr=pr, poff=poff, c=c: e.tensor_tensor(
                        pr[:, poff + c * 512:poff + c * 512 + 512],
                        self.SC[:, R3:R3 + 512], self.S1[:, 512:1024], ALU.mult),
                        reads=["SC.D", "S1"], writes=[pkey],
                        label=f"{tag}.p{hh}.{it}.{c}")
            def psrc(j, hh=hh):
                pr = (self.D if j < 16 else self.E)
                return pr[:, (j % 16) * 1024:(j % 16) * 1024 + 1024]
            def psrck(j):
                return f"{'D' if j < 16 else 'E'}.{j % 16}"
            self.emit_quant_scales(psrc, psrck, NI, 1024, f"{tag}.q2{hh}")
            self.emit_quant_apply("p", psrc, psrck, psrc, psrck,
                                  NI, 1024, f"{tag}.qa2{hh}")
            for c in range(2):
                t0 = hh * 1024 + c * 512
                for q in range(NI):
                    p.dma(self.WL[:, 0:1024], wqv[:, q, :], writes=["WLa"],
                          label=f"{tag}.dw{hh}.{c}.{q}")
                    def dmm(e, q=q, c=c, hh=hh):
                        last = None
                        for m in range(NT):
                            pr = (self.D if q < 16 else self.E)
                            last = e.matmul(
                                self.PS[m][:, 0:512],
                                self.WL[:, m * 128:(m + 1) * 128],
                                pr[:, (q % 16) * 1024 + c * 512:(q % 16) * 1024 + c * 512 + 512],
                                start=(q == 0), stop=(q == NI - 1))
                        return last
                    p.op("tensor", dmm,
                         reads=["WLa", psrck(q)], writes=self.PK[0:8],
                         label=f"{tag}.mmd{hh}.{c}.{q}")
                for m in range(NT):
                    p.op("vector", lambda e, m=m, c=c: e.scalar_tensor_tensor(
                        self.S1[:, 0:512], self.PS[m][:, 0:512], self.CCOL[:, 42:43],
                        self.SC[:, c * 512:c * 512 + 512], ALU.mult, ALU.mult),
                        reads=[self.PK[m], "CCOL", "SC.A"], writes=["S1"],
                        label=f"{tag}.ed{hh}.{c}.{m}")
                    p.op("vector", lambda e, m=m, t0=t0: e.tensor_tensor(
                        self.Av(m, t0, 512), self.Av(m, t0, 512),
                        self.S1[:, 0:512], ALU.add),
                        reads=["S1", f"A.{m}"], writes=[f"A.{m}"],
                        label=f"{tag}.acc{hh}.{c}.{m}")


def build_ffn_prog():
    nc = bass.Bass("TRN2", target_bir_lowering=False, debug=False)
    xT = nc.dram_tensor("xT", [H, T], F32, kind="ExternalInput").ap()
    wg = nc.dram_tensor("wg", [H, INNER], F32, kind="ExternalInput").ap()
    wu = nc.dram_tensor("wu", [H, INNER], F32, kind="ExternalInput").ap()
    wd = nc.dram_tensor("wd", [INNER, H], F32, kind="ExternalInput").ap()
    ident = nc.dram_tensor("ident", [128, 128], F32, kind="ExternalInput").ap()
    xout = nc.dram_tensor("xout", [H, T], F32, kind="ExternalOutput").ap()
    wdq = nc.dram_tensor("wdq", [INNER, H], BF16, kind="Internal").ap()
    with ExitStack() as stack:
        mk = MK(nc, stack)
        mk.init_consts(ident)
        mk.load_x(xT)
        mk.emit_ffn(wg, wu, wd, wdq, "f")
        outs = mk.store_x(xout)
        with nc.Block() as block:
            mk.p.emit(block, end_fence=outs)
    return nc


def _ident_np():
    return np.eye(128, dtype=np.float32)


def _colify(v):
    return np.ascontiguousarray(v.reshape(8, 128).T).astype(np.float32)



def _ck(i):
    a = (i * 2050) // 2048
    b = min((i * 2050 + 2049) // 2048, 7)
    return [f"C.{a}"] if a == b else [f"C.{a}", f"C.{b}"]


def _dk(i):
    return [f"D.{2 * i}", f"D.{2 * i + 1}"]


def _ek(i):
    return [f"E.{2 * i}", f"E.{2 * i + 1}"]


def _mk_emit_rwkv(self, wk, wv, wr, wo, xspill, tag):
    p = self.p
    def yv(i, off=1, n=2048):
        return self.C[:, i * 2050 + off: i * 2050 + off + n]
    for i in range(NT):
        p.op("vector", lambda e, i=i: e.memset(self.C[:, i * 2050:i * 2050 + 1], 0.0),
             writes=_ck(i), label=f"{tag}.z{i}")
    self.emit_ln(lambda i: self.Av(i), lambda i: f"A.{i}",
                 lambda i: yv(i), lambda i: _ck(i), f"{tag}.ln")
    for i in range(NT):
        p.dma(xspill[i * 128:(i + 1) * 128, :], self.Av(i),
              reads=[f"A.{i}"], writes=["XSP"], label=f"{tag}.sp{i}")
    for (nm, sl) in (("tmk", 48), ("tmv", 56), ("tmr", 64)):
        p.op("vector", lambda e, nm=nm, sl=sl: e.tensor_scalar(
            self.CCOL[:, sl:sl + 8], self.COLS[:, self.col_slots[nm]:self.col_slots[nm] + 8],
            -1.0, 1.0, ALU.mult, ALU.add),
            reads=[f"COLS.{nm}"], writes=["CCOL"], label=f"{tag}.om{nm}")
    p.op("scalar", lambda e: e.activation(
        self.CCOL[:, 72:80], self.COLS[:, self.col_slots["tdec"]:self.col_slots["tdec"] + 8],
        AF.Exp), reads=["COLS.tdec"], writes=["CCOL"], label=f"{tag}.td1")
    p.op("scalar", lambda e: e.activation(
        self.CCOL[:, 72:80], self.CCOL[:, 72:80], AF.Exp, scale=-1.0),
        reads=["CCOL"], writes=["CCOL"], label=f"{tag}.td2")
    p.op("scalar", lambda e: e.activation(
        self.CCOL[:, 80:88], self.COLS[:, self.col_slots["tfirst"]:self.col_slots["tfirst"] + 8],
        AF.Exp), reads=["COLS.tfirst"], writes=["CCOL"], label=f"{tag}.etf")

    def emit_mix(osl, dkey_pref, tag2):
        tm = {48: "tmk", 56: "tmv", 64: "tmr"}[osl]
        for i in range(NT):
            p.op("vector", lambda e, i=i, osl=osl: e.tensor_scalar(
                self.SC[:, 0:2048], yv(i, 0), self.CCOL[:, osl + i:osl + i + 1],
                None, ALU.mult),
                reads=_ck(i) + ["CCOL"], writes=["SC.A"], label=f"{tag2}.a{i}")
            p.op("vector", lambda e, i=i, tm=tm: e.scalar_tensor_tensor(
                self.D[:, i * 2048:(i + 1) * 2048], yv(i),
                self.col(tm, i), self.SC[:, 0:2048], ALU.mult, ALU.add),
                reads=_ck(i) + [f"COLS.{tm}", "SC.A"], writes=_dk(i),
                label=f"{tag2}.b{i}")

    def proj_weights(w_hbm, i, tag2):
        p.dma(self.WF[:, 0:1024].rearrange("p (j f) -> p j f", f=128),
              w_hbm.rearrange("(j p) o -> p j o", p=128)[:, :, i * 128:(i + 1) * 128],
              writes=["WF"], label=f"{tag2}.wl{i}")
        p.op("vector", lambda e: e.tensor_copy(self.WL[:, 0:1024], self.WF[:, 0:1024]),
             reads=["WF"], writes=["WLa"], label=f"{tag2}.wc{i}")

    def proj_group(e, i, c, src_pref):
        last = None
        for j in range(NT):
            last = e.matmul(self.PS[c % 2][:, 0:512],
                            self.WL[:, j * 128:(j + 1) * 128],
                            self.D[:, j * 2048 + c * 512: j * 2048 + c * 512 + 512],
                            start=(j == 0), stop=(j == NT - 1))
        return last

    emit_mix(56, "D", f"{tag}.vm")
    for i in range(NT):
        proj_weights(wv, i, f"{tag}.vw{i}")
        for c in range(4):
            p.op("tensor", lambda e, i=i, c=c: proj_group(e, i, c, "D"),
                 reads=["WLa"] + [k for j in range(NT) for k in _dk(j)],
                 writes=[self.PK[c % 2]], label=f"{tag}.vmm{i}.{c}")
            p.op("vector", lambda e, i=i, c=c: e.tensor_copy(
                self.E[:, i * 2048 + c * 512:i * 2048 + c * 512 + 512],
                self.PS[c % 2][:, 0:512]),
                reads=[self.PK[c % 2]], writes=_ek(i), label=f"{tag}.ve{i}.{c}")
    emit_mix(64, "D", f"{tag}.rm")
    for i in range(NT):
        proj_weights(wr, i, f"{tag}.rw{i}")
        for c in range(4):
            p.op("tensor", lambda e, i=i, c=c: proj_group(e, i, c, "D"),
                 reads=["WLa"] + [k for j in range(NT) for k in _dk(j)],
                 writes=[self.PK[c % 2]], label=f"{tag}.rmm{i}.{c}")
            p.op("scalar", lambda e, i=i, c=c: e.activation(
                self.Av(i, c * 512, 512), self.PS[c % 2][:, 0:512], AF.Sigmoid),
                reads=[self.PK[c % 2]], writes=[f"A.{i}"], label=f"{tag}.re{i}.{c}")
    emit_mix(48, "D", f"{tag}.km")
    for i in range(NT):
        proj_weights(wk, i, f"{tag}.kw{i}")
        for c in range(4):
            p.op("tensor", lambda e, i=i, c=c: proj_group(e, i, c, "D"),
                 reads=["WLa"] + [k for j in range(NT) for k in _dk(j)],
                 writes=[self.PK[c % 2]], label=f"{tag}.kmm{i}.{c}")
            p.op("scalar", lambda e, i=i, c=c: e.activation(
                self.SC[:, c * 512:c * 512 + 512], self.PS[c % 2][:, 0:512], AF.Exp),
                reads=[self.PK[c % 2]], writes=["SC.A"], label=f"{tag}.ek{i}.{c}")
        p.op("vector", lambda e, i=i: e.tensor_tensor(
            self.SC[:, 2048:4096], self.SC[:, 0:2048],
            self.E[:, i * 2048:(i + 1) * 2048], ALU.mult),
            reads=["SC.A"] + _ek(i), writes=["SC.B"], label=f"{tag}.ekv{i}")
        p.op("vector", lambda e: e.memset(self.S1[:, 0:2048], 1.0),
             writes=["S1"], label=f"{tag}.tdm{i}")
        p.op("vector", lambda e, i=i: e.tensor_scalar(
            self.S1[:, 0:2048], self.S1[:, 0:2048],
            self.CCOL[:, 72 + i:72 + i + 1], None, ALU.mult),
            reads=["S1", "CCOL"], writes=["S1"], label=f"{tag}.tdf{i}")
        p.op("vector", lambda e: e.tensor_tensor_scan(
            self.SC[:, 4096:6144], self.S1[:, 0:2048], self.SC[:, 2048:4096],
            0.0, ALU.mult, ALU.add),
            reads=["S1", "SC.B"], writes=["SC.C", "SC.D"], label=f"{tag}.nsc{i}")
        p.op("vector", lambda e, i=i: e.tensor_tensor_scan(
            self.C[:, i * 2050:i * 2050 + 2048], self.S1[:, 0:2048],
            self.SC[:, 0:2048], 0.0, ALU.mult, ALU.add),
            reads=["S1", "SC.A"] + _ck(i), writes=_ck(i), label=f"{tag}.dsc{i}")
        p.op("vector", lambda e, i=i: e.tensor_scalar(
            self.SC[:, 2048:4096], self.SC[:, 2048:4096],
            self.CCOL[:, 80 + i:80 + i + 1], None, ALU.mult),
            reads=["SC.B", "CCOL"], writes=["SC.B"], label=f"{tag}.e1v{i}")
        p.op("vector", lambda e, i=i: e.tensor_scalar(
            self.SC[:, 0:2048], self.SC[:, 0:2048],
            self.CCOL[:, 80 + i:80 + i + 1], None, ALU.mult),
            reads=["SC.A", "CCOL"], writes=["SC.A"], label=f"{tag}.e1{i}")
        p.op("vector", lambda e, i=i: e.tensor_copy(
            self.S1[:, 0:1], self.SC[:, 0:1]),
            reads=["SC.A"], writes=["S1"], label=f"{tag}.d0{i}")
        p.op("vector", lambda e, i=i: e.tensor_tensor(
            self.S1[:, 1:2048], self.C[:, i * 2050:i * 2050 + 2047],
            self.SC[:, 1:2048], ALU.add),
            reads=_ck(i) + ["SC.A", "S1"], writes=["S1"], label=f"{tag}.dsh{i}")
        p.op("vector", lambda e: e.reciprocal(self.S1[:, 0:2048], self.S1[:, 0:2048]),
             reads=["S1"], writes=["S1"], label=f"{tag}.rec{i}")
        p.op("vector", lambda e: e.tensor_tensor(
            self.SC[:, 2049:4096], self.SC[:, 4096:4096 + 2047],
            self.SC[:, 2049:4096], ALU.add),
            reads=["SC.C", "SC.D", "SC.B"], writes=["SC.B"], label=f"{tag}.nsh{i}")
        p.op("vector", lambda e: e.tensor_tensor(
            self.S1[:, 0:2048], self.SC[:, 2048:4096], self.S1[:, 0:2048], ALU.mult),
            reads=["SC.B", "S1"], writes=["S1"], label=f"{tag}.wkv{i}")
        p.op("vector", lambda e, i=i: e.tensor_tensor(
            self.C[:, i * 2050:i * 2050 + 2048], self.S1[:, 0:2048],
            self.Av(i), ALU.mult),
            reads=["S1", f"A.{i}"] + _ck(i), writes=_ck(i), label=f"{tag}.rw{i}")
    for m in range(NT):
        p.dma(self.Av(m), xspill[m * 128:(m + 1) * 128, :],
              reads=[f"A.{m}", "XSP"], writes=[f"A.{m}"], label=f"{tag}.rl{m}")
    for m in range(NT):
        proj_weights(wo, m, f"{tag}.ow{m}")
        for c in range(4):
            def omm(e, m=m, c=c):
                last = None
                for j in range(NT):
                    last = e.matmul(self.PS[2 + c % 2][:, 0:512],
                                    self.WL[:, j * 128:(j + 1) * 128],
                                    self.C[:, j * 2050 + c * 512: j * 2050 + c * 512 + 512],
                                    start=(j == 0), stop=(j == NT - 1))
                return last
            p.op("tensor", omm,
                 reads=["WLa"] + [f"C.{j}" for j in range(NT)],
                 writes=[self.PK[2 + c % 2]], label=f"{tag}.omm{m}.{c}")
            p.op("vector", lambda e, m=m, c=c: e.scalar_tensor_tensor(
                self.Av(m, c * 512, 512), self.PS[2 + c % 2][:, 0:512],
                self.col("bo", m), self.Av(m, c * 512, 512), ALU.add, ALU.add),
                reads=[self.PK[2 + c % 2], "COLS.bo", f"A.{m}"], writes=[f"A.{m}"],
                label=f"{tag}.oe{m}.{c}")


MK.emit_rwkv = _mk_emit_rwkv


def build_rwkv_prog():
    nc = bass.Bass("TRN2", target_bir_lowering=False, debug=False)
    xT = nc.dram_tensor("xT", [H, T], F32, kind="ExternalInput").ap()
    wk = nc.dram_tensor("wk", [H, H], F32, kind="ExternalInput").ap()
    wv = nc.dram_tensor("wv", [H, H], F32, kind="ExternalInput").ap()
    wr = nc.dram_tensor("wr", [H, H], F32, kind="ExternalInput").ap()
    wo = nc.dram_tensor("wo", [H, H], F32, kind="ExternalInput").ap()
    ident = nc.dram_tensor("ident", [128, 128], F32, kind="ExternalInput").ap()
    cols = {}
    for nm in ("tmk", "tmv", "tmr", "tdec", "tfirst", "bo"):
        cols[nm] = nc.dram_tensor(f"c_{nm}", [128, 8], F32, kind="ExternalInput").ap()
    xout = nc.dram_tensor("xout", [H, T], F32, kind="ExternalOutput").ap()
    xspill = nc.dram_tensor("xspill", [H, T], F32, kind="Internal").ap()
    with ExitStack() as stack:
        mk = MK(nc, stack)
        mk.init_consts(ident)
        for nm in cols:
            mk.load_cols(cols[nm], nm)
        mk.load_x(xT)
        mk.emit_rwkv(wk, wv, wr, wo, xspill, "r")
        outs = mk.store_x(xout)
        with nc.Block() as block:
            mk.p.emit(block, end_fence=outs)
    return nc


def _mk_emit_attn(self, Wq, Wk, Wv, Wo, Wrq, Wrk, Wrv, Wg, pastT, brow, xspill, tag):
    p = self.p
    ALLC = [f"C.{i}" for i in range(NT)]

    def wload(w_hbm, blk, woff, wkey, tag2):
        p.dma(self.WF[:, 0:1024].rearrange("p (j f) -> p j f", f=128),
              w_hbm.rearrange("(j p) o -> p j o", p=128)[:, :, blk * 128:(blk + 1) * 128],
              writes=["WF"], label=f"{tag2}.wl")
        p.op("vector", lambda e, woff=woff: e.tensor_copy(
            self.WL[:, woff:woff + 1024], self.WF[:, 0:1024]),
            reads=["WF"], writes=[wkey], label=f"{tag2}.wc")

    def projT(dst_off, dst_keys, rhs_keys, bias_name, blk, tag2, act=None,
              woff=0, wkey="WLa", rhs_tiles=None, dst_buf=None):
        dst_buf = self.D if dst_buf is None else dst_buf
        if rhs_tiles is None:
            rhs_tiles = [self.Cv(j) for j in range(NT)]
        for c in range(4):
            def mm(e, c=c, rhs_tiles=rhs_tiles, woff=woff):
                last = None
                for j in range(NT):
                    last = e.matmul(self.PS[c % 2][:, 0:512],
                                    self.WL[:, woff + j * 128:woff + (j + 1) * 128],
                                    rhs_tiles[j][:, c * 512:(c + 1) * 512],
                                    start=(j == 0), stop=(j == NT - 1))
                return last
            p.op("tensor", mm, reads=[wkey] + rhs_keys,
                 writes=[self.PK[c % 2]], label=f"{tag2}.mm{c}")
            if act is None:
                p.op("vector", lambda e, c=c, dst_off=dst_off, blk=blk,
                     bias_name=bias_name, dst_buf=dst_buf: e.tensor_scalar(
                    dst_buf[:, dst_off + c * 512:dst_off + c * 512 + 512],
                    self.PS[c % 2][:, 0:512], self.col(bias_name, blk), None,
                    ALU.add),
                    reads=[self.PK[c % 2], f"COLS.{bias_name}"], writes=dst_keys,
                    label=f"{tag2}.ev{c}")
            else:
                p.op("scalar", lambda e, c=c, dst_off=dst_off, blk=blk,
                     bias_name=bias_name, dst_buf=dst_buf, act=act: e.activation(
                    dst_buf[:, dst_off + c * 512:dst_off + c * 512 + 512],
                    self.PS[c % 2][:, 0:512], act, bias=self.col(bias_name, blk)),
                    reads=[self.PK[c % 2], f"COLS.{bias_name}"], writes=dst_keys,
                    label=f"{tag2}.ev{c}")

    def head_norm_cell(src_buf, src_off, hl, cell, skey, tag2):
        sq = self.SC[:, R2:R2 + 2048]
        p.op("scalar", lambda e: e.activation(
            sq[0:64, :], src_buf[hl:hl + 64, src_off:src_off + 2048], AF.Square),
            reads=skey, writes=["SC.C"], label=f"{tag2}.sq")
        for c in range(4):
            p.op("tensor", lambda e, c=c: e.matmul(
                self.PS[5][0:1, 0:512], self.ONESC[0:64, :],
                sq[0:64, c * 512:(c + 1) * 512], start=True, stop=True),
                reads=["SC.C", "ONESC"], writes=[self.PK[5]], label=f"{tag2}.nm{c}")
            p.op("vector", lambda e, c=c, cell=cell: e.reduce_max(
                self.CCOL[0:1, 40 + c:41 + c], self.PS[5][0:1, 0:512], axis=AX.X),
                reads=[self.PK[5]], writes=["CCOL"], label=f"{tag2}.nr{c}")
        p.op("vector", lambda e: e.tensor_tensor(
            self.CCOL[0:1, 40:41], self.CCOL[0:1, 40:41], self.CCOL[0:1, 41:42], ALU.max),
            reads=["CCOL"], writes=["CCOL"], label=f"{tag2}.nx1")
        p.op("vector", lambda e: e.tensor_tensor(
            self.CCOL[0:1, 42:43], self.CCOL[0:1, 42:43], self.CCOL[0:1, 43:44], ALU.max),
            reads=["CCOL"], writes=["CCOL"], label=f"{tag2}.nx2")
        p.op("vector", lambda e, cell=cell: e.tensor_tensor(
            self.CCOL[0:1, cell:cell + 1], self.CCOL[0:1, 40:41],
            self.CCOL[0:1, 42:43], ALU.max),
            reads=["CCOL"], writes=["CCOL"], label=f"{tag2}.nx3")

    def bias_col_from_cells(cell_q, cell_k, tag2):
        p.op("vector", lambda e: e.tensor_tensor(
            self.CCOL[0:1, 44:45], self.CCOL[0:1, cell_q:cell_q + 1],
            self.CCOL[0:1, cell_k:cell_k + 1], ALU.mult),
            reads=["CCOL"], writes=["CCOL"], label=f"{tag2}.bm")
        p.op("scalar", lambda e: e.activation(
            self.CCOL[0:1, 44:45], self.CCOL[0:1, 44:45], AF.Sqrt),
            reads=["CCOL"], writes=["CCOL"], label=f"{tag2}.bs")
        p.op("vector", lambda e: e.tensor_scalar_mul(
            self.CCOL[0:1, 44:45], self.CCOL[0:1, 44:45], -0.125),
            reads=["CCOL"], writes=["CCOL"], label=f"{tag2}.bn")
        p.op("tensor", lambda e: e.matmul(
            self.PS[5][:, 0:1], self.ONESRF[:], self.CCOL[0:1, 44:45],
            start=True, stop=True),
            reads=["CCOL", "ONESRF"], writes=[self.PK[5]], label=f"{tag2}.br")
        p.op("vector", lambda e: e.tensor_copy(self.CCOL[:, 46:47], self.PS[5][:, 0:1]),
             reads=[self.PK[5]], writes=["CCOL"], label=f"{tag2}.be")

    def flash(q_buf, q_off, q_keys, k_buf, k_off, k_keys, v_buf, v_off, v_keys,
              hl, dst_region, dst_keys, scale05, accumulate, tag2):
        for sc in range(4):
            for tau in range(16):
                pb = (sc * 16 + tau) % 2
                p.op("tensor", lambda e, sc=sc, tau=tau, pb=pb: e.matmul(
                    self.PS[pb][:, 0:512],
                    k_buf[hl:hl + 64, k_off + tau * 128:k_off + (tau + 1) * 128],
                    q_buf[hl:hl + 64, q_off + sc * 512:q_off + sc * 512 + 512],
                    start=True, stop=True),
                    reads=k_keys + q_keys, writes=[self.PK[pb]],
                    label=f"{tag2}.sc{sc}.{tau}")
                et_off, et_key = (R2, "SC.C") if tau % 2 == 0 else (R3, "SC.D")
                p.op("scalar", lambda e, pb=pb, et_off=et_off: e.activation(
                    self.SC[:, et_off:et_off + 512], self.PS[pb][:, 0:512],
                    AF.Exp, bias=self.CCOL[:, 46:47], scale=0.125),
                    reads=[self.PK[pb], "CCOL"], writes=[et_key],
                    label=f"{tag2}.ex{sc}.{tau}")
                def ctx(e, tau=tau, et_off=et_off):
                    e.matmul(self.PS[4][0:64, 0:512],
                             v_buf[:, v_off + tau * 128 + hl:v_off + tau * 128 + hl + 64],
                             self.SC[:, et_off:et_off + 512],
                             start=(tau == 0), stop=(tau == 15))
                    return e.matmul(self.PS[5][64:65, 0:512], self.ONESC[:],
                                    self.SC[:, et_off:et_off + 512],
                                    start=(tau == 0), stop=(tau == 15))
                p.op("tensor", ctx, reads=v_keys + [et_key, "ONESC"],
                     writes=[self.PK[4], self.PK[5]], label=f"{tag2}.cx{sc}.{tau}")
            p.op("vector", lambda e: e.reciprocal(self.S1[0:1, 0:512],
                                                  self.PS[5][64:65, 0:512]),
                 reads=[self.PK[5]], writes=["S1"], label=f"{tag2}.rc{sc}")
            if scale05:
                p.op("vector", lambda e: e.tensor_scalar_mul(
                    self.S1[0:1, 0:512], self.S1[0:1, 0:512], 0.5),
                    reads=["S1"], writes=["S1"], label=f"{tag2}.h{sc}")
            p.op("tensor", lambda e: e.matmul(
                self.PS[5][hl:hl + 64, 0:512], self.ONESRF[0:1, 0:64],
                self.S1[0:1, 0:512], start=True, stop=True),
                reads=["S1", "ONESRF"], writes=[self.PK[5]], label=f"{tag2}.rp{sc}")
            p.op("vector", lambda e: e.tensor_copy(
                self.SC[hl:hl + 64, 5120:5120 + 512], self.PS[5][hl:hl + 64, 0:512]),
                reads=[self.PK[5]], writes=["SC.C"], label=f"{tag2}.rpe{sc}")
            if not accumulate:
                p.op("vector", lambda e, sc=sc: e.tensor_tensor(
                    self.A[hl:hl + 64, dst_region + sc * 512:dst_region + sc * 512 + 512],
                    self.PS[4][0:64, 0:512], self.SC[hl:hl + 64, 5120:5120 + 512], ALU.mult),
                    reads=[self.PK[4], "SC.C"], writes=dst_keys,
                    label=f"{tag2}.no{sc}")
            else:
                p.op("vector", lambda e, sc=sc: e.tensor_tensor(
                    self.SC[hl:hl + 64, R2 + 512:R2 + 1024],
                    self.PS[4][0:64, 0:512], self.SC[hl:hl + 64, 5120:5120 + 512], ALU.mult),
                    reads=[self.PK[4], "SC.C"], writes=["SC.C"],
                    label=f"{tag2}.na{sc}")
                p.op("vector", lambda e, sc=sc: e.tensor_tensor(
                    self.A[hl:hl + 64, dst_region + sc * 512:dst_region + sc * 512 + 512],
                    self.A[hl:hl + 64, dst_region + sc * 512:dst_region + sc * 512 + 512],
                    self.SC[hl:hl + 64, R2 + 512:R2 + 1024], ALU.add),
                    reads=["SC.C"] + dst_keys, writes=dst_keys,
                    label=f"{tag2}.ac{sc}")

    self.emit_ln(lambda i: self.Av(i), lambda i: f"A.{i}",
                 lambda i: self.Cv(i), lambda i: [f"C.{i}"], f"{tag}.ln")
    for i in range(NT):
        p.dma(xspill[i * 128:(i + 1) * 128, :], self.Av(i),
              reads=[f"A.{i}"], writes=["XSP"], label=f"{tag}.sp{i}")
    p.dma(self.ROWS[0:1, 0:2048], brow, writes=["ROWS"], label=f"{tag}.brow")

    for g in range(NT):
        gk = [f"G{g}"]
        wload(Wq, g, 0, "WLa", f"{tag}.q{g}")
        projT(0, ["D.0", "D.1"], ALLC, "bq", g, f"{tag}.qp{g}")
        wload(Wk, g, 0, "WLa", f"{tag}.k{g}")
        projT(2048, ["D.2", "D.3"], ALLC, "bk", g, f"{tag}.kp{g}")
        wload(Wv, g, 0, "WLa", f"{tag}.v{g}")
        p.op("vector", lambda e, g=g: e.tensor_copy(
            self.SC[0:1, R2:R2 + 128], self.ROWS[0:1, g * 128:(g + 1) * 128]),
            reads=["ROWS"], writes=["SC.C"], label=f"{tag}.bv{g}")
        for tau in range(16):
            def vmm(e, tau=tau):
                for j in range(NT):
                    e.matmul(self.PS[tau % 2][:, 0:128],
                             self.Cv(j)[:, tau * 128:(tau + 1) * 128],
                             self.WL[:, j * 128:(j + 1) * 128],
                             start=(j == 0), stop=False)
                return e.matmul(self.PS[tau % 2][:, 0:128], self.ONESR[:],
                                self.SC[0:1, R2:R2 + 128], start=False, stop=True)
            p.op("tensor", vmm, reads=["WLa", "SC.C", "ONESR"] + ALLC,
                 writes=[self.PK[tau % 2]], label=f"{tag}.vm{g}.{tau}")
            p.op("vector", lambda e, tau=tau: e.tensor_copy(
                self.D[:, 4096 + tau * 128:4096 + (tau + 1) * 128],
                self.PS[tau % 2][:, 0:128]),
                reads=[self.PK[tau % 2]], writes=["D.4", "D.5"],
                label=f"{tag}.ve{g}.{tau}")
        for hh in range(2):
            hl = hh * 64
            head_norm_cell(self.D, 0, hl, 36, ["D.0", "D.1"], f"{tag}.nq{g}.{hh}")
            head_norm_cell(self.D, 2048, hl, 37, ["D.2", "D.3"], f"{tag}.nk{g}.{hh}")
            bias_col_from_cells(36, 37, f"{tag}.bc{g}.{hh}")
            flash(self.D, 0, ["D.0", "D.1"], self.D, 2048, ["D.2", "D.3"],
                  self.D, 4096, ["D.4", "D.5"], hl,
                  g * 2048, [f"A.{g}"], False, False, f"{tag}.fl{g}.{hh}")
    for i in range(NT):
        p.op("vector", lambda e, i=i: e.tensor_copy(self.Cv(i), self.Av(i)),
             reads=[f"A.{i}"], writes=[f"C.{i}"], label=f"{tag}.cc{i}")
    for m in range(NT):
        wload(Wo, m, 0, "WLa", f"{tag}.o{m}")
        projT(m * 2048, [f"D.{2 * m}", f"D.{2 * m + 1}"], ALLC, "bo2", m, f"{tag}.op{m}")
    ALLD = [f"D.{r}" for r in range(16)]
    for m in range(NT):
        wload(Wrq, m, 0, "WLa", f"{tag}.rq{m}")
        projT(m * 2048, [f"E.{2 * m}", f"E.{2 * m + 1}"], ALLD, "brq", m,
              f"{tag}.rqp{m}", rhs_tiles=[self.D[:, j * 2048:(j + 1) * 2048] for j in range(NT)],
              dst_buf=self.E)
    ALLE = [f"E.{r}" for r in range(16)]
    for h in range(NH):
        g, hl = h // 2, (h % 2) * 64
        head_norm_cell(self.E, g * 2048, hl, 16 + h, ALLE, f"{tag}.rqn{h}")
    for pp in range(2):
        for i in range(NT):
            p.dma(self.S1[:, 0:2048], pastT[pp, i * 128:(i + 1) * 128, :],
                  reads=["S1"], writes=["S1"], label=f"{tag}.pl{pp}.{i}")
            p.op("vector", lambda e, i=i: e.tensor_copy(self.Cv(i), self.S1[:, 0:2048]),
                 reads=["S1"], writes=[f"C.{i}"], label=f"{tag}.pc{pp}.{i}")
        for g in range(NT):
            wload(Wrk, g, 0, "WLa", f"{tag}.rk{pp}.{g}")
            for c in range(4):
                def rkmm(e, c=c):
                    last = None
                    for j in range(NT):
                        last = e.matmul(self.PS[c % 2][:, 0:512],
                                        self.WL[:, j * 128:(j + 1) * 128],
                                        self.Cv(j)[:, c * 512:(c + 1) * 512],
                                        start=(j == 0), stop=(j == NT - 1))
                    return last
                p.op("tensor", rkmm, reads=["WLa"] + ALLC,
                     writes=[self.PK[c % 2]], label=f"{tag}.rkm{pp}.{g}.{c}")
                p.op("vector", lambda e, c=c, g=g: e.tensor_scalar(
                    self.SC[:, c * 512:c * 512 + 512], self.PS[c % 2][:, 0:512],
                    self.col("brk", g), None, ALU.add),
                    reads=[self.PK[c % 2], "COLS.brk"], writes=["SC.A"],
                    label=f"{tag}.rke{pp}.{g}.{c}")
            wload(Wrv, g, 0, "WLa", f"{tag}.rv{pp}.{g}")
            p.op("vector", lambda e, g=g: e.tensor_copy(
                self.SC[0:1, R2:R2 + 128], self.ROWS[0:1, 1024 + g * 128:1024 + (g + 1) * 128]),
                reads=["ROWS"], writes=["SC.C"], label=f"{tag}.brv{pp}.{g}")
            for tau in range(16):
                def rvmm(e, tau=tau):
                    for j in range(NT):
                        e.matmul(self.PS[tau % 2][:, 0:128],
                                 self.Cv(j)[:, tau * 128:(tau + 1) * 128],
                                 self.WL[:, j * 128:(j + 1) * 128],
                                 start=(j == 0), stop=False)
                    return e.matmul(self.PS[tau % 2][:, 0:128], self.ONESR[:],
                                    self.SC[0:1, R2:R2 + 128], start=False, stop=True)
                p.op("tensor", rvmm, reads=["WLa", "SC.C", "ONESR"] + ALLC,
                     writes=[self.PK[tau % 2]], label=f"{tag}.rvm{pp}.{g}.{tau}")
                p.op("vector", lambda e, tau=tau: e.tensor_copy(
                    self.SC[:, R1 + tau * 128:R1 + (tau + 1) * 128],
                    self.PS[tau % 2][:, 0:128]),
                    reads=[self.PK[tau % 2]], writes=["SC.B"],
                    label=f"{tag}.rve{pp}.{g}.{tau}")
            for hh in range(2):
                hl = hh * 64
                h = g * 2 + hh
                head_norm_cell(self.SC, 0, hl, 37, ["SC.A"], f"{tag}.rkn{pp}.{g}.{hh}")
                bias_col_from_cells(16 + h, 37, f"{tag}.rbc{pp}.{g}.{hh}")
                flash(self.E, g * 2048, ALLE, self.SC, 0, ["SC.A"],
                      self.SC, R1, ["SC.B"], hl,
                      g * 2048, [f"A.{g}"], True, pp == 1, f"{tag}.rf{pp}.{g}.{hh}")
    for i in range(NT):
        p.op("vector", lambda e, i=i: e.tensor_copy(self.Cv(i), self.Av(i)),
             reads=[f"A.{i}"], writes=[f"C.{i}"], label=f"{tag}.rc{i}")
    for m in range(NT):
        p.dma(self.WF[:, 0:1024].rearrange("p (j f) -> p j f", f=128),
              Wg.rearrange("(j p) o -> p j o", p=128)[:, 0:8, m * 128:(m + 1) * 128],
              writes=["WF"], label=f"{tag}.g1{m}")
        p.op("vector", lambda e: e.tensor_copy(self.WL[:, 0:1024], self.WF[:, 0:1024]),
             reads=["WF"], writes=["WLa"], label=f"{tag}.g1c{m}")
        p.dma(self.WF[:, 0:1024].rearrange("p (j f) -> p j f", f=128),
              Wg.rearrange("(j p) o -> p j o", p=128)[:, 8:16, m * 128:(m + 1) * 128],
              writes=["WF"], label=f"{tag}.g2{m}")
        p.op("vector", lambda e: e.tensor_copy(self.SC[:, 2048:3072], self.WF[:, 0:1024]),
             reads=["WF"], writes=["SC.B"], label=f"{tag}.g2c{m}")
        for c in range(4):
            def gmm(e, c=c):
                last = None
                for j in range(16):
                    rhs = (self.D[:, (j) * 2048 + c * 512:j * 2048 + c * 512 + 512]
                           if j < 8 else
                           self.Cv(j - 8)[:, c * 512:(c + 1) * 512])
                    lhsT = (self.WL[:, j * 128:(j + 1) * 128] if j < 8
                            else self.SC[:, 2048 + (j - 8) * 128:2048 + (j - 7) * 128])
                    last = e.matmul(self.PS[2 + c % 2][:, 0:512],
                                    lhsT, rhs, start=(j == 0), stop=(j == 15))
                return last
            p.op("tensor", gmm, reads=["WLa", "SC.B"] + ALLD + ALLC,
                 writes=[self.PK[2 + c % 2]], label=f"{tag}.gm{m}.{c}")
            p.op("scalar", lambda e, c=c, m=m: e.activation(
                self.E[:, m * 2048 + c * 512:m * 2048 + c * 512 + 512],
                self.PS[2 + c % 2][:, 0:512], AF.Sigmoid, bias=self.col("bg", m)),
                reads=[self.PK[2 + c % 2], "COLS.bg"],
                writes=[f"E.{2 * m}", f"E.{2 * m + 1}"], label=f"{tag}.ge{m}.{c}")
    for m in range(NT):
        p.dma(self.Av(m), xspill[m * 128:(m + 1) * 128, :],
              reads=[f"A.{m}", "XSP"], writes=[f"A.{m}"], label=f"{tag}.rl{m}")
    for m in range(NT):
        for c in range(4):
            sl = lambda buf: buf[:, m * 2048 + c * 512:m * 2048 + c * 512 + 512]
            p.op("vector", lambda e, m=m, c=c: e.tensor_tensor(
                self.SC[:, 0:512], self.Cv(m)[:, c * 512:(c + 1) * 512],
                self.D[:, m * 2048 + c * 512:m * 2048 + c * 512 + 512], ALU.subtract),
                reads=[f"C.{m}", f"D.{2 * m}", f"D.{2 * m + 1}"], writes=["SC.A"],
                label=f"{tag}.cb1{m}.{c}")
            p.op("vector", lambda e, m=m, c=c: e.tensor_tensor(
                self.SC[:, 512:1024],
                self.E[:, m * 2048 + c * 512:m * 2048 + c * 512 + 512],
                self.SC[:, 0:512], ALU.mult),
                reads=[f"E.{2 * m}", f"E.{2 * m + 1}", "SC.A"], writes=["SC.A"],
                label=f"{tag}.cb2{m}.{c}")
            p.op("vector", lambda e, m=m, c=c: e.tensor_tensor(
                self.SC[:, 512:1024],
                self.D[:, m * 2048 + c * 512:m * 2048 + c * 512 + 512],
                self.SC[:, 512:1024], ALU.add),
                reads=[f"D.{2 * m}", f"D.{2 * m + 1}", "SC.A"], writes=["SC.A"],
                label=f"{tag}.cb3{m}.{c}")
            p.op("vector", lambda e, m=m, c=c: e.tensor_tensor(
                self.Av(m, c * 512, 512), self.Av(m, c * 512, 512),
                self.SC[:, 512:1024], ALU.add),
                reads=["SC.A", f"A.{m}"], writes=[f"A.{m}"],
                label=f"{tag}.cb4{m}.{c}")


MK.emit_attn = _mk_emit_attn


def build_attn_prog():
    nc = bass.Bass("TRN2", target_bir_lowering=False, debug=False)
    xT = nc.dram_tensor("xT", [H, T], F32, kind="ExternalInput").ap()
    ws = {}
    for nm in ("Wq", "Wk", "Wv", "Wo", "Wrq", "Wrk", "Wrv"):
        ws[nm] = nc.dram_tensor(nm, [H, H], F32, kind="ExternalInput").ap()
    Wg = nc.dram_tensor("Wg", [2 * H, H], F32, kind="ExternalInput").ap()
    pastT = nc.dram_tensor("pastT", [2, H, T], F32, kind="ExternalInput").ap()
    brow = nc.dram_tensor("brow", [1, 2048], F32, kind="ExternalInput").ap()
    ident = nc.dram_tensor("ident", [128, 128], F32, kind="ExternalInput").ap()
    cols = {}
    for nm in ("bq", "bk", "bo2", "brq", "brk", "bg"):
        cols[nm] = nc.dram_tensor(f"c_{nm}", [128, 8], F32, kind="ExternalInput").ap()
    xout = nc.dram_tensor("xout", [H, T], F32, kind="ExternalOutput").ap()
    xspill = nc.dram_tensor("xspill", [H, T], F32, kind="Internal").ap()
    with ExitStack() as stack:
        mk = MK(nc, stack)
        mk.init_consts(ident)
        for nm in cols:
            mk.load_cols(cols[nm], nm)
        mk.load_x(xT)
        mk.emit_attn(ws["Wq"], ws["Wk"], ws["Wv"], ws["Wo"], ws["Wrq"], ws["Wrk"],
                     ws["Wrv"], Wg, pastT, brow, xspill, "a")
        outs = mk.store_x(xout)
        with nc.Block() as block:
            mk.p.emit(block, end_fence=outs)
    return nc


_PROG_CACHE = {}


def _prog(name):
    if name not in _PROG_CACHE:
        _PROG_CACHE[name] = {"ffn": build_ffn_prog, "rwkv": build_rwkv_prog,
                             "attn": build_attn_prog}[name]()
    return _PROG_CACHE[name]


def kernel(**inputs):
    f = np.float32
    x = np.asarray(inputs["L0_attn_x"], dtype=f)
    ident = _ident_np()
    tr = lambda a: np.ascontiguousarray(np.asarray(a, dtype=f).T)

    ffnw = {}
    for L in ("L0", "L1"):
        ffnw[L] = dict(wg=tr(inputs[f"{L}_ffn_w_gate"]),
                       wu=tr(inputs[f"{L}_ffn_w_up"]),
                       wd=tr(inputs[f"{L}_ffn_w_down"]))
    rwkvw = dict(wk=tr(inputs["L0_attn_Wk"]), wv=tr(inputs["L0_attn_Wv"]),
                 wr=tr(inputs["L0_attn_Wr"]), wo=tr(inputs["L0_attn_Wo"]),
                 c_tmk=_colify(np.asarray(inputs["L0_attn_time_mix_key"], f)[0, 0]),
                 c_tmv=_colify(np.asarray(inputs["L0_attn_time_mix_value"], f)[0, 0]),
                 c_tmr=_colify(np.asarray(inputs["L0_attn_time_mix_receptance"], f)[0, 0]),
                 c_tdec=_colify(np.asarray(inputs["L0_attn_time_decay"], f)),
                 c_tfirst=_colify(np.asarray(inputs["L0_attn_time_first"], f)),
                 c_bo=_colify(np.asarray(inputs["L0_attn_bo"], f)))
    attnw = dict(
        Wq=np.asarray(inputs["L1_attn_Wq"], f), Wk=np.asarray(inputs["L1_attn_Wk"], f),
        Wv=np.asarray(inputs["L1_attn_Wv"], f), Wo=np.asarray(inputs["L1_attn_Wo"], f),
        Wrq=np.asarray(inputs["L1_attn_Wrq"], f), Wrk=np.asarray(inputs["L1_attn_Wrk"], f),
        Wrv=np.asarray(inputs["L1_attn_Wrv"], f), Wg=np.asarray(inputs["L1_attn_Wg"], f),
        pastT=np.ascontiguousarray(
            np.asarray(inputs["L1_attn_past_states"], f)[:, 0].transpose(0, 2, 1)),
        brow=np.concatenate([np.asarray(inputs["L1_attn_bv"], f),
                             np.asarray(inputs["L1_attn_brv"], f)])[None, :],
        c_bq=_colify(np.asarray(inputs["L1_attn_bq"], f)),
        c_bk=_colify(np.asarray(inputs["L1_attn_bk"], f)),
        c_bo=_colify(np.asarray(inputs["L1_attn_bo"], f)),
        c_brq=_colify(np.asarray(inputs["L1_attn_brq"], f)),
        c_brk=_colify(np.asarray(inputs["L1_attn_brk"], f)),
        c_bg=_colify(np.asarray(inputs["L1_attn_bg"], f)))

    xT = [tr(x[b]) for b in range(N_CORES)]

    def launch(prog_name, extra):
        nc = _prog(prog_name)
        maps = [dict(xT=xT[b], ident=ident, **extra) for b in range(N_CORES)]
        res = run_bass_kernel_spmd(nc, maps, core_ids=list(range(N_CORES)))
        for b in range(N_CORES):
            xT[b] = np.ascontiguousarray(np.asarray(res.results[b]["xout"]))

    launch("ffn", ffnw["L0"])
    launch("rwkv", rwkvw)
    launch("ffn", ffnw["L0"])
    launch("ffn", ffnw["L1"])
    launch("attn", attnw)
    launch("ffn", ffnw["L1"])

    return np.stack([xT[b].T for b in range(N_CORES)]).astype(np.float32)


kernel_multilaunch = kernel


def kernel(**inputs):
    return kernel_fused(**inputs)


def build_fused_prog():
    nc = bass.Bass("TRN2", target_bir_lowering=False, debug=False)
    xT = nc.dram_tensor("xT", [H, T], F32, kind="ExternalInput").ap()
    ident = nc.dram_tensor("ident", [128, 128], F32, kind="ExternalInput").ap()
    ffn_w = {}
    for L in ("l0", "l1"):
        ffn_w[L] = dict(
            wg=nc.dram_tensor(f"{L}_wg", [H, INNER], F32, kind="ExternalInput").ap(),
            wu=nc.dram_tensor(f"{L}_wu", [H, INNER], F32, kind="ExternalInput").ap(),
            wd=nc.dram_tensor(f"{L}_wd", [INNER, H], F32, kind="ExternalInput").ap(),
            wdq=nc.dram_tensor(f"{L}_wdq", [INNER, H], BF16, kind="Internal").ap())
    rw = {nm: nc.dram_tensor(f"r_{nm}", [H, H], F32, kind="ExternalInput").ap()
          for nm in ("wk", "wv", "wr", "wo")}
    aw = {nm: nc.dram_tensor(f"a_{nm}", [H, H], F32, kind="ExternalInput").ap()
          for nm in ("Wq", "Wk", "Wv", "Wo", "Wrq", "Wrk", "Wrv")}
    Wg = nc.dram_tensor("a_Wg", [2 * H, H], F32, kind="ExternalInput").ap()
    pastT = nc.dram_tensor("pastT", [2, H, T], F32, kind="ExternalInput").ap()
    brow = nc.dram_tensor("brow", [1, 2048], F32, kind="ExternalInput").ap()
    cols = {}
    for nm in ("tmk", "tmv", "tmr", "tdec", "tfirst", "bo",
               "bq", "bk", "bo2", "brq", "brk", "bg"):
        cols[nm] = nc.dram_tensor(f"c_{nm}", [128, 8], F32, kind="ExternalInput").ap()
    xout = nc.dram_tensor("xout", [H, T], F32, kind="ExternalOutput").ap()
    xspill = nc.dram_tensor("xspill", [H, T], F32, kind="Internal").ap()
    with ExitStack() as stack:
        mk = MK(nc, stack)
        mk.init_consts(ident)
        for nm in cols:
            mk.load_cols(cols[nm], nm)
        mk.load_x(xT)
        mk.emit_ffn(ffn_w["l0"]["wg"], ffn_w["l0"]["wu"], ffn_w["l0"]["wd"],
                    ffn_w["l0"]["wdq"], "f0a")
        mk.emit_rwkv(rw["wk"], rw["wv"], rw["wr"], rw["wo"], xspill, "rk")
        mk.emit_ffn(ffn_w["l0"]["wg"], ffn_w["l0"]["wu"], ffn_w["l0"]["wd"],
                    ffn_w["l0"]["wdq"], "f0b")
        mk.emit_ffn(ffn_w["l1"]["wg"], ffn_w["l1"]["wu"], ffn_w["l1"]["wd"],
                    ffn_w["l1"]["wdq"], "f1a")
        mk.emit_attn(aw["Wq"], aw["Wk"], aw["Wv"], aw["Wo"], aw["Wrq"], aw["Wrk"],
                     aw["Wrv"], Wg, pastT, brow, xspill, "at")
        mk.emit_ffn(ffn_w["l1"]["wg"], ffn_w["l1"]["wu"], ffn_w["l1"]["wd"],
                    ffn_w["l1"]["wdq"], "f1b")
        outs = mk.store_x(xout)
        with nc.Block() as block:
            mk.p.emit(block, end_fence=outs)
    return nc


def kernel_fused(**inputs):
    f = np.float32
    x = np.asarray(inputs["L0_attn_x"], dtype=f)
    tr = lambda a: np.ascontiguousarray(np.asarray(a, dtype=f).T)
    base = dict(
        ident=_ident_np(),
        l0_wg=tr(inputs["L0_ffn_w_gate"]), l0_wu=tr(inputs["L0_ffn_w_up"]),
        l0_wd=tr(inputs["L0_ffn_w_down"]),
        l1_wg=tr(inputs["L1_ffn_w_gate"]), l1_wu=tr(inputs["L1_ffn_w_up"]),
        l1_wd=tr(inputs["L1_ffn_w_down"]),
        r_wk=tr(inputs["L0_attn_Wk"]), r_wv=tr(inputs["L0_attn_Wv"]),
        r_wr=tr(inputs["L0_attn_Wr"]), r_wo=tr(inputs["L0_attn_Wo"]),
        a_Wq=np.asarray(inputs["L1_attn_Wq"], f), a_Wk=np.asarray(inputs["L1_attn_Wk"], f),
        a_Wv=np.asarray(inputs["L1_attn_Wv"], f), a_Wo=np.asarray(inputs["L1_attn_Wo"], f),
        a_Wrq=np.asarray(inputs["L1_attn_Wrq"], f),
        a_Wrk=np.asarray(inputs["L1_attn_Wrk"], f),
        a_Wrv=np.asarray(inputs["L1_attn_Wrv"], f),
        a_Wg=np.asarray(inputs["L1_attn_Wg"], f),
        pastT=np.ascontiguousarray(
            np.asarray(inputs["L1_attn_past_states"], f)[:, 0].transpose(0, 2, 1)),
        brow=np.concatenate([np.asarray(inputs["L1_attn_bv"], f),
                             np.asarray(inputs["L1_attn_brv"], f)])[None, :],
        c_tmk=_colify(np.asarray(inputs["L0_attn_time_mix_key"], f)[0, 0]),
        c_tmv=_colify(np.asarray(inputs["L0_attn_time_mix_value"], f)[0, 0]),
        c_tmr=_colify(np.asarray(inputs["L0_attn_time_mix_receptance"], f)[0, 0]),
        c_tdec=_colify(np.asarray(inputs["L0_attn_time_decay"], f)),
        c_tfirst=_colify(np.asarray(inputs["L0_attn_time_first"], f)),
        c_bo=_colify(np.asarray(inputs["L0_attn_bo"], f)),
        c_bq=_colify(np.asarray(inputs["L1_attn_bq"], f)),
        c_bk=_colify(np.asarray(inputs["L1_attn_bk"], f)),
        c_bo2=_colify(np.asarray(inputs["L1_attn_bo"], f)),
        c_brq=_colify(np.asarray(inputs["L1_attn_brq"], f)),
        c_brk=_colify(np.asarray(inputs["L1_attn_brk"], f)),
        c_bg=_colify(np.asarray(inputs["L1_attn_bg"], f)),
    )
    if "fused" not in _PROG_CACHE:
        _PROG_CACHE["fused"] = build_fused_prog()
    nc = _PROG_CACHE["fused"]
    maps = [dict(xT=np.ascontiguousarray(x[b].T), **base) for b in range(N_CORES)]
    res = run_bass_kernel_spmd(nc, maps, core_ids=list(range(N_CORES)))
    return np.stack([np.asarray(res.results[b]["xout"]).T
                     for b in range(N_CORES)]).astype(np.float32)


# version 15 (final)
# speedup vs baseline: 1.2228x; 1.0146x over previous
import numpy as np
import ml_dtypes
import concourse.bass as bass
import concourse.mybir as mybir
from concourse.bass_utils import run_bass_kernel_spmd
from contextlib import ExitStack

F32 = mybir.dt.float32
BF16 = mybir.dt.bfloat16
I32 = mybir.dt.int32
I16 = mybir.dt.int16
AF = mybir.ActivationFunctionType
ALU = mybir.AluOpType
AX = mybir.AxisListType

N_CORES = 8
H = 1024
T = 2048
NT = 8
INNER = 4096
NI = 32
NH = 16
DH = 64
EPS = 1e-5
LADDER_K = 8

ENGINES = ("tensor", "vector", "scalar", "gpsimd", "sync")

R0, R1, R2, R3, RS = 0, 2048, 4096, 5632, 5632


class Op:
    __slots__ = ("idx", "engine", "fn", "reads", "writes", "deps", "is_dma",
                 "dma_sem", "dma_val", "pre_wait", "needs_inc", "eng_seq", "label")

    def __init__(self, idx, engine, fn, reads, writes, is_dma, label):
        self.idx = idx
        self.engine = engine
        self.fn = fn
        self.reads = reads
        self.writes = writes
        self.deps = []
        self.is_dma = is_dma
        self.dma_sem = None
        self.dma_val = 0
        self.pre_wait = None
        self.needs_inc = False
        self.eng_seq = 0
        self.label = label


class Prog:
    N_DMA_SEMS = 24

    def __init__(self, nc, stack):
        self.nc = nc
        self.stack = stack
        self.ops = []
        self.state = {}
        self.sems = {e: stack.enter_context(nc.semaphore(f"pg_{e}")) for e in ENGINES}
        self.dma_pool = [stack.enter_context(nc.semaphore(f"dma{i}"))
                         for i in range(self.N_DMA_SEMS)]
        self.dma_counts = [0] * self.N_DMA_SEMS
        self.dma_i = 0
        self.swdge_count = 0

    def sbuf(self, name, shape, dtype):
        return self.stack.enter_context(self.nc.sbuf_tensor(name, list(shape), dtype))

    def psum(self, name, shape, dtype=F32):
        return self.stack.enter_context(self.nc.psum_tensor(name, list(shape), dtype))

    def _dep_update(self, op):
        deps = {}
        for k in op.reads:
            st = self.state.get(k)
            if st is not None and st[0] is not None:
                deps[st[0].idx] = st[0]
        for k in op.writes:
            st = self.state.get(k)
            if st is not None:
                if st[0] is not None:
                    deps[st[0].idx] = st[0]
                for r in st[1]:
                    deps[r.idx] = r
        deps.pop(op.idx, None)
        op.deps = list(deps.values())
        for k in op.reads:
            self.state.setdefault(k, [None, []])[1].append(op)
        for k in op.writes:
            self.state[k] = [op, []]

    def op(self, engine, fn, reads=(), writes=(), label=""):
        o = Op(len(self.ops), engine, fn, tuple(reads), tuple(writes), False, label)
        self._dep_update(o)
        self.ops.append(o)
        return o

    def dma(self, out_ap, in_ap, reads=(), writes=(), queue="sync", label="", **kw):
        if queue == "gpsimd":
            sem = self.stack.enter_context(self.nc.semaphore(f"sw{self.swdge_count}"))
            self.swdge_count += 1
            o = Op(len(self.ops), "gpsimd",
                   lambda e, _o=out_ap, _i=in_ap, _kw=kw: e.dma_start(_o, _i, **_kw),
                   tuple(reads), tuple(writes), True, label)
            o.dma_sem, o.dma_val = sem, 16
        else:
            i = self.dma_i % self.N_DMA_SEMS
            self.dma_i += 1
            self.dma_counts[i] += 1
            o = Op(len(self.ops), "sync",
                   lambda e, _o=out_ap, _i=in_ap, _kw=kw: e.dma_start(_o, _i, **_kw),
                   tuple(reads), tuple(writes), True, label)
            o.dma_sem, o.dma_val = self.dma_pool[i], 16 * self.dma_counts[i]
            if self.dma_counts[i] > 1:
                o.pre_wait = (self.dma_pool[i], 16 * (self.dma_counts[i] - 1))
        self._dep_update(o)
        self.ops.append(o)
        return o

    def emit(self, block, end_fence=()):
        for o in self.ops:
            for d in o.deps:
                if not d.is_dma:
                    d.needs_inc = True
        counts = {e: 0 for e in ENGINES}
        for o in self.ops:
            if not o.is_dma:
                if o.needs_inc:
                    counts[o.engine] += 1
                o.eng_seq = counts[o.engine]
        per_engine = {e: [] for e in ENGINES}
        for o in self.ops:
            per_engine[o.engine].append(o)
        fence_list = list(end_fence)

        def make_body(ename, ops_here):
            def body(eng):
                hw = {e: 0 for e in ENGINES}
                hwd = {}
                for o in ops_here:
                    if o.pre_wait is not None:
                        s, v = o.pre_wait
                        if hwd.get(id(s), 0) < v:
                            eng.wait_ge(s, v)
                            hwd[id(s)] = v
                    for d in o.deps:
                        if d.is_dma:
                            if hwd.get(id(d.dma_sem), 0) < d.dma_val:
                                eng.wait_ge(d.dma_sem, d.dma_val)
                                hwd[id(d.dma_sem)] = d.dma_val
                        else:
                            if d.eng_seq > 0 and hw[d.engine] < d.eng_seq:
                                eng.wait_ge(self.sems[d.engine], d.eng_seq)
                                hw[d.engine] = d.eng_seq
                    inst = o.fn(eng)
                    if o.is_dma:
                        inst.then_inc(o.dma_sem, 16)
                    elif o.needs_inc:
                        inst.then_inc(self.sems[o.engine], 1)
                if ename == "sync":
                    for d in fence_list:
                        if hwd.get(id(d.dma_sem), 0) < d.dma_val:
                            eng.wait_ge(d.dma_sem, d.dma_val)
                            hwd[id(d.dma_sem)] = d.dma_val
            return body

        for ename in ENGINES:
            if per_engine[ename] or (ename == "sync" and fence_list):
                getattr(block, ename)(make_body(ename, per_engine[ename]))



def _kl(x):
    return list(x) if isinstance(x, (list, tuple)) else [x]

class MK:
    def __init__(self, nc, stack):
        self.nc = nc
        self.p = Prog(nc, stack)
        p = self.p
        self.A = p.sbuf("A", [128, 16384], F32)
        self.C = p.sbuf("C", [128, 16400], BF16)
        self.D = p.sbuf("Db", [128, 16384], BF16)
        self.E = p.sbuf("Eb", [128, 16384], BF16)
        self.S1 = p.sbuf("S1", [128, 2048], F32)
        self.SC = p.sbuf("SCb", [128, 6144], BF16)
        self.QI = p.sbuf("QI", [128, 512], mybir.dt.int8)
        self.Q2B = p.sbuf("Q2B", [128, 512], mybir.dt.int8)
        self.WL = p.sbuf("WL", [128, 1024], BF16)
        self.WF = p.sbuf("WF", [128, 1024], F32)
        self.ROWS = p.sbuf("ROWS", [1, 4096], F32)
        self.QROWF = p.sbuf("QROWF", [1, 512], F32)
        self.COLS = p.sbuf("COLS", [128, 96], F32)
        self.CCOL = p.sbuf("CCOL", [128, 96], F32)
        self.ONESC = p.sbuf("ONESC", [128, 1], BF16)
        self.ONESCF = p.sbuf("ONESCF", [128, 1], F32)
        self.ONESR = p.sbuf("ONESR", [1, 128], BF16)
        self.ONESRF = p.sbuf("ONESRF", [1, 128], F32)
        self.ONES127 = p.sbuf("ONES127", [1, 128], F32)
        self.EPSR = p.sbuf("EPSR", [1, 1], F32)
        self.SCL = p.sbuf("SCL", [1, 16], F32)
        self.IDENT = p.sbuf("IDENT", [128, 128], F32)
        self.PS = [p.psum(f"ps{i}", [128, 512], F32) for i in range(8)]
        self.PK = [f"P{i}" for i in range(8)]
        self.col_slots = {}
        self.col_next = 0

    def Av(self, i, c0=0, n=2048):
        return self.A[:, i * 2048 + c0: i * 2048 + c0 + n]

    def Cv(self, i, c0=0, n=2048):
        return self.C[:, i * 2048 + c0: i * 2048 + c0 + n]

    def init_consts(self, ident_hbm):
        p = self.p
        p.op("vector", lambda e: e.memset(self.ONESC[:], 1.0), writes=["ONESC"])
        p.op("vector", lambda e: e.memset(self.ONESCF[:], 1.0), writes=["ONESCF"])
        p.op("vector", lambda e: e.memset(self.ONESR[:], 1.0), writes=["ONESR"])
        p.op("vector", lambda e: e.memset(self.ONESRF[:], 1.0), writes=["ONESRF"])
        p.op("vector", lambda e: e.memset(self.ONES127[:], 1.0 / 127.0), writes=["ONES127"])
        p.op("vector", lambda e: e.memset(self.EPSR[:], EPS), writes=["EPSR"])
        p.dma(self.IDENT[:], ident_hbm, writes=["IDENT"], label="ident")

    def load_cols(self, hbm_ap, name):
        s = self.col_next
        self.col_next += 8
        assert self.col_next <= 96
        self.col_slots[name] = s
        self.p.dma(self.COLS[:, s:s + 8], hbm_ap, writes=[f"COLS.{name}"],
                   label=f"col.{name}")

    def col(self, name, i):
        s = self.col_slots[name]
        return self.COLS[:, s + i:s + i + 1]

    def load_x(self, x_hbm):
        for i in range(NT):
            self.p.dma(self.Av(i), x_hbm[i * 128:(i + 1) * 128, :],
                       writes=[f"A.{i}"], label=f"ldx{i}")

    def store_x(self, x_hbm):
        outs = []
        for i in range(NT):
            outs.append(self.p.dma(x_hbm[i * 128:(i + 1) * 128, :], self.Av(i),
                                   reads=[f"A.{i}"], label=f"stx{i}"))
        return outs

    def emit_ln(self, x_ap, x_key, y_ap, y_key, tag, normalize=True):
        p = self.p
        xb = self.SC[:, R0:R0 + 2048]
        x2 = self.SC[:, R1:R1 + 2048]
        for i in range(NT):
            p.op("vector", lambda e, i=i: e.tensor_copy(xb, x_ap(i)),
                 reads=_kl(x_key(i)), writes=["SC.A"],
                 label=f"{tag}.cast{i}")
            p.op("scalar", lambda e: e.activation(x2, xb, AF.Square),
                 reads=["SC.A"], writes=["SC.B"], label=f"{tag}.sq{i}")
            for c in range(4):
                r = (c // 2) * 32
                p.op("tensor", lambda e, i=i, c=c, r=r: e.matmul(
                    self.PS[4 + c % 2][r:r + 1, 0:512],
                    self.ONESC[:], xb[:, c * 512:(c + 1) * 512],
                    start=(i == 0), stop=(i == NT - 1)),
                    reads=["SC.A", "ONESC"], writes=[self.PK[4 + c % 2]],
                    label=f"{tag}.s{i}.{c}")
                p.op("tensor", lambda e, i=i, c=c, r=r: e.matmul(
                    self.PS[6 + c % 2][r:r + 1, 0:512],
                    self.ONESC[:], x2[:, c * 512:(c + 1) * 512],
                    start=(i == 0), stop=(i == NT - 1)),
                    reads=["SC.B", "ONESC"], writes=[self.PK[6 + c % 2]],
                    label=f"{tag}.q{i}.{c}")
        row = self.ROWS[0:1, 0:2048]
        rowr = self.ROWS[0:1, 2048:4096]
        for c in range(4):
            r = (c // 2) * 32
            p.op("vector", lambda e, c=c, r=r: e.tensor_scalar_mul(
                row[:, c * 512:(c + 1) * 512],
                self.PS[4 + c % 2][r:r + 1, 0:512], 1.0 / H),
                reads=[self.PK[4 + c % 2]], writes=["ROWS"], label=f"{tag}.mu{c}")
        if normalize:
            for c in range(4):
                p.op("tensor", lambda e, c=c: e.matmul(
                    self.PS[c][:, 0:512], self.ONESRF[:],
                    row[:, c * 512:(c + 1) * 512], start=True, stop=True),
                    reads=["ROWS", "ONESRF"], writes=[self.PK[c]],
                    label=f"{tag}.repmu{c}")
        p.op("scalar", lambda e: e.activation(rowr, row, AF.Square),
             reads=["ROWS"], writes=["ROWS"], label=f"{tag}.mu2")
        for c in range(4):
            r = (c // 2) * 32
            p.op("vector", lambda e, c=c, r=r: e.scalar_tensor_tensor(
                rowr[:, c * 512:(c + 1) * 512],
                self.PS[6 + c % 2][r:r + 1, 0:512], 1.0 / H,
                rowr[:, c * 512:(c + 1) * 512], ALU.mult, ALU.subtract),
                reads=[self.PK[6 + c % 2], "ROWS"], writes=["ROWS"],
                label=f"{tag}.var{c}")
        p.op("scalar", lambda e: e.activation(rowr, rowr, AF.Sqrt, bias=self.EPSR[0:1, 0:1]),
             reads=["ROWS", "EPSR"], writes=["ROWS"], label=f"{tag}.sd")
        p.op("vector", lambda e: e.reciprocal(rowr, rowr),
             reads=["ROWS"], writes=["ROWS"], label=f"{tag}.rstd")
        if not normalize:
            return
        for c in range(4):
            p.op("tensor", lambda e, c=c: e.matmul(
                self.PS[4 + c][:, 0:512], self.ONESRF[:],
                rowr[:, c * 512:(c + 1) * 512], start=True, stop=True),
                reads=["ROWS", "ONESRF"], writes=[self.PK[4 + c]],
                label=f"{tag}.reprs{c}")
        for c in range(4):
            for i in range(NT):
                p.op("vector", lambda e, i=i, c=c: e.scalar_tensor_tensor(
                    self.S1[:, 0:512], self.PS[c][:, 0:512], -1.0,
                    x_ap(i)[:, c * 512:(c + 1) * 512], ALU.mult, ALU.add),
                    reads=[self.PK[c]] + _kl(x_key(i)), writes=["S1"],
                    label=f"{tag}.sub{i}.{c}")
                p.op("vector", lambda e, i=i, c=c: e.tensor_tensor(
                    y_ap(i)[:, c * 512:(c + 1) * 512], self.S1[:, 0:512],
                    self.PS[4 + c][:, 0:512], ALU.mult),
                    reads=["S1", self.PK[4 + c]], writes=_kl(y_key(i)),
                    label=f"{tag}.nrm{i}.{c}")

    def emit_quant_scales(self, src_ap, src_key, nsrc, tlen, tag):
        p = self.p
        ntile = tlen // 128
        s1 = self.S1[:, 0:tlen]
        sqt = self.SC[:, R1:R1 + tlen] if tlen <= 2048 else None
        p.op("vector", lambda e: e.tensor_tensor(s1, src_ap(0), src_ap(0), ALU.mult),
             reads=[src_key(0)], writes=["S1"], label=f"{tag}.m0")
        for j in range(1, nsrc):
            p.op("vector", lambda e, j=j: e.tensor_tensor(sqt, src_ap(j), src_ap(j), ALU.mult),
                 reads=_kl(src_key(j)), writes=["SC.B"], label=f"{tag}.msq{j}")
            p.op("vector", lambda e: e.tensor_tensor(s1, s1, sqt, ALU.max),
                 reads=["S1", "SC.B"], writes=["S1"], label=f"{tag}.m{j}")
        for tt in range(ntile):
            p.op("tensor", lambda e, tt=tt: e.transpose(
                self.PS[4][:, 0:128], s1[:, tt * 128:(tt + 1) * 128], self.IDENT[:]),
                reads=["S1", "IDENT"], writes=[self.PK[4]], label=f"{tag}.tr{tt}")
            p.op("vector", lambda e, tt=tt, ntile=ntile: e.reduce_max(
                self.CCOL[:, ntile + tt:ntile + tt + 1], self.PS[4][:, 0:128], axis=AX.X),
                reads=[self.PK[4]], writes=["CCOL"], label=f"{tag}.red{tt}")
        cc = self.CCOL[:, ntile:2 * ntile]
        qq = self.CCOL[:, 0:ntile]
        p.op("scalar", lambda e: e.activation(cc, cc, AF.Sqrt),
             reads=["CCOL"], writes=["CCOL"], label=f"{tag}.sqrt")
        p.op("vector", lambda e: e.tensor_scalar_max(cc, cc, 1e-5),
             reads=["CCOL"], writes=["CCOL"], label=f"{tag}.clip")
        p.op("vector", lambda e: e.reciprocal(qq, cc),
             reads=["CCOL"], writes=["CCOL"], label=f"{tag}.inv")
        p.op("vector", lambda e: e.tensor_scalar_mul(qq, qq, 127.0),
             reads=["CCOL"], writes=["CCOL"], label=f"{tag}.q")
        p.op("vector", lambda e: e.tensor_scalar_mul(cc, cc, 1.0 / 127.0),
             reads=["CCOL"], writes=["CCOL"], label=f"{tag}.d")
        p.op("tensor", lambda e, n2=2 * ntile: e.transpose(
            self.PS[4][0:n2, 0:128], self.CCOL[:, 0:n2], self.IDENT[:]),
            reads=["CCOL", "IDENT"], writes=[self.PK[4]], label=f"{tag}.tr2")
        p.op("vector", lambda e, n2=2 * ntile: e.tensor_copy(
            self.WF[0:n2, 0:128], self.PS[4][0:n2, 0:128]),
            reads=[self.PK[4]], writes=["WF"], label=f"{tag}.evqf")
        nch = (tlen + 511) // 512
        for c in range(nch):
            w = min(512, tlen - c * 512)
            nr = (w + 127) // 128
            p.dma(self.QROWF[0:1, 0:w],
                  self.WF[ntile + 4 * c:ntile + 4 * c + nr, 0:128],
                  reads=["WF", "QROWF"], writes=["QROWF"],
                  label=f"{tag}.lind{c}")
            p.op("tensor", lambda e, w=w: e.matmul(
                self.PS[4][:, 0:w], self.ONESRF[:], self.QROWF[0:1, 0:w],
                start=True, stop=True),
                reads=["QROWF", "ONESRF"], writes=[self.PK[4]],
                label=f"{tag}.rd{c}")
            p.op("vector", lambda e, c=c, w=w: e.tensor_copy(
                self.SC[:, c * 512:c * 512 + w], self.PS[4][:, 0:w]),
                reads=[self.PK[4]], writes=["SC.A"],
                label=f"{tag}.red{c}")

    def _ladder(self, v, dst_ap, dst_key, n, tag):
        p = self.p
        t0 = self.SC[:, R2:R2 + n]
        ca = self.SC[:, R2 + 512:R2 + 512 + n]
        md = self.SC[:, R2 + 1024:R2 + 1024 + n]
        p.op("vector", lambda e: e.tensor_copy(self.QI[:, 0:n], v),
             reads=["S1"], writes=["QI"], label=f"{tag}.i")
        p.op("vector", lambda e: e.tensor_copy(t0, self.QI[:, 0:n]),
             reads=["QI"], writes=["SC.C"], label=f"{tag}.t0")
        p.op("vector", lambda e: e.tensor_tensor(md, t0, v, ALU.subtract),
             reads=["SC.C", "S1"], writes=["SC.C"], label=f"{tag}.md")
        p.op("vector", lambda e: e.tensor_scalar(dst_ap, md, -0.5, None, ALU.is_le),
             reads=["SC.C"], writes=_kl(dst_key), label=f"{tag}.c1")
        p.op("vector", lambda e: e.tensor_tensor(dst_ap, t0, dst_ap, ALU.add),
             reads=["SC.C"] + _kl(dst_key), writes=_kl(dst_key), label=f"{tag}.a1")
        _LVL = ([(-0.5 - k, ALU.add, ALU.is_le) for k in range(1, LADDER_K)]
                + [(0.5 + k, ALU.subtract, ALU.is_ge) for k in range(LADDER_K)])
        for (thr, op, alu) in _LVL:
            p.op("vector", lambda e, thr=thr, alu=alu: e.tensor_scalar(
                ca, md, thr, None, alu),
                reads=["SC.C"], writes=["SC.C"], label=f"{tag}.c{thr}")
            p.op("vector", lambda e, op=op: e.tensor_tensor(dst_ap, dst_ap, ca, op),
                 reads=_kl(dst_key) + ["SC.C"], writes=_kl(dst_key), label=f"{tag}.a{thr}")

    def _ladder_sar(self, v, dst_ap, dst_key, n, tag, eng="vector", vkey="S1"):
        p = self.p
        qi = self.QI[:, 0:n]
        if eng == "vector":
            b = self.SC[:, R2 + 1024:R2 + 1024 + n]
            c = self.S1[:, 1536:1536 + n]
            d = self.S1[:, 1024:1024 + n]
            kb, kc, kd = "SC.C", "S1", "S1"
        else:
            b = self.Q2B[:, 0:n]
            c = self.SC[:, R2 + 512:R2 + 512 + n]
            d = v
            kb, kc, kd = "Q2B", "SC.C", vkey
        p.op(eng, lambda e: e.tensor_copy(qi, v),
             reads=[vkey], writes=["QI"], label=f"{tag}.i")
        p.op(eng, lambda e: e.tensor_tensor(d, v, qi, ALU.subtract),
             reads=[vkey, "QI"], writes=[kd], label=f"{tag}.d")
        p.op(eng, lambda e: e.tensor_scalar(b, d, 0.5, None, ALU.is_ge),
             reads=[kd], writes=[kb], label=f"{tag}.t")
        p.op(eng, lambda e: e.scalar_tensor_tensor(
            b, d, -0.5, b, ALU.is_le, ALU.subtract),
            reads=[kd, kb], writes=[kb], label=f"{tag}.u")
        p.op(eng, lambda e: e.tensor_tensor(dst_ap, qi, b, ALU.subtract),
             reads=["QI", kb], writes=_kl(dst_key), label=f"{tag}.fin")

    def emit_quant1(self, tag):
        p = self.p
        for c in range(4):
            p.op("tensor", lambda e, c=c: e.matmul(
                self.PS[6][:, 0:512], self.ONESRF[:],
                self.ROWS[0:1, c * 512:(c + 1) * 512], start=True, stop=True),
                reads=["ROWS", "ONESRF"], writes=[self.PK[6]], label=f"{tag}.m{c}")
            for i in range(NT):
                p.op("vector", lambda e, i=i, c=c: e.scalar_tensor_tensor(
                    self.S1[:, 0:512], self.PS[6][:, 0:512], -1.0,
                    self.Av(i, c * 512, 512), ALU.mult, ALU.add),
                    reads=[self.PK[6], f"A.{i}"], writes=["S1"], label=f"{tag}.d{c}.{i}")
                p.op("vector", lambda e: e.tensor_tensor(
                    self.S1[:, 512:1024], self.S1[:, 0:512], self.S1[:, 0:512], ALU.mult),
                    reads=["S1"], writes=["S1"], label=f"{tag}.sq{c}.{i}")
                if i == 0:
                    p.op("vector", lambda e: e.tensor_copy(
                        self.S1[:, 1024:1536], self.S1[:, 512:1024]),
                        reads=["S1"], writes=["S1"], label=f"{tag}.M0.{c}")
                else:
                    p.op("vector", lambda e: e.tensor_tensor(
                        self.S1[:, 1024:1536], self.S1[:, 1024:1536],
                        self.S1[:, 512:1024], ALU.max),
                        reads=["S1"], writes=["S1"], label=f"{tag}.M{c}.{i}")
            for tt in range(4):
                p.op("tensor", lambda e, tt=tt: e.transpose(
                    self.PS[4][:, 0:128],
                    self.S1[:, 1024 + tt * 128:1024 + (tt + 1) * 128], self.IDENT[:]),
                    reads=["S1", "IDENT"], writes=[self.PK[4]], label=f"{tag}.tr{c}.{tt}")
                p.op("vector", lambda e, c=c, tt=tt: e.reduce_max(
                    self.CCOL[:, 16 + 4 * c + tt:16 + 4 * c + tt + 1],
                    self.PS[4][:, 0:128], axis=AX.X),
                    reads=[self.PK[4]], writes=["CCOL"], label=f"{tag}.red{c}.{tt}")
        p.op("scalar", lambda e: e.activation(
            self.CCOL[:, 16:32], self.CCOL[:, 16:32], AF.Sqrt),
            reads=["CCOL"], writes=["CCOL"], label=f"{tag}.sqrt")
        p.op("tensor", lambda e: e.transpose(
            self.PS[4][0:16, 0:128], self.CCOL[:, 16:32], self.IDENT[:]),
            reads=["CCOL", "IDENT"], writes=[self.PK[4]], label=f"{tag}.tr2")
        p.op("vector", lambda e: e.tensor_copy(
            self.WF[0:16, 0:128], self.PS[4][0:16, 0:128]),
            reads=[self.PK[4]], writes=["WF"], label=f"{tag}.evqf")
        for c in range(4):
            p.dma(self.QROWF[0:1, 0:512], self.WF[4 * c:4 * c + 4, 0:128],
                  reads=["WF", "QROWF"], writes=["QROWF"], label=f"{tag}.ql{c}")
            p.op("vector", lambda e, c=c: e.tensor_tensor(
                self.QROWF[0:1, 0:512], self.QROWF[0:1, 0:512],
                self.ROWS[0:1, 2048 + c * 512:2048 + (c + 1) * 512], ALU.mult),
                reads=["QROWF", "ROWS"], writes=["QROWF"], label=f"{tag}.my{c}")
            p.op("vector", lambda e: e.tensor_scalar_max(
                self.QROWF[0:1, 0:512], self.QROWF[0:1, 0:512], 1e-5),
                reads=["QROWF"], writes=["QROWF"], label=f"{tag}.cl{c}")
            p.op("tensor", lambda e: e.matmul(
                self.PS[4][:, 0:512], self.ONES127[:], self.QROWF[0:1, 0:512],
                start=True, stop=True),
                reads=["QROWF", "ONES127"], writes=[self.PK[4]], label=f"{tag}.dr{c}")
            p.op("vector", lambda e, c=c: e.tensor_copy(
                self.SC[:, c * 512:(c + 1) * 512], self.PS[4][:, 0:512]),
                reads=[self.PK[4]], writes=["SC.A"], label=f"{tag}.de{c}")
            p.op("vector", lambda e: e.reciprocal(
                self.QROWF[0:1, 0:512], self.QROWF[0:1, 0:512]),
                reads=["QROWF"], writes=["QROWF"], label=f"{tag}.rc{c}")
            p.op("vector", lambda e: e.tensor_scalar_mul(
                self.QROWF[0:1, 0:512], self.QROWF[0:1, 0:512], 127.0),
                reads=["QROWF"], writes=["QROWF"], label=f"{tag}.q{c}")
            p.op("tensor", lambda e: e.matmul(
                self.PS[5][:, 0:512], self.ONESRF[:], self.QROWF[0:1, 0:512],
                start=True, stop=True),
                reads=["QROWF", "ONESRF"], writes=[self.PK[5]], label=f"{tag}.qr{c}")
            p.op("tensor", lambda e, c=c: e.matmul(
                self.PS[6][:, 0:512], self.ONESRF[:],
                self.ROWS[0:1, c * 512:(c + 1) * 512], start=True, stop=True),
                reads=["ROWS", "ONESRF"], writes=[self.PK[6]], label=f"{tag}.mr{c}")
            p.op("tensor", lambda e, c=c: e.matmul(
                self.PS[7][:, 0:512], self.ONESRF[:],
                self.ROWS[0:1, 2048 + c * 512:2048 + (c + 1) * 512],
                start=True, stop=True),
                reads=["ROWS", "ONESRF"], writes=[self.PK[7]], label=f"{tag}.rr{c}")
            for s in range(NT):
                odd = False
                s1v = self.S1[:, 0:512]
                v = s1v
                vk = "S1"
                p.op("vector", lambda e, s=s, c=c: e.scalar_tensor_tensor(
                    s1v, self.PS[6][:, 0:512], -1.0, self.Av(s, c * 512, 512),
                    ALU.mult, ALU.add),
                    reads=[self.PK[6], f"A.{s}"], writes=["S1"],
                    label=f"{tag}.v1.{c}.{s}")
                p.op("vector", lambda e: e.tensor_tensor(
                    s1v, s1v, self.PS[7][:, 0:512], ALU.mult),
                    reads=["S1", self.PK[7]], writes=["S1"], label=f"{tag}.v2.{c}.{s}")
                p.op("vector", lambda e, v=v: e.tensor_tensor(
                    v, s1v, self.PS[5][:, 0:512], ALU.mult),
                    reads=["S1", self.PK[5]], writes=[vk], label=f"{tag}.v3.{c}.{s}")
                self._ladder_sar(v, self.Cv(s, c * 512, 512), f"C.{s}", 512,
                                 f"{tag}.L{c}.{s}",
                                 eng="gpsimd" if odd else "vector", vkey=vk)

    def emit_quant_apply(self, mode, src_ap, src_key, dst_ap, dst_key, nsrc, tlen, tag):
        p = self.p
        for c in range(tlen // 512):
            p.dma(self.QROWF[0:1, 0:512], self.WF[4 * c:4 * c + 4, 0:128],
                  reads=["WF", "QROWF"], writes=["QROWF"], label=f"{tag}.ql{c}")
            p.op("tensor", lambda e: e.matmul(
                self.PS[5][:, 0:512], self.ONESRF[:], self.QROWF[0:1, 0:512],
                start=True, stop=True),
                reads=["QROWF", "ONESRF"], writes=[self.PK[5]], label=f"{tag}.qr{c}")
            if mode == "ln":
                p.op("tensor", lambda e, c=c: e.matmul(
                    self.PS[6][:, 0:512], self.ONESRF[:],
                    self.ROWS[0:1, c * 512:(c + 1) * 512], start=True, stop=True),
                    reads=["ROWS", "ONESRF"], writes=[self.PK[6]], label=f"{tag}.mr{c}")
                p.op("tensor", lambda e, c=c: e.matmul(
                    self.PS[7][:, 0:512], self.ONESRF[:],
                    self.ROWS[0:1, 2048 + c * 512:2048 + (c + 1) * 512],
                    start=True, stop=True),
                    reads=["ROWS", "ONESRF"], writes=[self.PK[7]], label=f"{tag}.rr{c}")
            for s in range(nsrc):
                sap = src_ap(s)[:, c * 512:(c + 1) * 512]
                dap = dst_ap(s)[:, c * 512:(c + 1) * 512]
                odd = False
                s1v = self.S1[:, 0:512]
                v = s1v
                vk = "S1"
                leng = "vector"
                if mode == "ln":
                    p.op("vector", lambda e, sap=sap: e.scalar_tensor_tensor(
                        s1v, self.PS[6][:, 0:512], -1.0, sap, ALU.mult, ALU.add),
                        reads=[self.PK[6]] + _kl(src_key(s)), writes=["S1"],
                        label=f"{tag}.v1.{c}.{s}")
                    p.op("vector", lambda e: e.tensor_tensor(
                        s1v, s1v, self.PS[7][:, 0:512], ALU.mult),
                        reads=["S1", self.PK[7]], writes=["S1"], label=f"{tag}.v2.{c}.{s}")
                    p.op("vector", lambda e, v=v: e.tensor_tensor(
                        v, s1v, self.PS[5][:, 0:512], ALU.mult),
                        reads=["S1", self.PK[5]], writes=[vk], label=f"{tag}.v3.{c}.{s}")
                    self._ladder_sar(v, dap, dst_key(s), 512, f"{tag}.L{c}.{s}",
                                     eng=leng, vkey=vk)
                else:
                    p.op("vector", lambda e, sap=sap, v=v: e.tensor_tensor(
                        v, sap, self.PS[5][:, 0:512], ALU.mult),
                        reads=_kl(src_key(s)) + [self.PK[5]], writes=[vk],
                        label=f"{tag}.v.{c}.{s}")
                    self._ladder_sar(v, dap, dst_key(s), 512, f"{tag}.L{c}.{s}",
                                     eng=leng, vkey=vk)

    def emit_wstats(self, hbm_w, rows, cols, slot, extra_mul, tag):
        p = self.p
        n = rows * cols
        view = hbm_w.rearrange("(q p) c -> p q c", p=128)
        nq = rows // 128
        csz = min(cols, 2048)
        chunks = [(q, c0) for q in range(nq) for c0 in range(0, cols, csz)]
        acc = self.CCOL[:, 44:45]
        first = True
        for ch, (q, c0) in enumerate(chunks):
            p.dma(self.S1[:, 0:csz], view[:, q, c0:c0 + csz],
                  writes=["S1"], label=f"{tag}.ld{ch}")
            if first:
                p.op("vector", lambda e, csz=csz: e.reduce_sum(
                    acc, self.S1[:, 0:csz], axis=AX.X, apply_absolute_value=True),
                    reads=["S1"], writes=["CCOL"], label=f"{tag}.r{ch}")
                first = False
            else:
                p.op("vector", lambda e, csz=csz: e.reduce_sum(
                    self.CCOL[:, 45:46], self.S1[:, 0:csz], axis=AX.X,
                    apply_absolute_value=True),
                    reads=["S1"], writes=["CCOL"], label=f"{tag}.r{ch}")
                p.op("vector", lambda e: e.tensor_add(acc, acc, self.CCOL[:, 45:46]),
                     reads=["CCOL", "CCOL"], writes=["CCOL"], label=f"{tag}.a{ch}")
        p.op("tensor", lambda e: e.matmul(self.PS[4][0:1, 0:1], acc, self.ONESCF[:],
                                          start=True, stop=True),
             reads=["CCOL", "ONESCF"], writes=[self.PK[4]], label=f"{tag}.x")
        p.op("vector", lambda e: e.tensor_scalar_mul(self.SCL[0:1, 0:1],
                                                     self.PS[4][0:1, 0:1], 1.0 / n),
             reads=[self.PK[4]], writes=["SCL"], label=f"{tag}.mean")
        p.op("vector", lambda e: e.tensor_scalar_max(self.SCL[0:1, 0:1],
                                                     self.SCL[0:1, 0:1], 1e-5),
             reads=["SCL"], writes=["SCL"], label=f"{tag}.clip")
        p.op("tensor", lambda e: e.matmul(self.PS[4][:, 0:1], self.ONESRF[:],
                                          self.SCL[0:1, 0:1], start=True, stop=True),
             reads=["SCL", "ONESRF"], writes=[self.PK[4]], label=f"{tag}.rep")
        p.op("vector", lambda e: e.tensor_scalar_mul(
            self.CCOL[:, slot:slot + 1], self.PS[4][:, 0:1], 0.5),
            reads=[self.PK[4]], writes=["CCOL"], label=f"{tag}.thr")
        p.op("vector", lambda e: e.tensor_scalar_mul(
            self.CCOL[:, slot + 1:slot + 2], self.PS[4][:, 0:1], -0.5),
            reads=[self.PK[4]], writes=["CCOL"], label=f"{tag}.nthr")
        p.op("vector", lambda e, em=extra_mul: e.tensor_scalar_mul(
            self.CCOL[:, slot + 2:slot + 3], self.PS[4][:, 0:1], em),
            reads=[self.PK[4]], writes=["CCOL"], label=f"{tag}.sw")

    def emit_ffn(self, wg, wu, wd, wdq, tag):
        p = self.p
        self.emit_wstats(wg, H, INNER, 32, 1.0, f"{tag}.wsg")
        self.emit_wstats(wu, H, INNER, 36, 1.0, f"{tag}.wsu")
        self.emit_wstats(wd, INNER, H, 40, 0.5, f"{tag}.wsd")
        wdv = wd.rearrange("(q p) c -> p q c", p=128)
        wqv = wdq.rearrange("(q p) c -> p q c", p=128)
        for q in range(NI):
            p.dma(self.WF[:, 0:1024], wdv[:, q, :], writes=["WF"], label=f"{tag}.dl{q}")
            p.op("vector", lambda e: e.tensor_scalar(
                self.WL[:, 0:1024], self.WF[:, 0:1024], self.CCOL[:, 40:41], None, ALU.is_ge),
                reads=["WF", "CCOL"], writes=["WLa"], label=f"{tag}.d1{q}")
            p.op("vector", lambda e: e.tensor_scalar(
                self.SC[:, 2048:3072], self.WF[:, 0:1024], self.CCOL[:, 41:42], None, ALU.is_le),
                reads=["WF", "CCOL"], writes=["SC.B"], label=f"{tag}.d2{q}")
            p.op("vector", lambda e: e.tensor_sub(
                self.WL[:, 0:1024], self.WL[:, 0:1024], self.SC[:, 2048:3072]),
                reads=["WLa", "SC.B"], writes=["WLa"], label=f"{tag}.d3{q}")
            p.dma(wqv[:, q, :], self.WL[:, 0:1024], reads=["WLa"], label=f"{tag}.ds{q}")
        self.emit_ln(lambda i: self.Av(i), lambda i: f"A.{i}",
                     lambda i: self.Cv(i), lambda i: f"C.{i}", f"{tag}.ln",
                     normalize=False)
        self.emit_quant1(f"{tag}.q1")
        wgv = wg.rearrange("(j p) i -> p j i", p=128)
        wuv = wu.rearrange("(j p) i -> p j i", p=128)
        for hh in range(2):
            for it in range(NI):
                pr = (self.D if it < 16 else self.E)
                poff = (it % 16) * 1024
                pkey = f"{'D' if it < 16 else 'E'}.{it % 16}"
                c2s = self.SC[:, R2:R2 + 1024]
                for (nm, off, sl, wv) in (("g", 0, 32, wgv), ("u", 1024, 36, wuv)):
                    wk = "WLa" if off == 0 else "SC.B"
                    p.dma(self.WF[:, 0:1024].rearrange("p (j f) -> p j f", f=128),
                          wv[:, :, it * 128:(it + 1) * 128], writes=["WF"],
                          label=f"{tag}.{nm}{hh}.{it}.ld")
                    tdst = (self.WL[:, 0:1024] if off == 0
                            else self.SC[:, 2048:3072])
                    p.op("vector", lambda e, sl=sl, tdst=tdst: e.tensor_scalar(
                        tdst, self.WF[:, 0:1024],
                        self.CCOL[:, sl:sl + 1], None, ALU.is_ge),
                        reads=["WF", "CCOL"], writes=[wk],
                        label=f"{tag}.t1{nm}{hh}.{it}")
                    p.op("vector", lambda e, off=off, sl=sl, c2s=c2s: e.tensor_scalar(
                        c2s, self.WF[:, 0:1024],
                        self.CCOL[:, sl + 1:sl + 2], None, ALU.is_le),
                        reads=["WF", "CCOL"], writes=["SC.C"],
                        label=f"{tag}.t2{nm}{hh}.{it}")
                    p.op("vector", lambda e, c2s=c2s, tdst=tdst: e.tensor_sub(
                        tdst, tdst, c2s),
                        reads=[wk, "SC.C"], writes=[wk],
                        label=f"{tag}.t3{nm}{hh}.{it}")
                for c in range(2):
                    t0 = hh * 1024 + c * 512
                    def gmm(e, it=it, t0=t0):
                        last = None
                        for j in range(NT):
                            last = e.matmul(self.PS[0][:, 0:512],
                                            self.WL[:, j * 128:(j + 1) * 128],
                                            self.Cv(j, t0, 512),
                                            start=(j == 0), stop=(j == NT - 1))
                        return last
                    p.op("tensor", gmm,
                         reads=["WLa"] + [f"C.{j}" for j in range(NT)],
                         writes=[self.PK[0]], label=f"{tag}.mmg{hh}.{it}.{c}")
                    def umm(e, it=it, t0=t0):
                        last = None
                        for j in range(NT):
                            last = e.matmul(self.PS[1][:, 0:512],
                                            self.SC[:, 2048 + j * 128:2048 + (j + 1) * 128],
                                            self.Cv(j, t0, 512),
                                            start=(j == 0), stop=(j == NT - 1))
                        return last
                    p.op("tensor", umm,
                         reads=["SC.B"] + [f"C.{j}" for j in range(NT)],
                         writes=[self.PK[1]], label=f"{tag}.mmu{hh}.{it}.{c}")
                    p.op("vector", lambda e, t0=t0: e.scalar_tensor_tensor(
                        self.S1[:, 0:512], self.PS[0][:, 0:512], self.CCOL[:, 34:35],
                        self.SC[:, t0:t0 + 512], ALU.mult, ALU.mult),
                        reads=[self.PK[0], "CCOL", "SC.A"], writes=["S1"],
                        label=f"{tag}.eg{hh}.{it}.{c}")
                    p.op("scalar", lambda e: e.activation(
                        self.SC[:, R3:R3 + 512], self.S1[:, 0:512], AF.Silu),
                        reads=["S1"], writes=["SC.D"], label=f"{tag}.sil{hh}.{it}.{c}")
                    p.op("vector", lambda e, t0=t0: e.scalar_tensor_tensor(
                        self.S1[:, 512:1024], self.PS[1][:, 0:512], self.CCOL[:, 38:39],
                        self.SC[:, t0:t0 + 512], ALU.mult, ALU.mult),
                        reads=[self.PK[1], "CCOL", "SC.A"], writes=["S1"],
                        label=f"{tag}.eu{hh}.{it}.{c}")
                    p.op("vector", lambda e, pr=pr, poff=poff, c=c: e.tensor_tensor(
                        pr[:, poff + c * 512:poff + c * 512 + 512],
                        self.SC[:, R3:R3 + 512], self.S1[:, 512:1024], ALU.mult),
                        reads=["SC.D", "S1"], writes=[pkey],
                        label=f"{tag}.p{hh}.{it}.{c}")
            def psrc(j, hh=hh):
                pr = (self.D if j < 16 else self.E)
                return pr[:, (j % 16) * 1024:(j % 16) * 1024 + 1024]
            def psrck(j):
                return f"{'D' if j < 16 else 'E'}.{j % 16}"
            self.emit_quant_scales(psrc, psrck, NI, 1024, f"{tag}.q2{hh}")
            self.emit_quant_apply("p", psrc, psrck, psrc, psrck,
                                  NI, 1024, f"{tag}.qa2{hh}")
            for c in range(2):
                t0 = hh * 1024 + c * 512
                for q in range(NI):
                    p.dma(self.WL[:, 0:1024], wqv[:, q, :], writes=["WLa"],
                          label=f"{tag}.dw{hh}.{c}.{q}")
                    def dmm(e, q=q, c=c, hh=hh):
                        last = None
                        for m in range(NT):
                            pr = (self.D if q < 16 else self.E)
                            last = e.matmul(
                                self.PS[m][:, 0:512],
                                self.WL[:, m * 128:(m + 1) * 128],
                                pr[:, (q % 16) * 1024 + c * 512:(q % 16) * 1024 + c * 512 + 512],
                                start=(q == 0), stop=(q == NI - 1))
                        return last
                    p.op("tensor", dmm,
                         reads=["WLa", psrck(q)], writes=self.PK[0:8],
                         label=f"{tag}.mmd{hh}.{c}.{q}")
                for m in range(NT):
                    p.op("vector", lambda e, m=m, c=c: e.scalar_tensor_tensor(
                        self.S1[:, 0:512], self.PS[m][:, 0:512], self.CCOL[:, 42:43],
                        self.SC[:, c * 512:c * 512 + 512], ALU.mult, ALU.mult),
                        reads=[self.PK[m], "CCOL", "SC.A"], writes=["S1"],
                        label=f"{tag}.ed{hh}.{c}.{m}")
                    p.op("vector", lambda e, m=m, t0=t0: e.tensor_tensor(
                        self.Av(m, t0, 512), self.Av(m, t0, 512),
                        self.S1[:, 0:512], ALU.add),
                        reads=["S1", f"A.{m}"], writes=[f"A.{m}"],
                        label=f"{tag}.acc{hh}.{c}.{m}")


def build_ffn_prog():
    nc = bass.Bass("TRN2", target_bir_lowering=False, debug=False)
    xT = nc.dram_tensor("xT", [H, T], F32, kind="ExternalInput").ap()
    wg = nc.dram_tensor("wg", [H, INNER], F32, kind="ExternalInput").ap()
    wu = nc.dram_tensor("wu", [H, INNER], F32, kind="ExternalInput").ap()
    wd = nc.dram_tensor("wd", [INNER, H], F32, kind="ExternalInput").ap()
    ident = nc.dram_tensor("ident", [128, 128], F32, kind="ExternalInput").ap()
    xout = nc.dram_tensor("xout", [H, T], F32, kind="ExternalOutput").ap()
    wdq = nc.dram_tensor("wdq", [INNER, H], BF16, kind="Internal").ap()
    with ExitStack() as stack:
        mk = MK(nc, stack)
        mk.init_consts(ident)
        mk.load_x(xT)
        mk.emit_ffn(wg, wu, wd, wdq, "f")
        outs = mk.store_x(xout)
        with nc.Block() as block:
            mk.p.emit(block, end_fence=outs)
    return nc


def _ident_np():
    return np.eye(128, dtype=np.float32)


def _colify(v):
    return np.ascontiguousarray(v.reshape(8, 128).T).astype(np.float32)



def _ck(i):
    a = (i * 2050) // 2048
    b = min((i * 2050 + 2049) // 2048, 7)
    return [f"C.{a}"] if a == b else [f"C.{a}", f"C.{b}"]


def _dk(i):
    return [f"D.{2 * i}", f"D.{2 * i + 1}"]


def _ek(i):
    return [f"E.{2 * i}", f"E.{2 * i + 1}"]


def _mk_emit_rwkv(self, wk, wv, wr, wo, xspill, tag):
    p = self.p
    def yv(i, off=1, n=2048):
        return self.C[:, i * 2050 + off: i * 2050 + off + n]
    for i in range(NT):
        p.op("vector", lambda e, i=i: e.memset(self.C[:, i * 2050:i * 2050 + 1], 0.0),
             writes=_ck(i), label=f"{tag}.z{i}")
    self.emit_ln(lambda i: self.Av(i), lambda i: f"A.{i}",
                 lambda i: yv(i), lambda i: _ck(i), f"{tag}.ln")
    for i in range(NT):
        p.dma(xspill[i * 128:(i + 1) * 128, :], self.Av(i),
              reads=[f"A.{i}"], writes=["XSP"], label=f"{tag}.sp{i}")
    for (nm, sl) in (("tmk", 48), ("tmv", 56), ("tmr", 64)):
        p.op("vector", lambda e, nm=nm, sl=sl: e.tensor_scalar(
            self.CCOL[:, sl:sl + 8], self.COLS[:, self.col_slots[nm]:self.col_slots[nm] + 8],
            -1.0, 1.0, ALU.mult, ALU.add),
            reads=[f"COLS.{nm}"], writes=["CCOL"], label=f"{tag}.om{nm}")
    p.op("scalar", lambda e: e.activation(
        self.CCOL[:, 72:80], self.COLS[:, self.col_slots["tdec"]:self.col_slots["tdec"] + 8],
        AF.Exp), reads=["COLS.tdec"], writes=["CCOL"], label=f"{tag}.td1")
    p.op("scalar", lambda e: e.activation(
        self.CCOL[:, 72:80], self.CCOL[:, 72:80], AF.Exp, scale=-1.0),
        reads=["CCOL"], writes=["CCOL"], label=f"{tag}.td2")
    p.op("scalar", lambda e: e.activation(
        self.CCOL[:, 80:88], self.COLS[:, self.col_slots["tfirst"]:self.col_slots["tfirst"] + 8],
        AF.Exp), reads=["COLS.tfirst"], writes=["CCOL"], label=f"{tag}.etf")

    def emit_mix(osl, dkey_pref, tag2):
        tm = {48: "tmk", 56: "tmv", 64: "tmr"}[osl]
        for i in range(NT):
            p.op("vector", lambda e, i=i, osl=osl: e.tensor_scalar(
                self.SC[:, 0:2048], yv(i, 0), self.CCOL[:, osl + i:osl + i + 1],
                None, ALU.mult),
                reads=_ck(i) + ["CCOL"], writes=["SC.A"], label=f"{tag2}.a{i}")
            p.op("vector", lambda e, i=i, tm=tm: e.scalar_tensor_tensor(
                self.D[:, i * 2048:(i + 1) * 2048], yv(i),
                self.col(tm, i), self.SC[:, 0:2048], ALU.mult, ALU.add),
                reads=_ck(i) + [f"COLS.{tm}", "SC.A"], writes=_dk(i),
                label=f"{tag2}.b{i}")

    def proj_weights(w_hbm, i, tag2):
        p.dma(self.WF[:, 0:1024].rearrange("p (j f) -> p j f", f=128),
              w_hbm.rearrange("(j p) o -> p j o", p=128)[:, :, i * 128:(i + 1) * 128],
              writes=["WF"], label=f"{tag2}.wl{i}")
        p.op("vector", lambda e: e.tensor_copy(self.WL[:, 0:1024], self.WF[:, 0:1024]),
             reads=["WF"], writes=["WLa"], label=f"{tag2}.wc{i}")

    def proj_group(e, i, c, src_pref):
        last = None
        for j in range(NT):
            last = e.matmul(self.PS[c % 2][:, 0:512],
                            self.WL[:, j * 128:(j + 1) * 128],
                            self.D[:, j * 2048 + c * 512: j * 2048 + c * 512 + 512],
                            start=(j == 0), stop=(j == NT - 1))
        return last

    emit_mix(56, "D", f"{tag}.vm")
    for i in range(NT):
        proj_weights(wv, i, f"{tag}.vw{i}")
        for c in range(4):
            p.op("tensor", lambda e, i=i, c=c: proj_group(e, i, c, "D"),
                 reads=["WLa"] + [k for j in range(NT) for k in _dk(j)],
                 writes=[self.PK[c % 2]], label=f"{tag}.vmm{i}.{c}")
            p.op("vector", lambda e, i=i, c=c: e.tensor_copy(
                self.E[:, i * 2048 + c * 512:i * 2048 + c * 512 + 512],
                self.PS[c % 2][:, 0:512]),
                reads=[self.PK[c % 2]], writes=_ek(i), label=f"{tag}.ve{i}.{c}")
    emit_mix(64, "D", f"{tag}.rm")
    for i in range(NT):
        proj_weights(wr, i, f"{tag}.rw{i}")
        for c in range(4):
            p.op("tensor", lambda e, i=i, c=c: proj_group(e, i, c, "D"),
                 reads=["WLa"] + [k for j in range(NT) for k in _dk(j)],
                 writes=[self.PK[c % 2]], label=f"{tag}.rmm{i}.{c}")
            p.op("scalar", lambda e, i=i, c=c: e.activation(
                self.Av(i, c * 512, 512), self.PS[c % 2][:, 0:512], AF.Sigmoid),
                reads=[self.PK[c % 2]], writes=[f"A.{i}"], label=f"{tag}.re{i}.{c}")
    emit_mix(48, "D", f"{tag}.km")
    for i in range(NT):
        proj_weights(wk, i, f"{tag}.kw{i}")
        for c in range(4):
            p.op("tensor", lambda e, i=i, c=c: proj_group(e, i, c, "D"),
                 reads=["WLa"] + [k for j in range(NT) for k in _dk(j)],
                 writes=[self.PK[c % 2]], label=f"{tag}.kmm{i}.{c}")
            p.op("scalar", lambda e, i=i, c=c: e.activation(
                self.SC[:, c * 512:c * 512 + 512], self.PS[c % 2][:, 0:512], AF.Exp),
                reads=[self.PK[c % 2]], writes=["SC.A"], label=f"{tag}.ek{i}.{c}")
        p.op("vector", lambda e, i=i: e.tensor_tensor(
            self.SC[:, 2048:4096], self.SC[:, 0:2048],
            self.E[:, i * 2048:(i + 1) * 2048], ALU.mult),
            reads=["SC.A"] + _ek(i), writes=["SC.B"], label=f"{tag}.ekv{i}")
        p.op("vector", lambda e: e.memset(self.S1[:, 0:2048], 1.0),
             writes=["S1"], label=f"{tag}.tdm{i}")
        p.op("vector", lambda e, i=i: e.tensor_scalar(
            self.S1[:, 0:2048], self.S1[:, 0:2048],
            self.CCOL[:, 72 + i:72 + i + 1], None, ALU.mult),
            reads=["S1", "CCOL"], writes=["S1"], label=f"{tag}.tdf{i}")
        p.op("vector", lambda e: e.tensor_tensor_scan(
            self.SC[:, 4096:6144], self.S1[:, 0:2048], self.SC[:, 2048:4096],
            0.0, ALU.mult, ALU.add),
            reads=["S1", "SC.B"], writes=["SC.C", "SC.D"], label=f"{tag}.nsc{i}")
        p.op("vector", lambda e, i=i: e.tensor_tensor_scan(
            self.C[:, i * 2050:i * 2050 + 2048], self.S1[:, 0:2048],
            self.SC[:, 0:2048], 0.0, ALU.mult, ALU.add),
            reads=["S1", "SC.A"] + _ck(i), writes=_ck(i), label=f"{tag}.dsc{i}")
        p.op("vector", lambda e, i=i: e.tensor_scalar(
            self.SC[:, 2048:4096], self.SC[:, 2048:4096],
            self.CCOL[:, 80 + i:80 + i + 1], None, ALU.mult),
            reads=["SC.B", "CCOL"], writes=["SC.B"], label=f"{tag}.e1v{i}")
        p.op("vector", lambda e, i=i: e.tensor_scalar(
            self.SC[:, 0:2048], self.SC[:, 0:2048],
            self.CCOL[:, 80 + i:80 + i + 1], None, ALU.mult),
            reads=["SC.A", "CCOL"], writes=["SC.A"], label=f"{tag}.e1{i}")
        p.op("vector", lambda e, i=i: e.tensor_copy(
            self.S1[:, 0:1], self.SC[:, 0:1]),
            reads=["SC.A"], writes=["S1"], label=f"{tag}.d0{i}")
        p.op("vector", lambda e, i=i: e.tensor_tensor(
            self.S1[:, 1:2048], self.C[:, i * 2050:i * 2050 + 2047],
            self.SC[:, 1:2048], ALU.add),
            reads=_ck(i) + ["SC.A", "S1"], writes=["S1"], label=f"{tag}.dsh{i}")
        p.op("vector", lambda e: e.reciprocal(self.S1[:, 0:2048], self.S1[:, 0:2048]),
             reads=["S1"], writes=["S1"], label=f"{tag}.rec{i}")
        p.op("vector", lambda e: e.tensor_tensor(
            self.SC[:, 2049:4096], self.SC[:, 4096:4096 + 2047],
            self.SC[:, 2049:4096], ALU.add),
            reads=["SC.C", "SC.D", "SC.B"], writes=["SC.B"], label=f"{tag}.nsh{i}")
        p.op("vector", lambda e: e.tensor_tensor(
            self.S1[:, 0:2048], self.SC[:, 2048:4096], self.S1[:, 0:2048], ALU.mult),
            reads=["SC.B", "S1"], writes=["S1"], label=f"{tag}.wkv{i}")
        p.op("vector", lambda e, i=i: e.tensor_tensor(
            self.C[:, i * 2050:i * 2050 + 2048], self.S1[:, 0:2048],
            self.Av(i), ALU.mult),
            reads=["S1", f"A.{i}"] + _ck(i), writes=_ck(i), label=f"{tag}.rw{i}")
    for m in range(NT):
        p.dma(self.Av(m), xspill[m * 128:(m + 1) * 128, :],
              reads=[f"A.{m}", "XSP"], writes=[f"A.{m}"], label=f"{tag}.rl{m}")
    for m in range(NT):
        proj_weights(wo, m, f"{tag}.ow{m}")
        for c in range(4):
            def omm(e, m=m, c=c):
                last = None
                for j in range(NT):
                    last = e.matmul(self.PS[2 + c % 2][:, 0:512],
                                    self.WL[:, j * 128:(j + 1) * 128],
                                    self.C[:, j * 2050 + c * 512: j * 2050 + c * 512 + 512],
                                    start=(j == 0), stop=(j == NT - 1))
                return last
            p.op("tensor", omm,
                 reads=["WLa"] + [f"C.{j}" for j in range(NT)],
                 writes=[self.PK[2 + c % 2]], label=f"{tag}.omm{m}.{c}")
            p.op("vector", lambda e, m=m, c=c: e.scalar_tensor_tensor(
                self.Av(m, c * 512, 512), self.PS[2 + c % 2][:, 0:512],
                self.col("bo", m), self.Av(m, c * 512, 512), ALU.add, ALU.add),
                reads=[self.PK[2 + c % 2], "COLS.bo", f"A.{m}"], writes=[f"A.{m}"],
                label=f"{tag}.oe{m}.{c}")


MK.emit_rwkv = _mk_emit_rwkv


def build_rwkv_prog():
    nc = bass.Bass("TRN2", target_bir_lowering=False, debug=False)
    xT = nc.dram_tensor("xT", [H, T], F32, kind="ExternalInput").ap()
    wk = nc.dram_tensor("wk", [H, H], F32, kind="ExternalInput").ap()
    wv = nc.dram_tensor("wv", [H, H], F32, kind="ExternalInput").ap()
    wr = nc.dram_tensor("wr", [H, H], F32, kind="ExternalInput").ap()
    wo = nc.dram_tensor("wo", [H, H], F32, kind="ExternalInput").ap()
    ident = nc.dram_tensor("ident", [128, 128], F32, kind="ExternalInput").ap()
    cols = {}
    for nm in ("tmk", "tmv", "tmr", "tdec", "tfirst", "bo"):
        cols[nm] = nc.dram_tensor(f"c_{nm}", [128, 8], F32, kind="ExternalInput").ap()
    xout = nc.dram_tensor("xout", [H, T], F32, kind="ExternalOutput").ap()
    xspill = nc.dram_tensor("xspill", [H, T], F32, kind="Internal").ap()
    with ExitStack() as stack:
        mk = MK(nc, stack)
        mk.init_consts(ident)
        for nm in cols:
            mk.load_cols(cols[nm], nm)
        mk.load_x(xT)
        mk.emit_rwkv(wk, wv, wr, wo, xspill, "r")
        outs = mk.store_x(xout)
        with nc.Block() as block:
            mk.p.emit(block, end_fence=outs)
    return nc


def _mk_emit_attn(self, Wq, Wk, Wv, Wo, Wrq, Wrk, Wrv, Wg, pastT, brow, xspill, tag):
    p = self.p
    ALLC = [f"C.{i}" for i in range(NT)]

    def wload(w_hbm, blk, woff, wkey, tag2):
        p.dma(self.WF[:, 0:1024].rearrange("p (j f) -> p j f", f=128),
              w_hbm.rearrange("(j p) o -> p j o", p=128)[:, :, blk * 128:(blk + 1) * 128],
              writes=["WF"], label=f"{tag2}.wl")
        p.op("vector", lambda e, woff=woff: e.tensor_copy(
            self.WL[:, woff:woff + 1024], self.WF[:, 0:1024]),
            reads=["WF"], writes=[wkey], label=f"{tag2}.wc")

    def projT(dst_off, dst_keys, rhs_keys, bias_name, blk, tag2, act=None,
              woff=0, wkey="WLa", rhs_tiles=None, dst_buf=None):
        dst_buf = self.D if dst_buf is None else dst_buf
        if rhs_tiles is None:
            rhs_tiles = [self.Cv(j) for j in range(NT)]
        for c in range(4):
            def mm(e, c=c, rhs_tiles=rhs_tiles, woff=woff):
                last = None
                for j in range(NT):
                    last = e.matmul(self.PS[c % 2][:, 0:512],
                                    self.WL[:, woff + j * 128:woff + (j + 1) * 128],
                                    rhs_tiles[j][:, c * 512:(c + 1) * 512],
                                    start=(j == 0), stop=(j == NT - 1))
                return last
            p.op("tensor", mm, reads=[wkey] + rhs_keys,
                 writes=[self.PK[c % 2]], label=f"{tag2}.mm{c}")
            if act is None:
                p.op("vector", lambda e, c=c, dst_off=dst_off, blk=blk,
                     bias_name=bias_name, dst_buf=dst_buf: e.tensor_scalar(
                    dst_buf[:, dst_off + c * 512:dst_off + c * 512 + 512],
                    self.PS[c % 2][:, 0:512], self.col(bias_name, blk), None,
                    ALU.add),
                    reads=[self.PK[c % 2], f"COLS.{bias_name}"], writes=dst_keys,
                    label=f"{tag2}.ev{c}")
            else:
                p.op("scalar", lambda e, c=c, dst_off=dst_off, blk=blk,
                     bias_name=bias_name, dst_buf=dst_buf, act=act: e.activation(
                    dst_buf[:, dst_off + c * 512:dst_off + c * 512 + 512],
                    self.PS[c % 2][:, 0:512], act, bias=self.col(bias_name, blk)),
                    reads=[self.PK[c % 2], f"COLS.{bias_name}"], writes=dst_keys,
                    label=f"{tag2}.ev{c}")

    def head_norm_cell(src_buf, src_off, hl, cell, skey, tag2):
        sq = self.SC[:, R2:R2 + 2048]
        p.op("scalar", lambda e: e.activation(
            sq[0:64, :], src_buf[hl:hl + 64, src_off:src_off + 2048], AF.Square),
            reads=skey, writes=["SC.C"], label=f"{tag2}.sq")
        for c in range(4):
            p.op("tensor", lambda e, c=c: e.matmul(
                self.PS[5][0:1, 0:512], self.ONESC[0:64, :],
                sq[0:64, c * 512:(c + 1) * 512], start=True, stop=True),
                reads=["SC.C", "ONESC"], writes=[self.PK[5]], label=f"{tag2}.nm{c}")
            p.op("vector", lambda e, c=c, cell=cell: e.reduce_max(
                self.CCOL[0:1, 40 + c:41 + c], self.PS[5][0:1, 0:512], axis=AX.X),
                reads=[self.PK[5]], writes=["CCOL"], label=f"{tag2}.nr{c}")
        p.op("vector", lambda e: e.tensor_tensor(
            self.CCOL[0:1, 40:41], self.CCOL[0:1, 40:41], self.CCOL[0:1, 41:42], ALU.max),
            reads=["CCOL"], writes=["CCOL"], label=f"{tag2}.nx1")
        p.op("vector", lambda e: e.tensor_tensor(
            self.CCOL[0:1, 42:43], self.CCOL[0:1, 42:43], self.CCOL[0:1, 43:44], ALU.max),
            reads=["CCOL"], writes=["CCOL"], label=f"{tag2}.nx2")
        p.op("vector", lambda e, cell=cell: e.tensor_tensor(
            self.CCOL[0:1, cell:cell + 1], self.CCOL[0:1, 40:41],
            self.CCOL[0:1, 42:43], ALU.max),
            reads=["CCOL"], writes=["CCOL"], label=f"{tag2}.nx3")

    def bias_col_from_cells(cell_q, cell_k, tag2):
        p.op("vector", lambda e: e.tensor_tensor(
            self.CCOL[0:1, 44:45], self.CCOL[0:1, cell_q:cell_q + 1],
            self.CCOL[0:1, cell_k:cell_k + 1], ALU.mult),
            reads=["CCOL"], writes=["CCOL"], label=f"{tag2}.bm")
        p.op("scalar", lambda e: e.activation(
            self.CCOL[0:1, 44:45], self.CCOL[0:1, 44:45], AF.Sqrt),
            reads=["CCOL"], writes=["CCOL"], label=f"{tag2}.bs")
        p.op("vector", lambda e: e.tensor_scalar_mul(
            self.CCOL[0:1, 44:45], self.CCOL[0:1, 44:45], -0.125),
            reads=["CCOL"], writes=["CCOL"], label=f"{tag2}.bn")
        p.op("tensor", lambda e: e.matmul(
            self.PS[5][:, 0:1], self.ONESRF[:], self.CCOL[0:1, 44:45],
            start=True, stop=True),
            reads=["CCOL", "ONESRF"], writes=[self.PK[5]], label=f"{tag2}.br")
        p.op("vector", lambda e: e.tensor_copy(self.CCOL[:, 46:47], self.PS[5][:, 0:1]),
             reads=[self.PK[5]], writes=["CCOL"], label=f"{tag2}.be")

    def flash(q_buf, q_off, q_keys, k_buf, k_off, k_keys, v_buf, v_off, v_keys,
              hl, dst_region, dst_keys, scale05, accumulate, tag2):
        for sc in range(4):
            for tau in range(16):
                pb = (sc * 16 + tau) % 2
                p.op("tensor", lambda e, sc=sc, tau=tau, pb=pb: e.matmul(
                    self.PS[pb][:, 0:512],
                    k_buf[hl:hl + 64, k_off + tau * 128:k_off + (tau + 1) * 128],
                    q_buf[hl:hl + 64, q_off + sc * 512:q_off + sc * 512 + 512],
                    start=True, stop=True),
                    reads=k_keys + q_keys, writes=[self.PK[pb]],
                    label=f"{tag2}.sc{sc}.{tau}")
                et_off, et_key = (R2, "SC.C") if tau % 2 == 0 else (R3, "SC.D")
                p.op("scalar", lambda e, pb=pb, et_off=et_off: e.activation(
                    self.SC[:, et_off:et_off + 512], self.PS[pb][:, 0:512],
                    AF.Exp, bias=self.CCOL[:, 46:47], scale=0.125),
                    reads=[self.PK[pb], "CCOL"], writes=[et_key],
                    label=f"{tag2}.ex{sc}.{tau}")
                def ctx(e, tau=tau, et_off=et_off):
                    e.matmul(self.PS[4][0:64, 0:512],
                             v_buf[:, v_off + tau * 128 + hl:v_off + tau * 128 + hl + 64],
                             self.SC[:, et_off:et_off + 512],
                             start=(tau == 0), stop=(tau == 15))
                    return e.matmul(self.PS[5][64:65, 0:512], self.ONESC[:],
                                    self.SC[:, et_off:et_off + 512],
                                    start=(tau == 0), stop=(tau == 15))
                p.op("tensor", ctx, reads=v_keys + [et_key, "ONESC"],
                     writes=[self.PK[4], self.PK[5]], label=f"{tag2}.cx{sc}.{tau}")
            p.op("vector", lambda e: e.reciprocal(self.S1[0:1, 0:512],
                                                  self.PS[5][64:65, 0:512]),
                 reads=[self.PK[5]], writes=["S1"], label=f"{tag2}.rc{sc}")
            if scale05:
                p.op("vector", lambda e: e.tensor_scalar_mul(
                    self.S1[0:1, 0:512], self.S1[0:1, 0:512], 0.5),
                    reads=["S1"], writes=["S1"], label=f"{tag2}.h{sc}")
            p.op("tensor", lambda e: e.matmul(
                self.PS[5][hl:hl + 64, 0:512], self.ONESRF[0:1, 0:64],
                self.S1[0:1, 0:512], start=True, stop=True),
                reads=["S1", "ONESRF"], writes=[self.PK[5]], label=f"{tag2}.rp{sc}")
            p.op("vector", lambda e: e.tensor_copy(
                self.SC[hl:hl + 64, 5120:5120 + 512], self.PS[5][hl:hl + 64, 0:512]),
                reads=[self.PK[5]], writes=["SC.C"], label=f"{tag2}.rpe{sc}")
            if not accumulate:
                p.op("vector", lambda e, sc=sc: e.tensor_tensor(
                    self.A[hl:hl + 64, dst_region + sc * 512:dst_region + sc * 512 + 512],
                    self.PS[4][0:64, 0:512], self.SC[hl:hl + 64, 5120:5120 + 512], ALU.mult),
                    reads=[self.PK[4], "SC.C"], writes=dst_keys,
                    label=f"{tag2}.no{sc}")
            else:
                p.op("vector", lambda e, sc=sc: e.tensor_tensor(
                    self.SC[hl:hl + 64, R2 + 512:R2 + 1024],
                    self.PS[4][0:64, 0:512], self.SC[hl:hl + 64, 5120:5120 + 512], ALU.mult),
                    reads=[self.PK[4], "SC.C"], writes=["SC.C"],
                    label=f"{tag2}.na{sc}")
                p.op("vector", lambda e, sc=sc: e.tensor_tensor(
                    self.A[hl:hl + 64, dst_region + sc * 512:dst_region + sc * 512 + 512],
                    self.A[hl:hl + 64, dst_region + sc * 512:dst_region + sc * 512 + 512],
                    self.SC[hl:hl + 64, R2 + 512:R2 + 1024], ALU.add),
                    reads=["SC.C"] + dst_keys, writes=dst_keys,
                    label=f"{tag2}.ac{sc}")

    self.emit_ln(lambda i: self.Av(i), lambda i: f"A.{i}",
                 lambda i: self.Cv(i), lambda i: [f"C.{i}"], f"{tag}.ln")
    for i in range(NT):
        p.dma(xspill[i * 128:(i + 1) * 128, :], self.Av(i),
              reads=[f"A.{i}"], writes=["XSP"], label=f"{tag}.sp{i}")
    p.dma(self.ROWS[0:1, 0:2048], brow, writes=["ROWS"], label=f"{tag}.brow")

    for g in range(NT):
        gk = [f"G{g}"]
        wload(Wq, g, 0, "WLa", f"{tag}.q{g}")
        projT(0, ["D.0", "D.1"], ALLC, "bq", g, f"{tag}.qp{g}")
        wload(Wk, g, 0, "WLa", f"{tag}.k{g}")
        projT(2048, ["D.2", "D.3"], ALLC, "bk", g, f"{tag}.kp{g}")
        wload(Wv, g, 0, "WLa", f"{tag}.v{g}")
        p.op("vector", lambda e, g=g: e.tensor_copy(
            self.SC[0:1, R2:R2 + 128], self.ROWS[0:1, g * 128:(g + 1) * 128]),
            reads=["ROWS"], writes=["SC.C"], label=f"{tag}.bv{g}")
        for tau in range(16):
            def vmm(e, tau=tau):
                for j in range(NT):
                    e.matmul(self.PS[tau % 2][:, 0:128],
                             self.Cv(j)[:, tau * 128:(tau + 1) * 128],
                             self.WL[:, j * 128:(j + 1) * 128],
                             start=(j == 0), stop=False)
                return e.matmul(self.PS[tau % 2][:, 0:128], self.ONESR[:],
                                self.SC[0:1, R2:R2 + 128], start=False, stop=True)
            p.op("tensor", vmm, reads=["WLa", "SC.C", "ONESR"] + ALLC,
                 writes=[self.PK[tau % 2]], label=f"{tag}.vm{g}.{tau}")
            p.op("vector", lambda e, tau=tau: e.tensor_copy(
                self.D[:, 4096 + tau * 128:4096 + (tau + 1) * 128],
                self.PS[tau % 2][:, 0:128]),
                reads=[self.PK[tau % 2]], writes=["D.4", "D.5"],
                label=f"{tag}.ve{g}.{tau}")
        for hh in range(2):
            hl = hh * 64
            head_norm_cell(self.D, 0, hl, 36, ["D.0", "D.1"], f"{tag}.nq{g}.{hh}")
            head_norm_cell(self.D, 2048, hl, 37, ["D.2", "D.3"], f"{tag}.nk{g}.{hh}")
            bias_col_from_cells(36, 37, f"{tag}.bc{g}.{hh}")
            flash(self.D, 0, ["D.0", "D.1"], self.D, 2048, ["D.2", "D.3"],
                  self.D, 4096, ["D.4", "D.5"], hl,
                  g * 2048, [f"A.{g}"], False, False, f"{tag}.fl{g}.{hh}")
    for i in range(NT):
        p.op("vector", lambda e, i=i: e.tensor_copy(self.Cv(i), self.Av(i)),
             reads=[f"A.{i}"], writes=[f"C.{i}"], label=f"{tag}.cc{i}")
    for m in range(NT):
        wload(Wo, m, 0, "WLa", f"{tag}.o{m}")
        projT(m * 2048, [f"D.{2 * m}", f"D.{2 * m + 1}"], ALLC, "bo2", m, f"{tag}.op{m}")
    ALLD = [f"D.{r}" for r in range(16)]
    for m in range(NT):
        wload(Wrq, m, 0, "WLa", f"{tag}.rq{m}")
        projT(m * 2048, [f"E.{2 * m}", f"E.{2 * m + 1}"], ALLD, "brq", m,
              f"{tag}.rqp{m}", rhs_tiles=[self.D[:, j * 2048:(j + 1) * 2048] for j in range(NT)],
              dst_buf=self.E)
    ALLE = [f"E.{r}" for r in range(16)]
    for h in range(NH):
        g, hl = h // 2, (h % 2) * 64
        head_norm_cell(self.E, g * 2048, hl, 16 + h, ALLE, f"{tag}.rqn{h}")
    for pp in range(2):
        for i in range(NT):
            p.dma(self.S1[:, 0:2048], pastT[pp, i * 128:(i + 1) * 128, :],
                  reads=["S1"], writes=["S1"], label=f"{tag}.pl{pp}.{i}")
            p.op("vector", lambda e, i=i: e.tensor_copy(self.Cv(i), self.S1[:, 0:2048]),
                 reads=["S1"], writes=[f"C.{i}"], label=f"{tag}.pc{pp}.{i}")
        for g in range(NT):
            wload(Wrk, g, 0, "WLa", f"{tag}.rk{pp}.{g}")
            for c in range(4):
                def rkmm(e, c=c):
                    last = None
                    for j in range(NT):
                        last = e.matmul(self.PS[c % 2][:, 0:512],
                                        self.WL[:, j * 128:(j + 1) * 128],
                                        self.Cv(j)[:, c * 512:(c + 1) * 512],
                                        start=(j == 0), stop=(j == NT - 1))
                    return last
                p.op("tensor", rkmm, reads=["WLa"] + ALLC,
                     writes=[self.PK[c % 2]], label=f"{tag}.rkm{pp}.{g}.{c}")
                p.op("vector", lambda e, c=c, g=g: e.tensor_scalar(
                    self.SC[:, c * 512:c * 512 + 512], self.PS[c % 2][:, 0:512],
                    self.col("brk", g), None, ALU.add),
                    reads=[self.PK[c % 2], "COLS.brk"], writes=["SC.A"],
                    label=f"{tag}.rke{pp}.{g}.{c}")
            wload(Wrv, g, 0, "WLa", f"{tag}.rv{pp}.{g}")
            p.op("vector", lambda e, g=g: e.tensor_copy(
                self.SC[0:1, R2:R2 + 128], self.ROWS[0:1, 1024 + g * 128:1024 + (g + 1) * 128]),
                reads=["ROWS"], writes=["SC.C"], label=f"{tag}.brv{pp}.{g}")
            for tau in range(16):
                def rvmm(e, tau=tau):
                    for j in range(NT):
                        e.matmul(self.PS[tau % 2][:, 0:128],
                                 self.Cv(j)[:, tau * 128:(tau + 1) * 128],
                                 self.WL[:, j * 128:(j + 1) * 128],
                                 start=(j == 0), stop=False)
                    return e.matmul(self.PS[tau % 2][:, 0:128], self.ONESR[:],
                                    self.SC[0:1, R2:R2 + 128], start=False, stop=True)
                p.op("tensor", rvmm, reads=["WLa", "SC.C", "ONESR"] + ALLC,
                     writes=[self.PK[tau % 2]], label=f"{tag}.rvm{pp}.{g}.{tau}")
                p.op("vector", lambda e, tau=tau: e.tensor_copy(
                    self.SC[:, R1 + tau * 128:R1 + (tau + 1) * 128],
                    self.PS[tau % 2][:, 0:128]),
                    reads=[self.PK[tau % 2]], writes=["SC.B"],
                    label=f"{tag}.rve{pp}.{g}.{tau}")
            for hh in range(2):
                hl = hh * 64
                h = g * 2 + hh
                head_norm_cell(self.SC, 0, hl, 37, ["SC.A"], f"{tag}.rkn{pp}.{g}.{hh}")
                bias_col_from_cells(16 + h, 37, f"{tag}.rbc{pp}.{g}.{hh}")
                flash(self.E, g * 2048, ALLE, self.SC, 0, ["SC.A"],
                      self.SC, R1, ["SC.B"], hl,
                      g * 2048, [f"A.{g}"], True, pp == 1, f"{tag}.rf{pp}.{g}.{hh}")
    for i in range(NT):
        p.op("vector", lambda e, i=i: e.tensor_copy(self.Cv(i), self.Av(i)),
             reads=[f"A.{i}"], writes=[f"C.{i}"], label=f"{tag}.rc{i}")
    for m in range(NT):
        p.dma(self.WF[:, 0:1024].rearrange("p (j f) -> p j f", f=128),
              Wg.rearrange("(j p) o -> p j o", p=128)[:, 0:8, m * 128:(m + 1) * 128],
              writes=["WF"], label=f"{tag}.g1{m}")
        p.op("vector", lambda e: e.tensor_copy(self.WL[:, 0:1024], self.WF[:, 0:1024]),
             reads=["WF"], writes=["WLa"], label=f"{tag}.g1c{m}")
        p.dma(self.WF[:, 0:1024].rearrange("p (j f) -> p j f", f=128),
              Wg.rearrange("(j p) o -> p j o", p=128)[:, 8:16, m * 128:(m + 1) * 128],
              writes=["WF"], label=f"{tag}.g2{m}")
        p.op("vector", lambda e: e.tensor_copy(self.SC[:, 2048:3072], self.WF[:, 0:1024]),
             reads=["WF"], writes=["SC.B"], label=f"{tag}.g2c{m}")
        for c in range(4):
            def gmm(e, c=c):
                last = None
                for j in range(16):
                    rhs = (self.D[:, (j) * 2048 + c * 512:j * 2048 + c * 512 + 512]
                           if j < 8 else
                           self.Cv(j - 8)[:, c * 512:(c + 1) * 512])
                    lhsT = (self.WL[:, j * 128:(j + 1) * 128] if j < 8
                            else self.SC[:, 2048 + (j - 8) * 128:2048 + (j - 7) * 128])
                    last = e.matmul(self.PS[2 + c % 2][:, 0:512],
                                    lhsT, rhs, start=(j == 0), stop=(j == 15))
                return last
            p.op("tensor", gmm, reads=["WLa", "SC.B"] + ALLD + ALLC,
                 writes=[self.PK[2 + c % 2]], label=f"{tag}.gm{m}.{c}")
            p.op("scalar", lambda e, c=c, m=m: e.activation(
                self.E[:, m * 2048 + c * 512:m * 2048 + c * 512 + 512],
                self.PS[2 + c % 2][:, 0:512], AF.Sigmoid, bias=self.col("bg", m)),
                reads=[self.PK[2 + c % 2], "COLS.bg"],
                writes=[f"E.{2 * m}", f"E.{2 * m + 1}"], label=f"{tag}.ge{m}.{c}")
    for m in range(NT):
        p.dma(self.Av(m), xspill[m * 128:(m + 1) * 128, :],
              reads=[f"A.{m}", "XSP"], writes=[f"A.{m}"], label=f"{tag}.rl{m}")
    for m in range(NT):
        for c in range(4):
            sl = lambda buf: buf[:, m * 2048 + c * 512:m * 2048 + c * 512 + 512]
            p.op("vector", lambda e, m=m, c=c: e.tensor_tensor(
                self.SC[:, 0:512], self.Cv(m)[:, c * 512:(c + 1) * 512],
                self.D[:, m * 2048 + c * 512:m * 2048 + c * 512 + 512], ALU.subtract),
                reads=[f"C.{m}", f"D.{2 * m}", f"D.{2 * m + 1}"], writes=["SC.A"],
                label=f"{tag}.cb1{m}.{c}")
            p.op("vector", lambda e, m=m, c=c: e.tensor_tensor(
                self.SC[:, 512:1024],
                self.E[:, m * 2048 + c * 512:m * 2048 + c * 512 + 512],
                self.SC[:, 0:512], ALU.mult),
                reads=[f"E.{2 * m}", f"E.{2 * m + 1}", "SC.A"], writes=["SC.A"],
                label=f"{tag}.cb2{m}.{c}")
            p.op("vector", lambda e, m=m, c=c: e.tensor_tensor(
                self.SC[:, 512:1024],
                self.D[:, m * 2048 + c * 512:m * 2048 + c * 512 + 512],
                self.SC[:, 512:1024], ALU.add),
                reads=[f"D.{2 * m}", f"D.{2 * m + 1}", "SC.A"], writes=["SC.A"],
                label=f"{tag}.cb3{m}.{c}")
            p.op("vector", lambda e, m=m, c=c: e.tensor_tensor(
                self.Av(m, c * 512, 512), self.Av(m, c * 512, 512),
                self.SC[:, 512:1024], ALU.add),
                reads=["SC.A", f"A.{m}"], writes=[f"A.{m}"],
                label=f"{tag}.cb4{m}.{c}")


MK.emit_attn = _mk_emit_attn


def build_attn_prog():
    nc = bass.Bass("TRN2", target_bir_lowering=False, debug=False)
    xT = nc.dram_tensor("xT", [H, T], F32, kind="ExternalInput").ap()
    ws = {}
    for nm in ("Wq", "Wk", "Wv", "Wo", "Wrq", "Wrk", "Wrv"):
        ws[nm] = nc.dram_tensor(nm, [H, H], F32, kind="ExternalInput").ap()
    Wg = nc.dram_tensor("Wg", [2 * H, H], F32, kind="ExternalInput").ap()
    pastT = nc.dram_tensor("pastT", [2, H, T], F32, kind="ExternalInput").ap()
    brow = nc.dram_tensor("brow", [1, 2048], F32, kind="ExternalInput").ap()
    ident = nc.dram_tensor("ident", [128, 128], F32, kind="ExternalInput").ap()
    cols = {}
    for nm in ("bq", "bk", "bo2", "brq", "brk", "bg"):
        cols[nm] = nc.dram_tensor(f"c_{nm}", [128, 8], F32, kind="ExternalInput").ap()
    xout = nc.dram_tensor("xout", [H, T], F32, kind="ExternalOutput").ap()
    xspill = nc.dram_tensor("xspill", [H, T], F32, kind="Internal").ap()
    with ExitStack() as stack:
        mk = MK(nc, stack)
        mk.init_consts(ident)
        for nm in cols:
            mk.load_cols(cols[nm], nm)
        mk.load_x(xT)
        mk.emit_attn(ws["Wq"], ws["Wk"], ws["Wv"], ws["Wo"], ws["Wrq"], ws["Wrk"],
                     ws["Wrv"], Wg, pastT, brow, xspill, "a")
        outs = mk.store_x(xout)
        with nc.Block() as block:
            mk.p.emit(block, end_fence=outs)
    return nc


_PROG_CACHE = {}


def _prog(name):
    if name not in _PROG_CACHE:
        _PROG_CACHE[name] = {"ffn": build_ffn_prog, "rwkv": build_rwkv_prog,
                             "attn": build_attn_prog}[name]()
    return _PROG_CACHE[name]


def kernel(**inputs):
    f = np.float32
    x = np.asarray(inputs["L0_attn_x"], dtype=f)
    ident = _ident_np()
    tr = lambda a: np.ascontiguousarray(np.asarray(a, dtype=f).T)

    ffnw = {}
    for L in ("L0", "L1"):
        ffnw[L] = dict(wg=tr(inputs[f"{L}_ffn_w_gate"]),
                       wu=tr(inputs[f"{L}_ffn_w_up"]),
                       wd=tr(inputs[f"{L}_ffn_w_down"]))
    rwkvw = dict(wk=tr(inputs["L0_attn_Wk"]), wv=tr(inputs["L0_attn_Wv"]),
                 wr=tr(inputs["L0_attn_Wr"]), wo=tr(inputs["L0_attn_Wo"]),
                 c_tmk=_colify(np.asarray(inputs["L0_attn_time_mix_key"], f)[0, 0]),
                 c_tmv=_colify(np.asarray(inputs["L0_attn_time_mix_value"], f)[0, 0]),
                 c_tmr=_colify(np.asarray(inputs["L0_attn_time_mix_receptance"], f)[0, 0]),
                 c_tdec=_colify(np.asarray(inputs["L0_attn_time_decay"], f)),
                 c_tfirst=_colify(np.asarray(inputs["L0_attn_time_first"], f)),
                 c_bo=_colify(np.asarray(inputs["L0_attn_bo"], f)))
    attnw = dict(
        Wq=np.asarray(inputs["L1_attn_Wq"], f), Wk=np.asarray(inputs["L1_attn_Wk"], f),
        Wv=np.asarray(inputs["L1_attn_Wv"], f), Wo=np.asarray(inputs["L1_attn_Wo"], f),
        Wrq=np.asarray(inputs["L1_attn_Wrq"], f), Wrk=np.asarray(inputs["L1_attn_Wrk"], f),
        Wrv=np.asarray(inputs["L1_attn_Wrv"], f), Wg=np.asarray(inputs["L1_attn_Wg"], f),
        pastT=np.ascontiguousarray(
            np.asarray(inputs["L1_attn_past_states"], f)[:, 0].transpose(0, 2, 1)),
        brow=np.concatenate([np.asarray(inputs["L1_attn_bv"], f),
                             np.asarray(inputs["L1_attn_brv"], f)])[None, :],
        c_bq=_colify(np.asarray(inputs["L1_attn_bq"], f)),
        c_bk=_colify(np.asarray(inputs["L1_attn_bk"], f)),
        c_bo=_colify(np.asarray(inputs["L1_attn_bo"], f)),
        c_brq=_colify(np.asarray(inputs["L1_attn_brq"], f)),
        c_brk=_colify(np.asarray(inputs["L1_attn_brk"], f)),
        c_bg=_colify(np.asarray(inputs["L1_attn_bg"], f)))

    xT = [tr(x[b]) for b in range(N_CORES)]

    def launch(prog_name, extra):
        nc = _prog(prog_name)
        maps = [dict(xT=xT[b], ident=ident, **extra) for b in range(N_CORES)]
        res = run_bass_kernel_spmd(nc, maps, core_ids=list(range(N_CORES)))
        for b in range(N_CORES):
            xT[b] = np.ascontiguousarray(np.asarray(res.results[b]["xout"]))

    launch("ffn", ffnw["L0"])
    launch("rwkv", rwkvw)
    launch("ffn", ffnw["L0"])
    launch("ffn", ffnw["L1"])
    launch("attn", attnw)
    launch("ffn", ffnw["L1"])

    return np.stack([xT[b].T for b in range(N_CORES)]).astype(np.float32)


kernel_multilaunch = kernel


def kernel(**inputs):
    return kernel_fused(**inputs)


def build_fused_prog():
    nc = bass.Bass("TRN2", target_bir_lowering=False, debug=False)
    xT = nc.dram_tensor("xT", [H, T], F32, kind="ExternalInput").ap()
    ident = nc.dram_tensor("ident", [128, 128], F32, kind="ExternalInput").ap()
    ffn_w = {}
    for L in ("l0", "l1"):
        ffn_w[L] = dict(
            wg=nc.dram_tensor(f"{L}_wg", [H, INNER], F32, kind="ExternalInput").ap(),
            wu=nc.dram_tensor(f"{L}_wu", [H, INNER], F32, kind="ExternalInput").ap(),
            wd=nc.dram_tensor(f"{L}_wd", [INNER, H], F32, kind="ExternalInput").ap(),
            wdq=nc.dram_tensor(f"{L}_wdq", [INNER, H], BF16, kind="Internal").ap())
    rw = {nm: nc.dram_tensor(f"r_{nm}", [H, H], F32, kind="ExternalInput").ap()
          for nm in ("wk", "wv", "wr", "wo")}
    aw = {nm: nc.dram_tensor(f"a_{nm}", [H, H], F32, kind="ExternalInput").ap()
          for nm in ("Wq", "Wk", "Wv", "Wo", "Wrq", "Wrk", "Wrv")}
    Wg = nc.dram_tensor("a_Wg", [2 * H, H], F32, kind="ExternalInput").ap()
    pastT = nc.dram_tensor("pastT", [2, H, T], F32, kind="ExternalInput").ap()
    brow = nc.dram_tensor("brow", [1, 2048], F32, kind="ExternalInput").ap()
    cols = {}
    for nm in ("tmk", "tmv", "tmr", "tdec", "tfirst", "bo",
               "bq", "bk", "bo2", "brq", "brk", "bg"):
        cols[nm] = nc.dram_tensor(f"c_{nm}", [128, 8], F32, kind="ExternalInput").ap()
    xout = nc.dram_tensor("xout", [H, T], F32, kind="ExternalOutput").ap()
    xspill = nc.dram_tensor("xspill", [H, T], F32, kind="Internal").ap()
    with ExitStack() as stack:
        mk = MK(nc, stack)
        mk.init_consts(ident)
        for nm in cols:
            mk.load_cols(cols[nm], nm)
        mk.load_x(xT)
        mk.emit_ffn(ffn_w["l0"]["wg"], ffn_w["l0"]["wu"], ffn_w["l0"]["wd"],
                    ffn_w["l0"]["wdq"], "f0a")
        mk.emit_rwkv(rw["wk"], rw["wv"], rw["wr"], rw["wo"], xspill, "rk")
        mk.emit_ffn(ffn_w["l0"]["wg"], ffn_w["l0"]["wu"], ffn_w["l0"]["wd"],
                    ffn_w["l0"]["wdq"], "f0b")
        mk.emit_ffn(ffn_w["l1"]["wg"], ffn_w["l1"]["wu"], ffn_w["l1"]["wd"],
                    ffn_w["l1"]["wdq"], "f1a")
        mk.emit_attn(aw["Wq"], aw["Wk"], aw["Wv"], aw["Wo"], aw["Wrq"], aw["Wrk"],
                     aw["Wrv"], Wg, pastT, brow, xspill, "at")
        mk.emit_ffn(ffn_w["l1"]["wg"], ffn_w["l1"]["wu"], ffn_w["l1"]["wd"],
                    ffn_w["l1"]["wdq"], "f1b")
        outs = mk.store_x(xout)
        with nc.Block() as block:
            mk.p.emit(block, end_fence=outs)
    return nc


def kernel_fused(**inputs):
    f = np.float32
    x = np.asarray(inputs["L0_attn_x"], dtype=f)
    tr = lambda a: np.ascontiguousarray(np.asarray(a, dtype=f).T)
    base = dict(
        ident=_ident_np(),
        l0_wg=tr(inputs["L0_ffn_w_gate"]), l0_wu=tr(inputs["L0_ffn_w_up"]),
        l0_wd=tr(inputs["L0_ffn_w_down"]),
        l1_wg=tr(inputs["L1_ffn_w_gate"]), l1_wu=tr(inputs["L1_ffn_w_up"]),
        l1_wd=tr(inputs["L1_ffn_w_down"]),
        r_wk=tr(inputs["L0_attn_Wk"]), r_wv=tr(inputs["L0_attn_Wv"]),
        r_wr=tr(inputs["L0_attn_Wr"]), r_wo=tr(inputs["L0_attn_Wo"]),
        a_Wq=np.asarray(inputs["L1_attn_Wq"], f), a_Wk=np.asarray(inputs["L1_attn_Wk"], f),
        a_Wv=np.asarray(inputs["L1_attn_Wv"], f), a_Wo=np.asarray(inputs["L1_attn_Wo"], f),
        a_Wrq=np.asarray(inputs["L1_attn_Wrq"], f),
        a_Wrk=np.asarray(inputs["L1_attn_Wrk"], f),
        a_Wrv=np.asarray(inputs["L1_attn_Wrv"], f),
        a_Wg=np.asarray(inputs["L1_attn_Wg"], f),
        pastT=np.ascontiguousarray(
            np.asarray(inputs["L1_attn_past_states"], f)[:, 0].transpose(0, 2, 1)),
        brow=np.concatenate([np.asarray(inputs["L1_attn_bv"], f),
                             np.asarray(inputs["L1_attn_brv"], f)])[None, :],
        c_tmk=_colify(np.asarray(inputs["L0_attn_time_mix_key"], f)[0, 0]),
        c_tmv=_colify(np.asarray(inputs["L0_attn_time_mix_value"], f)[0, 0]),
        c_tmr=_colify(np.asarray(inputs["L0_attn_time_mix_receptance"], f)[0, 0]),
        c_tdec=_colify(np.asarray(inputs["L0_attn_time_decay"], f)),
        c_tfirst=_colify(np.asarray(inputs["L0_attn_time_first"], f)),
        c_bo=_colify(np.asarray(inputs["L0_attn_bo"], f)),
        c_bq=_colify(np.asarray(inputs["L1_attn_bq"], f)),
        c_bk=_colify(np.asarray(inputs["L1_attn_bk"], f)),
        c_bo2=_colify(np.asarray(inputs["L1_attn_bo"], f)),
        c_brq=_colify(np.asarray(inputs["L1_attn_brq"], f)),
        c_brk=_colify(np.asarray(inputs["L1_attn_brk"], f)),
        c_bg=_colify(np.asarray(inputs["L1_attn_bg"], f)),
    )
    if "fused" not in _PROG_CACHE:
        _PROG_CACHE["fused"] = build_fused_prog()
    nc = _PROG_CACHE["fused"]
    maps = [dict(xT=np.ascontiguousarray(x[b].T), **base) for b in range(N_CORES)]
    res = run_bass_kernel_spmd(nc, maps, core_ids=list(range(N_CORES)))
    return np.stack([np.asarray(res.results[b]["xout"]).T
                     for b in range(N_CORES)]).astype(np.float32)
